# version 05
# baseline (speedup 1.0000x reference)
import math
import numpy as np
import ml_dtypes

import concourse.bass as bass
import concourse.mybir as mybir
from concourse.bass_utils import run_bass_kernel_spmd

F32 = mybir.dt.float32
BF16 = mybir.dt.bfloat16
AF = mybir.ActivationFunctionType
ALU = mybir.AluOpType
AX = mybir.AxisListType

B, L, D = 2, 8192, 2048
NCORES = 8
TLOC = 2048
CTX = 1024
TROWS = TLOC + CTX
CH = 64
NCH = TROWS // CH
NCTX = CTX // CH
DK = 128
HID = 8192
EPS = 1e-5
RSQN = 1.0 / math.sqrt(float(HID))

DBG = set()
PHASES = 6
GATE_CASTS = False
SKIP_ATTN = False
T_UPTO = 3
T_CS = 64
T_QQ = 4
DEFER_CASTS = False

ENGS = ("sync", "gpsimd", "scalar", "vector", "tensor")


def _hadamard(n):
    h = np.ones((n, n), np.float32)
    for i in range(n):
        for j in range(n):
            if bin(i & j).count("1") & 1:
                h[i, j] = -1.0
    return h


class P:

    def __init__(self, nc):
        self.nc = nc
        self.ops = {e: [] for e in ENGS}
        self.sems = {}
        self.cnt = {}
        for s in ("sy", "sc", "gp", "pe", "ac", "dv", "ph"):
            self.sems[s] = nc.alloc_semaphore(f"s_{s}")
            self.cnt[s] = 0

    def rec(self, eng, emit, waits=(), inc=None, n=1, dma=False):
        if inc is not None and inc not in self.sems:
            self.sems[inc] = self.nc.alloc_semaphore(f"s_{inc}")
            self.cnt[inc] = 0
        amt = 16 if dma else 1
        ws = []
        for w in waits:
            if w is None:
                continue
            s, v = w
            assert v <= self.cnt[s], f"wait on future token: {s} {v} > {self.cnt[s]}"
            ws.append((s, v))
        tok = None
        if inc is not None and n > 0:
            self.cnt[inc] += amt * n
            tok = (inc, self.cnt[inc])
        self.ops[eng].append((tuple(ws), emit, inc, amt, n))
        return tok

    def barrier(self):
        self.cnt["ph"] += 5
        target = self.cnt["ph"]
        ph = self.sems["ph"]
        for e in ENGS:
            self.ops[e].append(((), (lambda E, ph=ph: [E.sem_inc(ph, 1)]), None, 1, 1))
            self.ops[e].append(((("ph", target),), None, None, 1, 0))
        return ("ph", target)

    def emit(self, block):
        for e in ENGS:
            oplist = self.ops[e]
            sems = self.sems

            def runner(E, oplist=oplist):
                hw = {}
                for ws, em, inc, amt, n in oplist:
                    for s, v in ws:
                        if hw.get(s, 0) < v:
                            E.wait_ge(sems[s], v)
                            hw[s] = v
                    if em is None:
                        continue
                    instrs = em(E)
                    if inc is not None:
                        assert len(instrs) == n, f"{e}: {len(instrs)} != {n}"
                        for i in instrs:
                            i.then_inc(sems[inc], amt)

            getattr(block, e)(runner)


def build_nc():
    nc = bass.Bass("TRN2", target_bir_lowering=False)

    def din(name, shape):
        return nc.dram_tensor(name, shape, F32, kind="ExternalInput")

    x_in = din("x", [TROWS, D])
    wq, wk = din("wq", [DK, D]), din("wk", [DK, D])
    wog, wig, wo = din("wog", [D, D]), din("wig", [D, D]), din("wo", [D, D])
    gnw = din("gnw", [D])
    w1, w2, w3 = din("w1", [HID, D]), din("w2", [D, HID]), din("w3", [HID, D])
    out = nc.dram_tensor("out", [TLOC, D], F32, kind="ExternalOutput")

    def scr(name, shape, dt):
        kind = "ExternalOutput" if name in DBG else "Internal"
        return nc.dram_tensor(name, shape, dt, kind=kind)

    wqb, wkb = scr("wqb", [DK, D], BF16), scr("wkb", [DK, D], BF16)
    wogb, wigb, wob = scr("wogb", [D, D], BF16), scr("wigb", [D, D], BF16), scr("wob", [D, D], BF16)
    w1b, w3b, w2b = scr("w1b", [HID, D], BF16), scr("w3b", [HID, D], BF16), scr("w2b", [D, HID], BF16)
    w1t2 = scr("w1t2", [64, 128, 16, 128], BF16)
    w3t2 = scr("w3t2", [64, 128, 16, 128], BF16)
    wogt2 = scr("wogt2", [16, 128, D], BF16)
    wigt2 = scr("wigt2", [16, 128, D], BF16)
    wot2 = scr("wot2", [16, 128, D], BF16)
    zw = scr("zw", [64, 128, D], BF16)
    w2p = scr("w2p", [4, 8, 128, 8, 512], BF16)
    lnscr = scr("lnscr", [TROWS, D], BF16)
    x1scr = scr("x1scr", [TLOC, D], F32)
    ln2scr = scr("ln2scr", [TLOC, D], BF16)

    idf_c = nc.inline_tensor(np.eye(128, dtype=np.float32), name="idf_c")
    onesf_c = nc.inline_tensor(np.ones((128, 128), np.float32), name="onesf_c")
    u65_np = np.zeros((64, 65), np.float32)
    for j in range(64):
        u65_np[j, j:64] = 1.0
        u65_np[j, 64] = 1.0
    u65_c = nc.inline_tensor(u65_np, name="u65_c")
    mask_np = np.zeros((64, 64), np.float32)
    for j in range(64):
        mask_np[j, j:] = 1.0
    mask_c = nc.inline_tensor(mask_np, name="mask_c")
    h128_c = nc.inline_tensor(_hadamard(128).astype(ml_dtypes.bfloat16), name="h128_c")
    h64 = _hadamard(64)
    h64i2_np = np.zeros((128, 128), np.float32)
    for gp in range(64):
        for s in range(2):
            col = (gp // 8) * 16 + s * 8 + (gp % 8)
            for g in range(64):
                h64i2_np[2 * g + s, col] = h64[g, gp]
    h64i2_c = nc.inline_tensor(h64i2_np.astype(ml_dtypes.bfloat16), name="h64i2_c")
    ones1b_c = nc.inline_tensor(np.ones((128, 1), ml_dtypes.bfloat16), name="ones1b_c")
    eps_c = nc.inline_tensor(np.full((128, 1), EPS, np.float32), name="eps_c")

    AKB = 176
    arena = nc.alloc_sbuf_tensor("arena", [128, AKB * 1024], mybir.dt.uint8)

    def av(off_kb, pdim, shape, dt):
        nb = int(np.prod(shape)) * mybir.dt.size(dt)
        assert off_kb * 1024 + nb <= AKB * 1024
        ap = arena[0:pdim, off_kb * 1024: off_kb * 1024 + nb].bitcast(dt)
        if len(shape) > 1:
            names = " ".join(f"a{i}" for i in range(len(shape)))
            kw = {f"a{i}": int(shape[i]) for i in range(len(shape))}
            ap = ap.rearrange(f"p ({names}) -> p {names}", **kw)
        return ap

    idf = nc.alloc_sbuf_tensor("idf", [128, 128], F32)
    onesf = nc.alloc_sbuf_tensor("onesf", [128, 128], F32)
    u65 = nc.alloc_sbuf_tensor("u65", [64, 65], F32)
    maskat = nc.alloc_sbuf_tensor("maskat", [64, 64], F32)
    h128b = nc.alloc_sbuf_tensor("h128b", [128, 128], BF16)
    h64i2b = nc.alloc_sbuf_tensor("h64i2b", [128, 128], BF16)
    ones1b = nc.alloc_sbuf_tensor("ones1b", [128, 1], BF16)
    epsv = nc.alloc_sbuf_tensor("epsv", [128, 1], F32)
    wqT = nc.alloc_sbuf_tensor("wqT", [128, 16, 128], BF16)
    wkT = nc.alloc_sbuf_tensor("wkT", [128, 16, 128], BF16)
    gnat = nc.alloc_sbuf_tensor("gnat", [16, 128], F32)
    gcol = nc.alloc_sbuf_tensor("gcol", [128, 16], F32)
    stats = nc.alloc_sbuf_tensor("stats", [128, 16], F32)
    ktm = nc.alloc_sbuf_tensor("ktm", [64, 128], F32)
    t1c = nc.alloc_sbuf_tensor("t1c", [64, 128], F32)
    t2c = nc.alloc_sbuf_tensor("t2c", [64, 128], F32)
    gftm = nc.alloc_sbuf_tensor("gftm", [64, 128], F32)
    expb = nc.alloc_sbuf_tensor("expb", [65, 128], F32)
    enb = nc.alloc_sbuf_tensor("enb", [64, 128], F32)
    qxp = nc.alloc_sbuf_tensor("qxp", [64, 128], F32)
    qetm = nc.alloc_sbuf_tensor("qetm", [64, 128], F32)
    ketm = nc.alloc_sbuf_tensor("ketm", [64, 128], F32)
    kdb = nc.alloc_sbuf_tensor("kdb", [64, 128], BF16)
    blc = nc.alloc_sbuf_tensor("blc", [128, 1], F32)
    qefmb = nc.alloc_sbuf_tensor("qefmb", [128, 64], BF16)
    kefmb = nc.alloc_sbuf_tensor("kefmb", [128, 64], BF16)
    atb = nc.alloc_sbuf_tensor("atb", [64, 64], BF16)
    ebcol = nc.alloc_sbuf_tensor("ebcol", [128, 2], F32)
    nmx = nc.alloc_sbuf_tensor("nmx", [64, 4], F32)
    rrow = nc.alloc_sbuf_tensor("rrow", [1, 1024], F32)
    rstd_bc = nc.alloc_sbuf_tensor("rstd_bc", [128, 512], F32)
    wprep = nc.alloc_sbuf_tensor("wprep", [128, 2, 2048], BF16)

    xt = [av(0, 128, (D,), F32), av(8, 128, (D,), F32)]
    lnt = [av(16, 128, (D,), BF16), av(20, 128, (D,), BF16)]
    sqscr = av(28, 128, (D,), BF16)
    qpre = av(0, 128, (TROWS,), F32)
    kpre = av(12, 128, (TROWS,), F32)
    T = av(24, 128, (D,), F32)
    S_b = av(32, 128, (D,), BF16)
    vtm = [av(36, 64, (D,), BF16), av(40, 64, (D,), BF16)]
    o_fm = av(44, 128, (16, 512), BF16)
    lnfm = [av(76, 128, (16, 512), BF16), av(92, 128, (16, 512), BF16)]
    ln1q = av(76, 128, (16, 512), BF16)
    sgq = av(92, 128, (16, 512), BF16)
    wstA = [av(108, 128, (16, 512), BF16), av(124, 128, (16, 512), BF16)]
    wstB = [av(140, 128, (16, 512), BF16), av(156, 128, (16, 512), BF16)]
    woT = av(108, 128, (16, D), BF16)
    osq = av(172, 128, (2, 512), BF16)
    o2t = av(174, 128, (512,), F32)
    xres = av(76, 128, (D,), F32)
    ln2t = av(84, 128, (D,), BF16)
    sqscr2 = av(88, 128, (D,), BF16)
    h_fm = av(0, 128, (64, 1024), BF16)
    ln2fm = av(128, 128, (16, 1024), BF16)
    w1st = [av(160, 128, (16, 128), BF16), av(164, 128, (16, 128), BF16)]
    w3st = [av(168, 128, (16, 128), BF16), av(172, 128, (16, 128), BF16)]
    w2pb = [av(160, 128, (8, 512), BF16), av(168, 128, (8, 512), BF16)]
    x1t = [av(128 + 2 * i, 128, (512,), F32) for i in range(8)]
    outt = [av(144 + 2 * i, 128, (512,), F32) for i in range(8)]
    span = [av(0, 128, (32768,), BF16), av(64, 128, (32768,), BF16)]
    w2ev = [av(132, 128, (2048,), BF16), av(136, 128, (2048,), BF16)]

    PS = nc.alloc_psum_tensor("PS", [128, 4096], F32)
    B65 = PS[0:65, 0:128]
    EBB = PS[0:64, 128:256]
    ATP = PS[0:64, 256:320]
    TK = PS[0:64, 512:640]
    TQ = PS[0:64, 640:768]
    TQE = PS[0:128, 768:832]
    TKE = PS[0:128, 832:896]
    TGN = PS[0:128, 896:912]
    GFP = PS[0:128, 960:1024]
    ps_S = PS[:, 1024:3072]
    ps_o = PS[:, 3072:4096]

    p = P(nc)
    rec = p.rec

    cast_tok = {}

    def em_consts(E):
        r = [
            E.dma_start(idf[:, :], idf_c[:, :]),
            E.dma_start(onesf[:, :], onesf_c[:, :]),
            E.dma_start(u65[:, :], u65_c[:, :]),
            E.dma_start(maskat[:, :], mask_c[:, :]),
            E.dma_start(h128b[:, :], h128_c[:, :]),
            E.dma_start(h64i2b[:, :], h64i2_c[:, :]),
            E.dma_start(ones1b[:, :], ones1b_c[:, :]),
            E.dma_start(epsv[:, :], eps_c[:, :]),
            E.dma_start(gnat[:, :], gnw.ap().rearrange("(a b) -> a b", a=16)),
        ]
        return r

    tok_c = rec("sync", em_consts, inc="cst", n=9, dma=True)
    for _e in ENGS:
        rec(_e, None, waits=[tok_c], n=0)

    wqkT_box = [None]

    tok_gn = rec("tensor",
                 lambda E: [E.matmul(TGN, gnat[:, :], idf[0:16, 0:16], start=True, stop=True,
                                     is_transpose=True, skip_group_check=True)],
                 waits=[tok_c], inc="pe", n=1)
    gcol_tok = rec("vector", lambda E: [E.tensor_copy(gcol[:, :], TGN)], waits=[tok_gn], inc="dv", n=1)


    prep_list = []
    cast_done = {}

    def _cast_pieces(name, fsrc, bdst, row_lo, row_hi, col_lo, col_hi):
        for r0 in range(row_lo, row_hi, 128):
            for c0 in range(col_lo, col_hi, 2048):
                prep_list.append(("cast", name, fsrc, bdst, r0, c0))

    _cast_pieces("wq", wq, wqb, 0, DK, 0, D)
    _cast_pieces("wk", wk, wkb, 0, DK, 0, D)
    prep_list.append(("wqkT",))
    _cast_pieces("wog", wog, wogb, 0, D, 0, D)
    for dc in range(16):
        prep_list.append(("repT", "wog", wogb, wogt2, dc))
    _cast_pieces("wig", wig, wigb, 0, D, 0, D)
    for dc in range(16):
        prep_list.append(("repT", "wig", wigb, wigt2, dc))
    _cast_pieces("wo", wo, wob, 0, D, 0, D)
    for dc in range(16):
        prep_list.append(("repT", "wo", wob, wot2, dc))
    for wn, wfsrc, wbdst, wtdst in (("w1", w1, w1b, w1t2), ("w3", w3, w3b, w3t2)):
        for hq in range(4):
            _cast_pieces(wn, wfsrc, wbdst, hq * 2048, (hq + 1) * 2048, 0, D)
            for dc in range(16):
                prep_list.append(("repack", wbdst, wtdst, dc, hq))
    for pc in range(4):
        _cast_pieces("w2", w2, w2b, 0, D, pc * 2048, (pc + 1) * 2048)
        for g in range(pc * 16, (pc + 1) * 16):
            prep_list.append(("zw", g))

    prep_idx = [0]
    prep_buf_tok = [None, None]
    last_repack_tok = [None]
    last_zw_tok = [None]
    repT_done = {}

    def do_prep_steps(k):
        for _ in range(k):
            if prep_idx[0] >= len(prep_list):
                return
            step = prep_list[prep_idx[0]]
            bi = prep_idx[0] % 2
            prep_idx[0] += 1
            if step[0] == "cast":
                _, name, fsrc, bdst, r0, c0 = step
                nr = min(128, fsrc.shape[0] - r0)
                nc = min(2048, fsrc.shape[1] - c0)
                ld = rec("gpsimd",
                         lambda E, fsrc=fsrc, r0=r0, c0=c0, nr=nr, nc=nc, bi=bi: [
                             E.dma_start(wprep[0:nr, bi, 0:nc],
                                         fsrc[r0:r0 + nr, c0:c0 + nc])],
                         waits=[prep_buf_tok[bi]], inc=f"cl{bi}", n=1, dma=True)
                wr = rec("scalar",
                         lambda E, bdst=bdst, r0=r0, c0=c0, nr=nr, nc=nc, bi=bi: [
                             E.dma_start(bdst[r0:r0 + nr, c0:c0 + nc],
                                         wprep[0:nr, bi, 0:nc])],
                         waits=[ld], inc=f"pw{bi}", n=1, dma=True)
                cast_done[name] = wr
            elif step[0] == "wqkT":
                def em_wqkT(E):
                    r = []
                    for dc in range(16):
                        r.append(E.dma_start(wqT[:, dc, :],
                                             wqb[0:DK, dc * 128:(dc + 1) * 128], transpose=True))
                        r.append(E.dma_start(wkT[:, dc, :],
                                             wkb[0:DK, dc * 128:(dc + 1) * 128], transpose=True))
                    return r
                wr = rec("sync", em_wqkT, waits=[cast_done["wq"], cast_done["wk"]],
                         inc="wqk", n=32, dma=True)
                wqkT_box[0] = wr
            elif step[0] == "repT":
                _, name, bsrc, tdst, dc = step
                ld = rec("sync",
                         lambda E, bsrc=bsrc, dc=dc, bi=bi: [
                             E.dma_start(wprep[:, bi, :],
                                         bsrc[0:D, dc * 128:(dc + 1) * 128],
                                         transpose=True)],
                         waits=[cast_done[name], prep_buf_tok[bi]],
                         inc=f"pl{bi}", n=1, dma=True)
                wr = rec("scalar",
                         lambda E, tdst=tdst, dc=dc, bi=bi: [
                             E.dma_start(tdst[dc, :, :], wprep[:, bi, :])],
                         waits=[ld], inc=f"pw{bi}", n=1, dma=True)
                repT_done[name] = wr
            elif step[0] == "repack":
                _, wbsrc, wdst, dc, hq = step
                ld = rec("sync",
                         lambda E, wbsrc=wbsrc, dc=dc, hq=hq, bi=bi: [
                             E.dma_start(wprep[:, bi, :],
                                         wbsrc[hq * 2048:(hq + 1) * 2048,
                                               dc * 128:(dc + 1) * 128],
                                         transpose=True)],
                         waits=[cast_done["w1" if wbsrc is w1b else "w3"], prep_buf_tok[bi]],
                         inc=f"pl{bi}", n=1, dma=True)
                wr = rec("scalar",
                         lambda E, wdst=wdst, dc=dc, hq=hq, bi=bi: [
                             E.dma_start(wdst[hq * 16:(hq + 1) * 16, :, dc, :]
                                         .rearrange("hb p c -> p hb c"),
                                         wprep[:, bi, :].rearrange("p (hb c) -> p hb c", hb=16))],
                         waits=[ld], inc=f"pw{bi}", n=1, dma=True)
                last_repack_tok[0] = wr
            else:
                g = step[1]
                ld = rec("sync",
                         lambda E, g=g, bi=bi: [
                             E.dma_start(wprep[:, bi, :], w2b[0:D, g * 128:(g + 1) * 128],
                                         transpose=True)],
                         waits=[cast_done["w2"], prep_buf_tok[bi]], inc=f"pl{bi}", n=1, dma=True)
                wr = rec("scalar",
                         lambda E, g=g, bi=bi: [E.dma_start(zw[g, :, :], wprep[:, bi, :])],
                         waits=[ld], inc=f"pw{bi}", n=1, dma=True)
                last_zw_tok[0] = wr
            prep_buf_tok[bi] = wr

    def _finish():
        global _LAST_P
        _LAST_P = p
        fin = [(s, p.cnt[s]) for s in list(p.cnt) if p.cnt[s] > 0]
        rec("sync", lambda E: [], waits=fin, n=0)
        with nc.Block() as block:
            p.emit(block)
        return nc

    statsd = scr("statsd", [TROWS // 128, 128, 8], F32) if "statsd" in DBG else None

    def ln_pass():
        NT = TROWS // 128
        cons = [[], []]
        st_tok = [None, None]
        st_list = []
        sq_guard = [None]
        for i in range(NT):
            b = i % 2
            r0 = i * 128
            ld = rec("sync",
                     lambda E, b=b, r0=r0: [E.dma_start(xt[b][:, :], x_in[r0:r0 + 128, :])],
                     waits=cons[b], inc=f"xt{b}", n=1, dma=True)
            s_sum = rec("vector",
                        lambda E, b=b: [E.reduce_sum(stats[:, b * 8:b * 8 + 1], xt[b][:, :], axis=AX.X)],
                        waits=[ld], inc="dv", n=1)
            a_sq = rec("scalar",
                       lambda E, b=b: [E.activation(sqscr[:, :], xt[b][:, :], AF.Square)],
                       waits=[ld, sq_guard[0]], inc="ac", n=1)
            s_sq = rec("vector",
                       lambda E, b=b: [E.reduce_sum(stats[:, b * 8 + 1:b * 8 + 2], sqscr[:, :],
                                                    axis=AX.X)],
                       waits=[a_sq], inc="dv", n=1)
            sq_guard[0] = s_sq

            t_m = rec("vector",
                      lambda E, b=b: [E.tensor_scalar(stats[:, b * 8 + 2:b * 8 + 3],
                                                      stats[:, b * 8:b * 8 + 1],
                                                      1.0 / D, None, ALU.mult)],
                      waits=[s_sum], inc="dv", n=1)
            t_m2 = rec("vector",
                       lambda E, b=b: [E.scalar_tensor_tensor(stats[:, b * 8 + 3:b * 8 + 4],
                                                              stats[:, b * 8:b * 8 + 1],
                                                              1.0 / (D * D),
                                                              stats[:, b * 8:b * 8 + 1],
                                                              ALU.mult, ALU.mult)],
                       waits=[s_sum], inc="dv", n=1)
            v_tok = rec("vector",
                        lambda E, b=b: [E.tensor_scalar(stats[:, b * 8 + 4:b * 8 + 5],
                                                        stats[:, b * 8 + 1:b * 8 + 2],
                                                        1.0 / D, stats[:, b * 8 + 3:b * 8 + 4],
                                                        ALU.mult, ALU.subtract)],
                        waits=[s_sq, t_m2], inc="dv", n=1)
            sq_tok = rec("scalar",
                         lambda E, b=b: [E.activation(stats[:, b * 8 + 5:b * 8 + 6],
                                                      stats[:, b * 8 + 4:b * 8 + 5], AF.Sqrt, bias=epsv[:, 0:1])],
                         waits=[v_tok], inc="ac", n=1)
            r_tok = rec("vector",
                        lambda E, b=b: [E.reciprocal(stats[:, b * 8 + 6:b * 8 + 7],
                                                     stats[:, b * 8 + 5:b * 8 + 6])],
                        waits=[sq_tok], inc="dv", n=1)
            rb_tok = rec("vector",
                         lambda E, b=b: [E.tensor_scalar(stats[:, b * 8 + 7:b * 8 + 8],
                                                         stats[:, b * 8 + 2:b * 8 + 3],
                                                         stats[:, b * 8 + 6:b * 8 + 7], -1.0,
                                                         ALU.mult, ALU.mult)],
                         waits=[r_tok, t_m], inc="dv", n=1)
            ap_tok = rec("vector",
                         lambda E, b=b: [E.tensor_scalar(lnt[b][:, :], xt[b][:, :],
                                                         stats[:, b * 8 + 6:b * 8 + 7],
                                                         stats[:, b * 8 + 7:b * 8 + 8],
                                                         ALU.mult, ALU.add)],
                         waits=[rb_tok, st_tok[b]], inc="dv", n=1)
            cons[b] = [ap_tok, s_sq]
            if statsd is not None:
                cons[b] = cons[b] + [rec("scalar",
                                         lambda E, b=b, i=i: [
                                             E.dma_start(statsd[i], stats[:, b * 8:b * 8 + 8])],
                                         waits=[rb_tok], inc=f"sd{b}", n=1, dma=True)]
            st_tok[b] = rec("scalar",
                            lambda E, b=b, r0=r0: [E.dma_start(lnscr[r0:r0 + 128, :], lnt[b][:, :])],
                            waits=[ap_tok], inc=f"lnt{b}", n=1, dma=True)
            st_list.append(st_tok[b])
        return st_tok, st_list

    ln_done, ln_st_list = ln_pass()
    do_prep_steps(8)
    wqkT_tok = wqkT_box[0]
    assert wqkT_tok is not None
    if PHASES <= 1:
        return _finish()
    p.barrier()

    zinit_tok = rec("vector", lambda E: [E.memset(T[:, :], 0.0), E.memset(ebcol[:, :], 0.0)],
                    waits=[ln_done[0], ln_done[1]], inc="dv", n=2)

    def proj_pass():
        NSEG = TROWS // 512
        ev_tok = [None, None]
        for s in range(NSEG):
            b = s % 2

            def em_ld(E, s=s, b=b):
                r = []
                for dc in range(16):
                    r.append(E.dma_start(lnfm[b][:, dc, :],
                                         lnscr[s * 512:(s + 1) * 512, dc * 128:(dc + 1) * 128],
                                         transpose=True))
                return r
            seg_ready = [ln_st_list[min(4 * s + 4, len(ln_st_list) - 1)],
                         ln_st_list[min(4 * s + 3, len(ln_st_list) - 1)]]
            ld = rec("sync", em_ld, waits=[wqkT_tok, ev_tok[b]] + seg_ready,
                     inc=f"lf{b}", n=16, dma=True)

            def em_mm(E, b=b):
                r = []
                for dc in range(16):
                    r.append(E.matmul(ps_S[:, 0:512], wqT[:, dc, :], lnfm[b][:, dc, :],
                                      start=(dc == 0), stop=(dc == 15)))
                for dc in range(16):
                    r.append(E.matmul(ps_S[:, 512:1024], wkT[:, dc, :], lnfm[b][:, dc, :],
                                      start=(dc == 0), stop=(dc == 15)))
                return r
            mm = rec("tensor", em_mm, waits=[ld, ev_tok[(b + 1) % 2]], inc="pe", n=32)

            def em_ev(E, s=s):
                return [
                    E.tensor_copy(qpre[:, s * 512:(s + 1) * 512], ps_S[:, 0:512]),
                    E.tensor_copy(kpre[:, s * 512:(s + 1) * 512], ps_S[:, 512:1024]),
                ]
            ev_tok[b] = rec("vector", em_ev, waits=[mm, zinit_tok], inc="dv", n=2)
        return ev_tok[(NSEG - 1) % 2]

    proj_tok = proj_pass()
    if PHASES <= 2:
        return _finish()
    p.barrier()


    tk = {
        "act_read_tr": None, "dve_read_tr": None,
        "act_read_b0": None, "dve_read_b0": None,
        "t_update": zinit_tok, "o_evac": None, "s_b": None,
        "vload": [None, None], "vcons": [None, None],
        "ebcol_guard": [None, None],
        "q_sb_free": None,
        "qg": [],
        "q_ln1q_guard": None,
    }

    def emit_chunk(c):
        is_ctx = c < NCTX
        b = c % 2
        r0 = c * CH

        tk["vload"][b] = rec("gpsimd",
                             lambda E, b=b, r0=r0: [E.dma_start(vtm[b][:, :], lnscr[r0:r0 + CH, :])],
                             waits=[tk["vcons"][b], ln_done[0], ln_done[1]],
                             inc=f"vt{b}", n=1, dma=True)

        t_k = rec("tensor",
                  lambda E, r0=r0: [E.matmul(TK, kpre[:, r0:r0 + CH], idf[:, :],
                                             start=True, stop=True, is_transpose=True,
                                             skip_group_check=True)],
                  waits=[proj_tok, tk["act_read_tr"], tk["dve_read_tr"]] + tk["qg"],
                  inc="pe", n=1)
        t_q = None
        if not is_ctx:
            t_q = rec("tensor",
                      lambda E, r0=r0: [E.matmul(TQ, qpre[:, r0:r0 + CH], idf[:, :],
                                                 start=True, stop=True, is_transpose=True,
                                                 skip_group_check=True)],
                      inc="pe", n=1)

        a_k = rec("scalar", lambda E: [E.activation(ktm[:, :], TK, AF.Sigmoid)],
                  waits=[t_k], inc="ac", n=1)
        a_e = rec("scalar", lambda E: [E.activation(t1c[:, :], ktm[:, :], AF.Exp, scale=-1.0)],
                  waits=[a_k], inc="ac", n=1)
        a_l = rec("scalar", lambda E: [E.activation(t2c[:, :], t1c[:, :], AF.Ln, bias=1.0)],
                  waits=[a_e], inc="ac", n=1)
        d_gf = rec("vector",
                   lambda E: [E.scalar_tensor_tensor(gftm[:, :], ktm[:, :], -1.0, t2c[:, :],
                                                     ALU.mult, ALU.subtract)],
                   waits=[a_l], inc="dv", n=1)

        mm_cs = rec("tensor",
                    lambda E: [E.matmul(B65, u65[:, :], gftm[:, :], start=True, stop=True,
                                        skip_group_check=True)],
                    waits=[d_gf, tk["act_read_b0"], tk["dve_read_b0"]], inc="pe", n=1)

        a_eb = rec("scalar", lambda E: [E.activation(expb[:, :], B65, AF.Exp)],
                   waits=[mm_cs], inc="ac", n=1)
        a_en = rec("scalar", lambda E: [E.activation(enb[:, :], PS[0:64, 0:128], AF.Exp, scale=-1.0)],
                   waits=[mm_cs], inc="ac", n=1)
        tk["act_read_b0"] = a_en

        t_gf = rec("tensor",
                   lambda E: [E.matmul(GFP, gftm[:, :], idf[0:64, 0:64],
                                       start=True, stop=True, is_transpose=True,
                                       skip_group_check=True)],
                   waits=[d_gf], inc="pe", n=1)
        d_bl = rec("vector", lambda E: [E.reduce_sum(blc[:, 0:1], GFP, axis=AX.X)],
                   waits=[t_gf], inc="dv", n=1)
        a_ebc = rec("scalar",
                    lambda E, b=b: [E.activation(ebcol[:, b:b + 1], blc[:, 0:1], AF.Exp)],
                    waits=[d_bl, tk["ebcol_guard"][b]], inc="ac", n=1)
        mm_ebb = rec("tensor",
                     lambda E: [E.matmul(EBB, onesf[64:65, 0:64], expb[64:65, :],
                                         start=True, stop=True, skip_group_check=True)],
                     waits=[a_eb], inc="pe", n=1)

        if not is_ctx:
            d_nm = rec("vector", lambda E: [E.reduce_max(nmx[:, 0:1], TQ, axis=AX.X, negate=True)],
                       waits=[t_q], inc="dv", n=1)
            a_qx = rec("scalar", lambda E: [E.activation(qxp[:, :], TQ, AF.Exp, bias=nmx[:, 0:1])],
                       waits=[d_nm], inc="ac", n=1)

            d_qs = rec("vector", lambda E: [E.reduce_sum(nmx[:, 1:2], qxp[:, :], axis=AX.X)],
                       waits=[a_qx], inc="dv", n=1)
            d_qr = rec("vector", lambda E: [E.reciprocal(nmx[:, 2:3], nmx[:, 1:2])],
                       waits=[d_qs], inc="dv", n=1)
            d_q = rec("vector",
                      lambda E: [E.scalar_tensor_tensor(qetm[:, :], qxp[:, :], nmx[:, 2:3],
                                                        expb[0:64, :], ALU.mult, ALU.mult)],
                      waits=[d_qr, a_eb], inc="dv", n=1)
            tk["act_read_tr"] = a_qx
        else:
            tk["act_read_tr"] = a_k

        d_ke1 = rec("vector", lambda E: [E.tensor_mul(ketm[:, :], ktm[:, :], enb[:, :])],
                    waits=[a_en], inc="dv", n=1)
        d_kd = rec("vector",
                   lambda E: [E.tensor_tensor(kdb[:, :], ketm[:, :], EBB, ALU.mult)],
                   waits=[d_ke1, mm_ebb], inc="dv", n=1)

        mm_at = None
        if not is_ctx:
            t_qe = rec("tensor",
                       lambda E: [E.matmul(TQE, qetm[:, :], idf[0:64, 0:64],
                                           start=True, stop=True, is_transpose=True,
                                           skip_group_check=True)],
                       waits=[d_q], inc="pe", n=1)
            t_ke = rec("tensor",
                       lambda E: [E.matmul(TKE, ketm[:, :], idf[0:64, 0:64],
                                           start=True, stop=True, is_transpose=True,
                                           skip_group_check=True)],
                       waits=[d_ke1], inc="pe", n=1)

            def em_fm(E):
                return [
                    E.tensor_copy(qefmb[:, :], TQE),
                    E.tensor_copy(kefmb[:, :], TKE),
                ]
            d_fm = rec("vector", em_fm, waits=[t_qe, t_ke], inc="dv", n=2)
            tk["dve_read_tr"] = d_fm

            mm_at = rec("tensor",
                        lambda E: [E.matmul(ATP, kefmb[:, 0:64], qefmb[:, 0:64],
                                            start=True, stop=True, skip_group_check=True)],
                        waits=[d_fm], inc="pe", n=1)
        else:
            tk["dve_read_tr"] = d_kd

        if not is_ctx:
            d_at = rec("vector",
                       lambda E: [E.tensor_tensor(atb[:, :], ATP, maskat[:, :], ALU.mult)],
                       waits=[mm_at], inc="dv", n=1)
            tk["dve_read_b0"] = d_at
        else:
            tk["dve_read_b0"] = d_kd
        a_sb = None
        if not is_ctx:
            a_sb = rec("scalar",
                       lambda E: [E.activation(S_b[:, :], T[:, :], AF.Copy)],
                       waits=[tk["t_update"]], inc="ac", n=1)

        mm_o = None
        if not is_ctx:
            def em_o(E, b=b):
                r = []
                for dvc in range(16):
                    col = 3072 + dvc * 64
                    r.append(E.matmul(PS[:, col:col + 64], vtm[b][:, dvc * 128:(dvc + 1) * 128],
                                      atb[:, :], start=True, stop=False,
                                      skip_group_check=True))
                    r.append(E.matmul(PS[:, col:col + 64], S_b[:, dvc * 128:(dvc + 1) * 128],
                                      qefmb[:, :], start=False, stop=True,
                                      skip_group_check=True))
                return r
            mm_o = rec("tensor", em_o,
                       waits=[tk["dve_read_b0"], a_sb, tk["vload"][b], tk["o_evac"],
                              tk["q_sb_free"]] + tk["qg"],
                       inc="pe", n=32)

        def em_s(E, b=b):
            r = []
            for q4 in range(4):
                r.append(E.matmul(ps_S[:, q4 * 512:(q4 + 1) * 512], kdb[:, :],
                                  vtm[b][:, q4 * 512:(q4 + 1) * 512], start=True, stop=True))
            return r
        mm_s = rec("tensor", em_s,
                   waits=[d_kd, tk["vload"][b], tk["t_update"]] + tk["qg"], inc="pe", n=4)
        tk["vcons"][b] = mm_s if is_ctx else mm_o
        tk["qg"] = []

        d_T = rec("vector",
                  lambda E, b=b: [E.scalar_tensor_tensor(T[:, :], T[:, :], ebcol[:, b:b + 1],
                                                         ps_S[:, :], ALU.mult, ALU.add)],
                  waits=([mm_s, a_ebc, a_sb] if not is_ctx else [mm_s, a_ebc]),
                  inc="dv", n=1)
        tk["t_update"] = d_T
        tk["ebcol_guard"][b] = d_T

        if not is_ctx:
            qi = (c - NCTX) % 8
            d_oe = rec("vector",
                       lambda E, qi=qi: [E.tensor_copy(o_fm[:, :, qi * 64:(qi + 1) * 64],
                                                       ps_o.rearrange("p (a b) -> p a b", a=16))],
                       waits=[mm_o], inc="dv", n=1)
            tk["o_evac"] = d_oe
        if glad is not None and c == NCTX:
            def em_dbg(E):
                return [
                    E.dma_start(glad["ktm"][:, :], ktm[:, :]),
                    E.dma_start(glad["expb"][:, :], expb[:, :]),
                    E.dma_start(glad["qetm"][:, :], qetm[:, :]),
                    E.dma_start(glad["ketm"][:, :], ketm[:, :]),
                    E.dma_start(glad["atb"][:, :], atb[:, :]),
                    E.dma_start(glad["kdb"][:, :], kdb[:, :]),
                    E.dma_start(glad["ofm0"][:, :, :], o_fm[:, :, 0:64]),
                    E.dma_start(glad["qefmb"][:, :], qefmb[:, :]),
                    E.dma_start(glad["S_b"][:, :], S_b[:, :]),
                    E.dma_start(glad["ebcol"][:, :], ebcol[:, :]),
                    E.dma_start(glad["T"][:, :], T[:, :]),
                ]
            rec("scalar", em_dbg, waits=[tk["t_update"], tk["o_evac"]], inc="dbg9", n=11, dma=True)

    wo_psum_free = [None, None]

    glad = None
    if "glad" in DBG:
        def dout(name, shape, dt):
            return nc.dram_tensor(name, shape, dt, kind="ExternalOutput")
        glad = {
            "ktm": dout("g_ktm", [64, 128], F32), "expb": dout("g_expb", [65, 128], F32),
            "qetm": dout("g_qetm", [64, 128], F32), "ketm": dout("g_ketm", [64, 128], F32),
            "atb": dout("g_atb", [64, 64], BF16), "S_b": dout("g_Sb", [128, D], BF16),
            "ebcol": dout("g_ebcol", [128, 2], F32), "T": dout("g_T", [128, D], F32),
            "kdb": dout("g_kdb", [64, 128], BF16),
            "ofm0": dout("g_ofm0", [128, 16, 64], F32),
            "qefmb": dout("g_qefmb", [128, 64], BF16),
        }
    ofmd = scr("ofmd", [4, 128, 16, 512], BF16) if "ofmd" in DBG else None
    qkd = scr("qkd", [2, 128, TROWS], F32) if "qkd" in DBG else None
    sgd = scr("sgd", [4, 128, 16, 512], BF16) if "sgd" in DBG else None
    rbd = scr("rbd", [4, 128, 512], F32) if "rbd" in DBG else None
    gcold = nc.dram_tensor("gcold", [128, 16], F32, kind="ExternalOutput") if "gcold" in DBG else None

    def emit_quarter(q):
        mq0 = q * 512
        xr0 = CTX + mq0
        if ofmd is not None:
            rec("scalar", lambda E, q=q: [E.dma_start(ofmd[q], o_fm[:, :, :])],
                waits=[tk["o_evac"]], inc="dbg1", n=1, dma=True)
        if qkd is not None and q == 0:
            rec("scalar", lambda E: [E.dma_start(qkd[0], qpre[:, :]),
                                     E.dma_start(qkd[1], kpre[:, :])],
                waits=[proj_tok], inc="dbg2", n=2, dma=True)

        def em_lq(E):
            r = []
            for dc in range(16):
                r.append(E.dma_start(ln1q[:, dc, :],
                                     lnscr[xr0:xr0 + 512, dc * 128:(dc + 1) * 128], transpose=True))
            return r
        lq = rec("sync", em_lq, waits=[tk["q_ln1q_guard"], ln_done[0], ln_done[1]],
                 inc="l1q", n=16, dma=True)

        def emit_rms():
            prev_mm = None
            mm_ms = None
            for oc in range(16):
                a_sq = rec("scalar",
                           lambda E, oc=oc: [E.activation(osq[:, oc % 2, :], o_fm[:, oc, :],
                                                          AF.Square)],
                           waits=[tk["o_evac"], prev_mm], inc="ac", n=1)
                mm_ms = rec("tensor",
                            lambda E, oc=oc: [E.matmul(PS[0:1, 2048:2560], ones1b[:, :],
                                                       osq[:, oc % 2, :],
                                                       start=(oc == 0), stop=(oc == 15),
                                                       skip_group_check=True)],
                            waits=[a_sq, tk["t_update"]], inc="pe", n=1)
                prev_mm = mm_ms
            a_rs = rec("scalar",
                       lambda E: [E.activation(rrow[0:1, 0:512], PS[0:1, 2048:2560], AF.Sqrt,
                                               scale=1.0 / D, bias=epsv[0:1, 0:1])],
                       waits=[mm_ms], inc="ac", n=1)
            d_rr = rec("vector", lambda E: [E.reciprocal(rrow[0:1, 512:1024], rrow[0:1, 0:512])],
                       waits=[a_rs], inc="dv", n=1)
            mm_bc = rec("tensor",
                        lambda E: [E.matmul(PS[:, 2560:3072], onesf[0:1, :], rrow[0:1, 512:1024],
                                            start=True, stop=True, skip_group_check=True)],
                        waits=[d_rr], inc="pe", n=1)
            d_bc = rec("vector", lambda E: [E.tensor_copy(rstd_bc[:, :], PS[:, 2560:3072])],
                       waits=[mm_bc], inc="dv", n=1)
            return mm_bc, d_bc

        wst_tok = [None, None]
        sg_toks = []
        mm_bc = None
        d_bc = None
        for ocg in range(4):
            bi = ocg % 2

            def em_wst(E, ocg=ocg, bi=bi):
                r = []
                for d0 in range(0, 16, 4):
                    r.append(E.dma_start(
                        wstA[bi][:, d0:d0 + 4, :],
                        wogt2[d0:d0 + 4, :, ocg * 512:(ocg + 1) * 512]
                        .rearrange("dc p w -> p dc w")))
                    r.append(E.dma_start(
                        wstB[bi][:, d0:d0 + 4, :],
                        wigt2[d0:d0 + 4, :, ocg * 512:(ocg + 1) * 512]
                        .rearrange("dc p w -> p dc w")))
                return r
            wl = rec("sync", em_wst,
                     waits=[repT_done["wog"], repT_done["wig"], wst_tok[bi]],
                     inc=f"ws{bi}", n=8, dma=True)

            last_mm = None
            for oci in range(4):
                oc = ocg * 4 + oci

                def em_go(E, oci=oci, bi=bi):
                    r = []
                    for dc in range(16):
                        r.append(E.matmul(PS[:, oci * 512:(oci + 1) * 512],
                                          wstA[bi][:, dc, oci * 128:(oci + 1) * 128],
                                          o_fm[:, dc, :], start=(dc == 0), stop=False))
                    for dc in range(16):
                        r.append(E.matmul(PS[:, oci * 512:(oci + 1) * 512],
                                          wstB[bi][:, dc, oci * 128:(oci + 1) * 128],
                                          ln1q[:, dc, :], start=False, stop=(dc == 15)))
                    return r
                waits = [wl, lq, tk["o_evac"]]
                if ocg == 0 and oci == 0:
                    waits += [tk["t_update"], tk["act_read_tr"], tk["dve_read_tr"],
                              tk["act_read_b0"], tk["dve_read_b0"]]
                if len(sg_toks) >= 4:
                    waits += [sg_toks[len(sg_toks) - 4]]
                mm_go = rec("tensor", em_go, waits=waits, inc="pe", n=32)
                last_mm = mm_go
                a_sg = rec("scalar",
                           lambda E, oc=oc, oci=oci: [E.activation(sgq[:, oc, :],
                                                                   PS[:, oci * 512:(oci + 1) * 512],
                                                                   AF.Silu)],
                           waits=[mm_go], inc="ac", n=1)
                sg_toks.append(a_sg)
            wst_tok[bi] = last_mm
            if ocg == 0:
                mm_bc, d_bc = emit_rms()

        dbg_sg_tok = None
        if sgd is not None:
            dbg_sg_tok = rec("scalar", lambda E, q=q: [E.dma_start(sgd[q], sgq[:, :, :])],
                             waits=[sg_toks[-1]], inc="dbg3", n=1, dma=True)
        if "gcold" in DBG and q == 0:
            rec("scalar", lambda E: [E.dma_start(gcold[:, :], gcol[:, :])],
                waits=[gcol_tok], inc="dbg5", n=1, dma=True)
        o2_last = dbg_sg_tok
        for oc in range(16):
            o2_a = rec("vector",
                       lambda E, oc=oc: [E.scalar_tensor_tensor(o2t[:, :], o_fm[:, oc, :],
                                                                gcol[:, oc:oc + 1],
                                                                rstd_bc[:, :], ALU.mult, ALU.mult)],
                       waits=[sg_toks[oc], d_bc, gcol_tok, o2_last], inc="dv", n=1)
            o2_last = rec("vector",
                          lambda E, oc=oc: [E.tensor_tensor(sgq[:, oc, :], o2t[:, :],
                                                            sgq[:, oc, :], ALU.mult)],
                          waits=[o2_a], inc="dv", n=1)
        tk["q_sb_free"] = o2_last

        def em_wot(E):
            r = []
            for d0 in range(0, 16, 2):
                r.append(E.dma_start(
                    woT[:, d0:d0 + 2, :],
                    wot2[d0:d0 + 2, :, :].rearrange("dv p w -> p dv w")))
            return r
        wot_l = rec("sync", em_wot, waits=[repT_done["wo"], wst_tok[0], wst_tok[1]],
                    inc="wot", n=8, dma=True)

        last_st = None
        sq2_guard = [None]
        for t in range(4):
            pbase = 2048 if t % 2 == 0 else 0
            xl = rec("scalar",
                     lambda E, t=t: [E.dma_start(xres[:, :],
                                                 x_in[xr0 + t * 128:xr0 + (t + 1) * 128, :])],
                     waits=[wst_tok[0], wst_tok[1], last_st], inc="xr", n=1, dma=True)

            def em_wmm(E, t=t, pbase=pbase):
                r = []
                for dvc in range(16):
                    for os_ in range(4):
                        r.append(E.matmul(PS[:, pbase + os_ * 512:pbase + (os_ + 1) * 512],
                                          sgq[:, dvc, t * 128:(t + 1) * 128],
                                          woT[:, dvc, os_ * 512:(os_ + 1) * 512],
                                          start=(dvc == 0), stop=(dvc == 15),
                                          skip_group_check=True))
                return r
            mm_wo = rec("tensor", em_wmm,
                        waits=[wot_l, o2_last, wo_psum_free[t % 2]], inc="pe", n=64)

            d_x1 = rec("vector",
                       lambda E, pbase=pbase: [E.tensor_add(xres[:, :], xres[:, :],
                                                            PS[:, pbase:pbase + 2048])],
                       waits=[mm_wo, xl], inc="dv", n=1)
            wo_psum_free[t % 2] = d_x1
            s_sum = rec("vector", lambda E: [E.reduce_sum(stats[:, 0:1], xres[:, :], axis=AX.X)],
                        waits=[d_x1], inc="dv", n=1)
            a_sq2 = rec("scalar", lambda E: [E.activation(sqscr2[:, :], xres[:, :], AF.Square)],
                        waits=[d_x1, sq2_guard[0]], inc="ac", n=1)
            s_sq = rec("vector",
                       lambda E: [E.reduce_sum(stats[:, 1:2], sqscr2[:, :], axis=AX.X)],
                       waits=[a_sq2], inc="dv", n=1)
            sq2_guard[0] = s_sq

            t_m = rec("vector",
                      lambda E: [E.tensor_scalar(stats[:, 2:3], stats[:, 0:1], 1.0 / D,
                                                 None, ALU.mult)],
                      waits=[s_sum], inc="dv", n=1)
            t_m2 = rec("vector",
                       lambda E: [E.scalar_tensor_tensor(stats[:, 3:4], stats[:, 0:1],
                                                         1.0 / (D * D), stats[:, 0:1],
                                                         ALU.mult, ALU.mult)],
                       waits=[s_sum], inc="dv", n=1)
            v_tok = rec("vector",
                        lambda E: [E.tensor_scalar(stats[:, 4:5], stats[:, 1:2], 1.0 / D,
                                                   stats[:, 3:4], ALU.mult, ALU.subtract)],
                        waits=[s_sq, t_m2], inc="dv", n=1)
            sq_tok = rec("scalar", lambda E: [E.activation(stats[:, 5:6], stats[:, 4:5],
                                                           AF.Sqrt, bias=epsv[:, 0:1])],
                         waits=[v_tok], inc="ac", n=1)
            r_tok = rec("vector",
                        lambda E: [E.reciprocal(stats[:, 6:7], stats[:, 5:6])],
                        waits=[sq_tok], inc="dv", n=1)
            rb_tok = rec("vector",
                         lambda E: [E.tensor_scalar(stats[:, 7:8], stats[:, 2:3], stats[:, 6:7],
                                                    -1.0, ALU.mult, ALU.mult)],
                         waits=[r_tok, t_m], inc="dv", n=1)
            a_ln2 = rec("vector",
                        lambda E: [E.tensor_scalar(ln2t[:, :], xres[:, :], stats[:, 6:7],
                                                   stats[:, 7:8], ALU.mult, ALU.add)],
                        waits=[rb_tok, s_sq], inc="dv", n=1)

            def em_st(E, t=t):
                return [
                    E.dma_start(x1scr[mq0 + t * 128:mq0 + (t + 1) * 128, :], xres[:, :]),
                    E.dma_start(ln2scr[mq0 + t * 128:mq0 + (t + 1) * 128, :], ln2t[:, :]),
                ]
            last_st = rec("scalar", em_st, waits=[a_ln2], inc="qst", n=2, dma=True)

        if rbd is not None:
            rec("scalar", lambda E, q=q: [E.dma_start(rbd[q], rstd_bc[:, :])],
                waits=[d_bc], inc="dbg4", n=1, dma=True)
        tk["qg"] = [wo_psum_free[0], wo_psum_free[1]]
        tk["q_ln1q_guard"] = last_st
        return last_st

    q_done = []
    if SKIP_ATTN:
        q_done.append(ln_done[1])
    else:
        for c in range(NCH):
            emit_chunk(c)
            do_prep_steps(9)
            if PHASES >= 4 and c >= NCTX and (c - NCTX) % 8 == 7:
                q_done.append(emit_quarter((c - NCTX) // 8))
                do_prep_steps(12)
    do_prep_steps(10000)
    if PHASES <= 3:
        return _finish()
    p.barrier()

    def w2_transform():
        zw_flat = zw.ap().rearrange("g (s lr) d -> (g s) (lr d)", s=2)
        w2p_v = w2p.ap().rearrange("q gb (s lr) gi c -> lr gb s gi q c", s=2)
        sp_tok = [None, None]
        ev_toks = []
        wr_toks = []
        for qq in range(4):
            bi = qq % 2
            ld = rec("sync",
                     lambda E, qq=qq, bi=bi: [
                         E.dma_start(span[bi][:, :], zw_flat[:, qq * 32768:(qq + 1) * 32768])],
                     waits=[("pw0", p.cnt.get("pw0", 0)), ("pw1", p.cnt.get("pw1", 0)),
                            sp_tok[bi], q_done[-1]],
                     inc=f"sp{bi}", n=1, dma=True)
            for wv in range(16):
                eb = wv % 2

                def em_mm(E, bi=bi, wv=wv):
                    r = []
                    for j in range(4):
                        cs = wv * 4 + j
                        r.append(E.matmul(PS[:, j * 512:(j + 1) * 512], h64i2b[:, :],
                                          span[bi][:, cs * 512:(cs + 1) * 512],
                                          start=True, stop=True))
                    return r
                mm = rec("tensor", em_mm,
                         waits=[ld, ev_toks[-1] if ev_toks else None], inc="pe", n=4)
                a_ev = rec("vector",
                           lambda E, eb=eb: [E.tensor_copy(w2ev[eb][:, :], PS[:, 0:2048])],
                           waits=[mm, wr_toks[-2] if len(wr_toks) >= 2 else None],
                           inc="dv", n=1)
                ev_toks.append(a_ev)
                lq_ = (qq * 32768 + wv * 2048) // D

                def em_wr(E, lq_=lq_, eb=eb):
                    r = []
                    for dqi in range(4):
                        dst = w2p_v[lq_:lq_ + 1, :, :, :, dqi:dqi + 1].squeeze(0).squeeze(3)
                        r.append(E.dma_start(dst, w2ev[eb][:, dqi * 512:(dqi + 1) * 512]))
                    return r
                wr = rec("scalar", em_wr, waits=[a_ev], inc=f"we{eb}", n=4, dma=True)
                wr_toks.append(wr)
            sp_tok[bi] = wr_toks[-1]
        return [("we0", p.cnt.get("we0", 0)), ("we1", p.cnt.get("we1", 0))]

    if PHASES <= 4:
        return _finish()
    w2p_tok = w2_transform()
    if PHASES <= 5:
        return _finish()
    p.barrier()

    prev_tt_guard = []

    def ffn_tt(tt):
        tok0 = tt * 1024

        def em_ln2(E, tok0=tok0):
            r = []
            for dc in range(16):
                r.append(E.dma_start(ln2fm[:, dc, :],
                                     ln2scr[tok0:tok0 + 1024, dc * 128:(dc + 1) * 128],
                                     transpose=True))
            return r
        lq = rec("sync", em_ln2, waits=[q_done[-1]] + list(prev_tt_guard) + list(w2p_tok),
                 inc="lfm", n=16, dma=True)

        st_tok = [None, None]
        h_toks = []
        for g in range(64):
            bi = g % 2

            def em_st(E, g=g, bi=bi):
                return [
                    E.dma_start(w1st[bi][:, :, :].rearrange("p a b -> p (a b)"),
                                w1t2[g].rearrange("p a b -> p (a b)")),
                    E.dma_start(w3st[bi][:, :, :].rearrange("p a b -> p (a b)"),
                                w3t2[g].rearrange("p a b -> p (a b)")),
                ]
            sl = rec("sync", em_st,
                     waits=[("pw0", p.cnt.get("pw0", 0)), ("pw1", p.cnt.get("pw1", 0)),
                            st_tok[bi]],
                     inc=f"fs{bi}", n=2, dma=True)

            pA = 0 if g % 2 == 0 else 2048
            pB = 1024 if g % 2 == 0 else 3072

            def em_up(E, bi=bi, pA=pA, pB=pB):
                r = []
                for th in range(2):
                    for dc in range(16):
                        r.append(E.matmul(PS[:, pA + th * 512:pA + (th + 1) * 512],
                                          w1st[bi][:, dc, :],
                                          ln2fm[:, dc, th * 512:(th + 1) * 512],
                                          start=(dc == 0), stop=(dc == 15)))
                for th in range(2):
                    for dc in range(16):
                        r.append(E.matmul(PS[:, pB + th * 512:pB + (th + 1) * 512],
                                          w3st[bi][:, dc, :],
                                          ln2fm[:, dc, th * 512:(th + 1) * 512],
                                          start=(dc == 0), stop=(dc == 15)))
                return r
            waits = [sl, lq]
            if len(h_toks) >= 2:
                waits.append(h_toks[-2][1])
            if g < 2:
                waits.extend(prev_tt_guard)
            mm_up = rec("tensor", em_up, waits=waits, inc="pe", n=64)
            st_tok[bi] = mm_up

            a_si = rec("scalar",
                       lambda E, g=g, pA=pA: [E.activation(h_fm[:, g, :], PS[:, pA:pA + 1024],
                                                           AF.Silu)],
                       waits=[mm_up], inc="ac", n=1)
            d_h = rec("vector",
                      lambda E, g=g, pB=pB: [E.tensor_tensor(h_fm[:, g, :], h_fm[:, g, :],
                                                             PS[:, pB:pB + 1024], ALU.mult)],
                      waits=[a_si], inc="dv", n=1)

            def em_h128(E, g=g, pA=pA):
                return [
                    E.matmul(PS[:, pA:pA + 512], h128b[:, :], h_fm[:, g, 0:512],
                             start=True, stop=True),
                    E.matmul(PS[:, pA + 512:pA + 1024], h128b[:, :], h_fm[:, g, 512:1024],
                             start=True, stop=True),
                ]
            mm_h = rec("tensor", em_h128, waits=[d_h], inc="pe", n=2)
            d_y = rec("vector",
                      lambda E, g=g, pA=pA: [E.tensor_copy(h_fm[:, g, :], PS[:, pA:pA + 1024])],
                      waits=[mm_h], inc="dv", n=1)
            h_toks.append((d_h, d_y))

        w2_buf_tok = [None, None]
        last_ev = {t: None for t in range(8)}
        last_out_dv = []
        for dq in range(4):
            mm_w2 = None
            for gblk in range(8):
                bi = gblk % 2

                def em_wl(E, dq=dq, gblk=gblk, bi=bi):
                    return [E.dma_start(w2pb[bi][:, :, :].rearrange("p a b -> p (a b)"),
                                        w2p[dq, gblk].rearrange("p a b -> p (a b)"))]
                wl_waits = list(w2p_tok) + [w2_buf_tok[bi]]
                if dq == 0 and gblk < 2:
                    wl_waits += [h_toks[-1][1]]
                wl = rec("sync", em_wl, waits=wl_waits, inc=f"wp{bi}", n=1, dma=True)

                def em_w2(E, gblk=gblk, bi=bi):
                    r = []
                    for gin in range(8):
                        g = gblk * 8 + gin
                        for tg in range(8):
                            r.append(E.matmul(PS[:, tg * 512:(tg + 1) * 512],
                                              h_fm[:, g, tg * 128:(tg + 1) * 128],
                                              w2pb[bi][:, gin, :],
                                              start=(gblk == 0 and gin == 0),
                                              stop=(gblk == 7 and gin == 7),
                                              skip_group_check=True))
                    return r
                waits = [wl, h_toks[-1][1], h_toks[-2][1]]
                if gblk == 0 and last_out_dv:
                    waits += [last_out_dv[-1]]
                mm_w2 = rec("tensor", em_w2, waits=waits, inc="pe", n=64)
                w2_buf_tok[bi] = mm_w2

            for tg in range(8):
                xl = rec("scalar",
                         lambda E, tok0=tok0, dq=dq, tg=tg: [
                             E.dma_start(x1t[tg][:, :],
                                         x1scr[tok0 + tg * 128:tok0 + (tg + 1) * 128,
                                               dq * 512:(dq + 1) * 512])],
                         waits=list(w2p_tok) + [last_ev[tg]], inc=f"x1{tg}", n=1, dma=True)
                d_o = rec("vector",
                          lambda E, tg=tg: [
                              E.scalar_tensor_tensor(outt[tg][:, :],
                                                     PS[:, tg * 512:(tg + 1) * 512], RSQN,
                                                     x1t[tg][:, :], ALU.mult, ALU.add)],
                          waits=[mm_w2, xl, last_ev[tg]], inc="dv", n=1)
                last_out_dv.append(d_o)
                st = rec("scalar",
                         lambda E, tok0=tok0, dq=dq, tg=tg: [
                             E.dma_start(out[tok0 + tg * 128:tok0 + (tg + 1) * 128,
                                             dq * 512:(dq + 1) * 512], outt[tg][:, :])],
                         waits=[d_o], inc=f"os{tg}", n=1, dma=True)
                last_ev[tg] = st
        prev_tt_guard[:] = [(f"os{t}", p.cnt.get(f"os{t}", 0)) for t in range(8)]

    ffn_tt(0)
    ffn_tt(1)

    rec("sync", lambda E: [],
        waits=[(f"os{t}", p.cnt.get(f"os{t}", 0)) for t in range(8)], n=0)

    with nc.Block() as block:
        p.emit(block)
    return nc


_NC_CACHE = None
_LAST_P = None


def _get_nc():
    global _NC_CACHE
    if _NC_CACHE is None:
        _NC_CACHE = build_nc()
    return _NC_CACHE


def kernel(**inputs):
    x = np.ascontiguousarray(np.asarray(inputs["L0_attn_hidden_states"], dtype=np.float32))
    ws = {
        "wq": np.ascontiguousarray(np.asarray(inputs["L0_attn_Wq"], np.float32)),
        "wk": np.ascontiguousarray(np.asarray(inputs["L0_attn_Wk"], np.float32)),
        "wog": np.ascontiguousarray(np.asarray(inputs["L0_attn_Wog"], np.float32)),
        "wig": np.ascontiguousarray(np.asarray(inputs["L0_attn_Wig"], np.float32)),
        "wo": np.ascontiguousarray(np.asarray(inputs["L0_attn_Wo"], np.float32)),
        "gnw": np.ascontiguousarray(np.asarray(inputs["L0_attn_g_norm_w"], np.float32)),
        "w1": np.ascontiguousarray(np.asarray(inputs["L0_ffn_W1"], np.float32)),
        "w2": np.ascontiguousarray(np.asarray(inputs["L0_ffn_W2"], np.float32)),
        "w3": np.ascontiguousarray(np.asarray(inputs["L0_ffn_W3"], np.float32)),
    }
    in_maps = []
    for cid in range(NCORES):
        b, s = cid // 4, cid % 4
        xin = np.zeros((TROWS, D), np.float32)
        xin[CTX:] = x[b, s * TLOC:(s + 1) * TLOC]
        if s > 0:
            xin[:CTX] = x[b, s * TLOC - CTX:s * TLOC]
        m = {"x": xin}
        m.update(ws)
        in_maps.append(m)
    nc = _get_nc()
    res = run_bass_kernel_spmd(nc, in_maps, core_ids=list(range(NCORES)))
    outa = np.empty((B, L, D), np.float32)
    for cid in range(NCORES):
        b, s = cid // 4, cid % 4
        outa[b, s * TLOC:(s + 1) * TLOC] = res.results[cid]["out"]
    return outa


# version 06
# speedup vs baseline: 1.0280x; 1.0280x over previous
import math
import numpy as np
import ml_dtypes

import concourse.bass as bass
import concourse.mybir as mybir
from concourse.bass_utils import run_bass_kernel_spmd

F32 = mybir.dt.float32
BF16 = mybir.dt.bfloat16
AF = mybir.ActivationFunctionType
ALU = mybir.AluOpType
AX = mybir.AxisListType

B, L, D = 2, 8192, 2048
NCORES = 8
TLOC = 2048
CTX = 1024
TROWS = TLOC + CTX
CH = 64
NCH = TROWS // CH
NCTX = CTX // CH
DK = 128
HID = 8192
EPS = 1e-5
RSQN = 1.0 / math.sqrt(float(HID))

DBG = set()
PHASES = 6
GATE_CASTS = False
SKIP_ATTN = False
T_UPTO = 3
T_CS = 64
T_QQ = 4
DEFER_CASTS = False

ENGS = ("sync", "gpsimd", "scalar", "vector", "tensor")


def _hadamard(n):
    h = np.ones((n, n), np.float32)
    for i in range(n):
        for j in range(n):
            if bin(i & j).count("1") & 1:
                h[i, j] = -1.0
    return h


class P:

    def __init__(self, nc):
        self.nc = nc
        self.ops = {e: [] for e in ENGS}
        self.sems = {}
        self.cnt = {}
        for s in ("sy", "sc", "gp", "pe", "ac", "dv", "ph"):
            self.sems[s] = nc.alloc_semaphore(f"s_{s}")
            self.cnt[s] = 0

    def rec(self, eng, emit, waits=(), inc=None, n=1, dma=False):
        if inc is not None and inc not in self.sems:
            self.sems[inc] = self.nc.alloc_semaphore(f"s_{inc}")
            self.cnt[inc] = 0
        amt = 16 if dma else 1
        ws = []
        for w in waits:
            if w is None:
                continue
            s, v = w
            assert v <= self.cnt[s], f"wait on future token: {s} {v} > {self.cnt[s]}"
            ws.append((s, v))
        tok = None
        if inc is not None and n > 0:
            self.cnt[inc] += amt * n
            tok = (inc, self.cnt[inc])
        self.ops[eng].append((tuple(ws), emit, inc, amt, n))
        return tok

    def barrier(self):
        self.cnt["ph"] += 5
        target = self.cnt["ph"]
        ph = self.sems["ph"]
        for e in ENGS:
            self.ops[e].append(((), (lambda E, ph=ph: [E.sem_inc(ph, 1)]), None, 1, 1))
            self.ops[e].append(((("ph", target),), None, None, 1, 0))
        return ("ph", target)

    def emit(self, block):
        for e in ENGS:
            oplist = self.ops[e]
            sems = self.sems

            def runner(E, oplist=oplist):
                hw = {}
                for ws, em, inc, amt, n in oplist:
                    for s, v in ws:
                        if hw.get(s, 0) < v:
                            E.wait_ge(sems[s], v)
                            hw[s] = v
                    if em is None:
                        continue
                    instrs = em(E)
                    if inc is not None:
                        assert len(instrs) == n, f"{e}: {len(instrs)} != {n}"
                        for i in instrs:
                            i.then_inc(sems[inc], amt)

            getattr(block, e)(runner)


def build_nc():
    nc = bass.Bass("TRN2", target_bir_lowering=False)

    def din(name, shape):
        return nc.dram_tensor(name, shape, F32, kind="ExternalInput")

    x_in = din("x", [TROWS, D])
    wq, wk = din("wq", [DK, D]), din("wk", [DK, D])
    wog, wig, wo = din("wog", [D, D]), din("wig", [D, D]), din("wo", [D, D])
    gnw = din("gnw", [D])
    w1, w2, w3 = din("w1", [HID, D]), din("w2", [D, HID]), din("w3", [HID, D])
    out = nc.dram_tensor("out", [TLOC, D], F32, kind="ExternalOutput")

    def scr(name, shape, dt):
        kind = "ExternalOutput" if name in DBG else "Internal"
        return nc.dram_tensor(name, shape, dt, kind=kind)

    wqb, wkb = scr("wqb", [DK, D], BF16), scr("wkb", [DK, D], BF16)
    wogb, wigb, wob = scr("wogb", [D, D], BF16), scr("wigb", [D, D], BF16), scr("wob", [D, D], BF16)
    w1b, w3b, w2b = scr("w1b", [HID, D], BF16), scr("w3b", [HID, D], BF16), scr("w2b", [D, HID], BF16)
    w1t2 = scr("w1t2", [64, 128, 16, 128], BF16)
    w3t2 = scr("w3t2", [64, 128, 16, 128], BF16)
    wogt2 = scr("wogt2", [16, 128, D], BF16)
    wigt2 = scr("wigt2", [16, 128, D], BF16)
    wot2 = scr("wot2", [16, 128, D], BF16)
    zw = scr("zw", [64, 128, D], BF16)
    w2p = scr("w2p", [4, 8, 128, 8, 512], BF16)
    lnscr = scr("lnscr", [TROWS, D], BF16)
    x1scr = scr("x1scr", [TLOC, D], F32)
    ln2scr = scr("ln2scr", [TLOC, D], BF16)

    idf_c = nc.inline_tensor(np.eye(128, dtype=np.float32), name="idf_c")
    onesf_c = nc.inline_tensor(np.ones((128, 128), np.float32), name="onesf_c")
    u65_np = np.zeros((64, 65), np.float32)
    for j in range(64):
        u65_np[j, j:64] = 1.0
        u65_np[j, 64] = 1.0
    u65_c = nc.inline_tensor(u65_np, name="u65_c")
    mask_np = np.zeros((64, 64), np.float32)
    for j in range(64):
        mask_np[j, j:] = 1.0
    mask_c = nc.inline_tensor(mask_np, name="mask_c")
    h128_c = nc.inline_tensor(_hadamard(128).astype(ml_dtypes.bfloat16), name="h128_c")
    h64 = _hadamard(64)
    h64i2_np = np.zeros((128, 128), np.float32)
    for gp in range(64):
        for s in range(2):
            col = (gp // 8) * 16 + s * 8 + (gp % 8)
            for g in range(64):
                h64i2_np[2 * g + s, col] = h64[g, gp]
    h64i2_c = nc.inline_tensor(h64i2_np.astype(ml_dtypes.bfloat16), name="h64i2_c")
    ones1b_c = nc.inline_tensor(np.ones((128, 1), ml_dtypes.bfloat16), name="ones1b_c")
    eps_c = nc.inline_tensor(np.full((128, 1), EPS, np.float32), name="eps_c")

    AKB = 176
    arena = nc.alloc_sbuf_tensor("arena", [128, AKB * 1024], mybir.dt.uint8)

    def av(off_kb, pdim, shape, dt):
        nb = int(np.prod(shape)) * mybir.dt.size(dt)
        assert off_kb * 1024 + nb <= AKB * 1024
        ap = arena[0:pdim, off_kb * 1024: off_kb * 1024 + nb].bitcast(dt)
        if len(shape) > 1:
            names = " ".join(f"a{i}" for i in range(len(shape)))
            kw = {f"a{i}": int(shape[i]) for i in range(len(shape))}
            ap = ap.rearrange(f"p ({names}) -> p {names}", **kw)
        return ap

    idf = nc.alloc_sbuf_tensor("idf", [128, 128], F32)
    onesf = nc.alloc_sbuf_tensor("onesf", [128, 128], F32)
    u65 = nc.alloc_sbuf_tensor("u65", [64, 65], F32)
    maskat = nc.alloc_sbuf_tensor("maskat", [64, 64], F32)
    h128b = nc.alloc_sbuf_tensor("h128b", [128, 128], BF16)
    h64i2b = nc.alloc_sbuf_tensor("h64i2b", [128, 128], BF16)
    ones1b = nc.alloc_sbuf_tensor("ones1b", [128, 1], BF16)
    epsv = nc.alloc_sbuf_tensor("epsv", [128, 1], F32)
    wqT = nc.alloc_sbuf_tensor("wqT", [128, 16, 128], BF16)
    wkT = nc.alloc_sbuf_tensor("wkT", [128, 16, 128], BF16)
    gnat = nc.alloc_sbuf_tensor("gnat", [16, 128], F32)
    gcol = nc.alloc_sbuf_tensor("gcol", [128, 16], F32)
    stats = nc.alloc_sbuf_tensor("stats", [128, 16], F32)
    ktm = nc.alloc_sbuf_tensor("ktm", [64, 128], F32)
    t1c = nc.alloc_sbuf_tensor("t1c", [64, 128], F32)
    t2c = nc.alloc_sbuf_tensor("t2c", [64, 128], F32)
    gftm = nc.alloc_sbuf_tensor("gftm", [64, 128], F32)
    expb = nc.alloc_sbuf_tensor("expb", [65, 128], F32)
    enb = nc.alloc_sbuf_tensor("enb", [64, 128], F32)
    qxp = nc.alloc_sbuf_tensor("qxp", [64, 128], F32)
    qetm = nc.alloc_sbuf_tensor("qetm", [64, 128], F32)
    ketm = nc.alloc_sbuf_tensor("ketm", [64, 128], F32)
    kdb = nc.alloc_sbuf_tensor("kdb", [64, 128], BF16)
    blc = nc.alloc_sbuf_tensor("blc", [128, 1], F32)
    qefmb = nc.alloc_sbuf_tensor("qefmb", [128, 64], BF16)
    kefmb = nc.alloc_sbuf_tensor("kefmb", [128, 64], BF16)
    atb = nc.alloc_sbuf_tensor("atb", [64, 64], BF16)
    ebcol = nc.alloc_sbuf_tensor("ebcol", [128, 2], F32)
    nmx = nc.alloc_sbuf_tensor("nmx", [64, 4], F32)
    rrow = nc.alloc_sbuf_tensor("rrow", [1, 1024], F32)
    rstd_bc = nc.alloc_sbuf_tensor("rstd_bc", [128, 512], F32)
    wprep = nc.alloc_sbuf_tensor("wprep", [128, 2, 2048], BF16)

    xt = [av(0, 128, (D,), F32), av(8, 128, (D,), F32)]
    lnt = [av(16, 128, (D,), BF16), av(20, 128, (D,), BF16)]
    sqscr = av(28, 128, (D,), BF16)
    qpre = av(0, 128, (TROWS,), F32)
    kpre = av(12, 128, (TROWS,), F32)
    T = av(24, 128, (D,), F32)
    S_b = av(32, 128, (D,), BF16)
    vtm = [av(36, 64, (D,), BF16), av(40, 64, (D,), BF16)]
    o_fm = av(44, 128, (16, 512), BF16)
    lnfm = [av(76, 128, (16, 512), BF16), av(92, 128, (16, 512), BF16)]
    ln1q = av(76, 128, (16, 512), BF16)
    sgq = av(92, 128, (16, 512), BF16)
    wstA = [av(108, 128, (16, 512), BF16), av(124, 128, (16, 512), BF16)]
    wstB = [av(140, 128, (16, 512), BF16), av(156, 128, (16, 512), BF16)]
    woT = av(108, 128, (16, D), BF16)
    osq = av(172, 128, (2, 512), BF16)
    o2t = av(174, 128, (512,), F32)
    xres = av(76, 128, (D,), F32)
    ln2t = av(84, 128, (D,), BF16)
    sqscr2 = av(88, 128, (D,), BF16)
    h_fm = av(0, 128, (64, 1024), BF16)
    ln2fm = av(128, 128, (16, 1024), BF16)
    w1st = [av(160, 128, (16, 128), BF16), av(164, 128, (16, 128), BF16)]
    w3st = [av(168, 128, (16, 128), BF16), av(172, 128, (16, 128), BF16)]
    w2pb = [av(160, 128, (8, 512), BF16), av(168, 128, (8, 512), BF16)]
    x1t = [av(128 + 2 * i, 128, (512,), F32) for i in range(8)]
    outt = [av(144 + 2 * i, 128, (512,), F32) for i in range(8)]
    span = [av(0, 128, (32768,), BF16), av(64, 128, (32768,), BF16)]
    w2ev = [av(132, 128, (2048,), BF16), av(136, 128, (2048,), BF16)]

    PS = nc.alloc_psum_tensor("PS", [128, 4096], F32)
    B65 = PS[0:65, 0:128]
    EBB = PS[0:64, 128:256]
    ATP = PS[0:64, 256:320]
    TK = PS[0:64, 512:640]
    TQ = PS[0:64, 640:768]
    TQE = PS[0:128, 768:832]
    TKE = PS[0:128, 832:896]
    TGN = PS[0:128, 896:912]
    GFP = PS[0:128, 960:1024]
    ps_S = PS[:, 1024:3072]
    ps_o = PS[:, 3072:4096]

    p = P(nc)
    rec = p.rec

    cast_tok = {}

    def em_consts(E):
        r = [
            E.dma_start(idf[:, :], idf_c[:, :]),
            E.dma_start(onesf[:, :], onesf_c[:, :]),
            E.dma_start(u65[:, :], u65_c[:, :]),
            E.dma_start(maskat[:, :], mask_c[:, :]),
            E.dma_start(h128b[:, :], h128_c[:, :]),
            E.dma_start(h64i2b[:, :], h64i2_c[:, :]),
            E.dma_start(ones1b[:, :], ones1b_c[:, :]),
            E.dma_start(epsv[:, :], eps_c[:, :]),
            E.dma_start(gnat[:, :], gnw.ap().rearrange("(a b) -> a b", a=16)),
        ]
        return r

    tok_c = rec("sync", em_consts, inc="cst", n=9, dma=True)
    for _e in ENGS:
        rec(_e, None, waits=[tok_c], n=0)

    wqkT_box = [None]

    tok_gn = rec("tensor",
                 lambda E: [E.matmul(TGN, gnat[:, :], idf[0:16, 0:16], start=True, stop=True,
                                     is_transpose=True, skip_group_check=True)],
                 waits=[tok_c], inc="pe", n=1)
    gcol_tok = rec("vector", lambda E: [E.tensor_copy(gcol[:, :], TGN)], waits=[tok_gn], inc="dv", n=1)


    prep_list = []
    cast_done = {}

    def _cast_pieces(name, fsrc, bdst, row_lo, row_hi, col_lo, col_hi):
        for r0 in range(row_lo, row_hi, 128):
            for c0 in range(col_lo, col_hi, 2048):
                prep_list.append(("cast", name, fsrc, bdst, r0, c0))

    _cast_pieces("wq", wq, wqb, 0, DK, 0, D)
    _cast_pieces("wk", wk, wkb, 0, DK, 0, D)
    prep_list.append(("wqkT",))
    _cast_pieces("wog", wog, wogb, 0, D, 0, D)
    for dc in range(16):
        prep_list.append(("repT", "wog", wogb, wogt2, dc))
    _cast_pieces("wig", wig, wigb, 0, D, 0, D)
    for dc in range(16):
        prep_list.append(("repT", "wig", wigb, wigt2, dc))
    _cast_pieces("wo", wo, wob, 0, D, 0, D)
    for dc in range(16):
        prep_list.append(("repT", "wo", wob, wot2, dc))
    for wn, wfsrc, wbdst, wtdst in (("w1", w1, w1b, w1t2), ("w3", w3, w3b, w3t2)):
        for hq in range(4):
            _cast_pieces(wn, wfsrc, wbdst, hq * 2048, (hq + 1) * 2048, 0, D)
            for dc in range(16):
                prep_list.append(("repack", wbdst, wtdst, dc, hq))
    for pc in range(4):
        _cast_pieces("w2", w2, w2b, 0, D, pc * 2048, (pc + 1) * 2048)
        for g in range(pc * 16, (pc + 1) * 16):
            prep_list.append(("zw", g))

    prep_idx = [0]
    prep_buf_tok = [None, None]
    last_repack_tok = [None]
    last_zw_tok = [None]
    repT_done = {}

    def do_prep_steps(k):
        for _ in range(k):
            if prep_idx[0] >= len(prep_list):
                return
            step = prep_list[prep_idx[0]]
            bi = prep_idx[0] % 2
            prep_idx[0] += 1
            if step[0] == "cast":
                _, name, fsrc, bdst, r0, c0 = step
                nr = min(128, fsrc.shape[0] - r0)
                nc = min(2048, fsrc.shape[1] - c0)
                ld = rec("gpsimd",
                         lambda E, fsrc=fsrc, r0=r0, c0=c0, nr=nr, nc=nc, bi=bi: [
                             E.dma_start(wprep[0:nr, bi, 0:nc],
                                         fsrc[r0:r0 + nr, c0:c0 + nc])],
                         waits=[prep_buf_tok[bi]], inc=f"cl{bi}", n=1, dma=True)
                wr = rec("scalar",
                         lambda E, bdst=bdst, r0=r0, c0=c0, nr=nr, nc=nc, bi=bi: [
                             E.dma_start(bdst[r0:r0 + nr, c0:c0 + nc],
                                         wprep[0:nr, bi, 0:nc])],
                         waits=[ld], inc=f"pw{bi}", n=1, dma=True)
                cast_done[name] = wr
            elif step[0] == "wqkT":
                def em_wqkT(E):
                    r = []
                    for dc in range(16):
                        r.append(E.dma_start(wqT[:, dc, :],
                                             wqb[0:DK, dc * 128:(dc + 1) * 128], transpose=True))
                        r.append(E.dma_start(wkT[:, dc, :],
                                             wkb[0:DK, dc * 128:(dc + 1) * 128], transpose=True))
                    return r
                wr = rec("sync", em_wqkT, waits=[cast_done["wq"], cast_done["wk"]],
                         inc="wqk", n=32, dma=True)
                wqkT_box[0] = wr
            elif step[0] == "repT":
                _, name, bsrc, tdst, dc = step
                ld = rec("sync",
                         lambda E, bsrc=bsrc, dc=dc, bi=bi: [
                             E.dma_start(wprep[:, bi, :],
                                         bsrc[0:D, dc * 128:(dc + 1) * 128],
                                         transpose=True)],
                         waits=[cast_done[name], prep_buf_tok[bi]],
                         inc=f"pl{bi}", n=1, dma=True)
                wr = rec("scalar",
                         lambda E, tdst=tdst, dc=dc, bi=bi: [
                             E.dma_start(tdst[dc, :, :], wprep[:, bi, :])],
                         waits=[ld], inc=f"pw{bi}", n=1, dma=True)
                repT_done[name] = wr
            elif step[0] == "repack":
                _, wbsrc, wdst, dc, hq = step
                ld = rec("sync",
                         lambda E, wbsrc=wbsrc, dc=dc, hq=hq, bi=bi: [
                             E.dma_start(wprep[:, bi, :],
                                         wbsrc[hq * 2048:(hq + 1) * 2048,
                                               dc * 128:(dc + 1) * 128],
                                         transpose=True)],
                         waits=[cast_done["w1" if wbsrc is w1b else "w3"], prep_buf_tok[bi]],
                         inc=f"pl{bi}", n=1, dma=True)
                wr = rec("scalar",
                         lambda E, wdst=wdst, dc=dc, hq=hq, bi=bi: [
                             E.dma_start(wdst[hq * 16:(hq + 1) * 16, :, dc, :]
                                         .rearrange("hb p c -> p hb c"),
                                         wprep[:, bi, :].rearrange("p (hb c) -> p hb c", hb=16))],
                         waits=[ld], inc=f"pw{bi}", n=1, dma=True)
                last_repack_tok[0] = wr
            else:
                g = step[1]
                ld = rec("sync",
                         lambda E, g=g, bi=bi: [
                             E.dma_start(wprep[:, bi, :], w2b[0:D, g * 128:(g + 1) * 128],
                                         transpose=True)],
                         waits=[cast_done["w2"], prep_buf_tok[bi]], inc=f"pl{bi}", n=1, dma=True)
                wr = rec("scalar",
                         lambda E, g=g, bi=bi: [E.dma_start(zw[g, :, :], wprep[:, bi, :])],
                         waits=[ld], inc=f"pw{bi}", n=1, dma=True)
                last_zw_tok[0] = wr
            prep_buf_tok[bi] = wr

    def _finish():
        global _LAST_P
        _LAST_P = p
        fin = [(s, p.cnt[s]) for s in list(p.cnt) if p.cnt[s] > 0]
        rec("sync", lambda E: [], waits=fin, n=0)
        with nc.Block() as block:
            p.emit(block)
        return nc

    statsd = scr("statsd", [TROWS // 128, 128, 8], F32) if "statsd" in DBG else None

    def ln_pass():
        NT = TROWS // 128
        cons = [[], []]
        st_tok = [None, None]
        st_list = []
        sq_guard = [None]
        for i in range(NT):
            b = i % 2
            r0 = i * 128
            ld = rec("sync",
                     lambda E, b=b, r0=r0: [E.dma_start(xt[b][:, :], x_in[r0:r0 + 128, :])],
                     waits=cons[b], inc=f"xt{b}", n=1, dma=True)
            s_sum = rec("vector",
                        lambda E, b=b: [E.reduce_sum(stats[:, b * 8:b * 8 + 1], xt[b][:, :], axis=AX.X)],
                        waits=[ld], inc="dv", n=1)
            a_sq = rec("scalar",
                       lambda E, b=b: [E.activation(sqscr[:, :], xt[b][:, :], AF.Square)],
                       waits=[ld, sq_guard[0]], inc="ac", n=1)
            s_sq = rec("vector",
                       lambda E, b=b: [E.reduce_sum(stats[:, b * 8 + 1:b * 8 + 2], sqscr[:, :],
                                                    axis=AX.X)],
                       waits=[a_sq], inc="dv", n=1)
            sq_guard[0] = s_sq

            t_m = rec("vector",
                      lambda E, b=b: [E.tensor_scalar(stats[:, b * 8 + 2:b * 8 + 3],
                                                      stats[:, b * 8:b * 8 + 1],
                                                      1.0 / D, None, ALU.mult)],
                      waits=[s_sum], inc="dv", n=1)
            t_m2 = rec("vector",
                       lambda E, b=b: [E.scalar_tensor_tensor(stats[:, b * 8 + 3:b * 8 + 4],
                                                              stats[:, b * 8:b * 8 + 1],
                                                              1.0 / (D * D),
                                                              stats[:, b * 8:b * 8 + 1],
                                                              ALU.mult, ALU.mult)],
                       waits=[s_sum], inc="dv", n=1)
            v_tok = rec("vector",
                        lambda E, b=b: [E.tensor_scalar(stats[:, b * 8 + 4:b * 8 + 5],
                                                        stats[:, b * 8 + 1:b * 8 + 2],
                                                        1.0 / D, stats[:, b * 8 + 3:b * 8 + 4],
                                                        ALU.mult, ALU.subtract)],
                        waits=[s_sq, t_m2], inc="dv", n=1)
            sq_tok = rec("scalar",
                         lambda E, b=b: [E.activation(stats[:, b * 8 + 5:b * 8 + 6],
                                                      stats[:, b * 8 + 4:b * 8 + 5], AF.Sqrt, bias=epsv[:, 0:1])],
                         waits=[v_tok], inc="ac", n=1)
            r_tok = rec("vector",
                        lambda E, b=b: [E.reciprocal(stats[:, b * 8 + 6:b * 8 + 7],
                                                     stats[:, b * 8 + 5:b * 8 + 6])],
                        waits=[sq_tok], inc="dv", n=1)
            rb_tok = rec("vector",
                         lambda E, b=b: [E.tensor_scalar(stats[:, b * 8 + 7:b * 8 + 8],
                                                         stats[:, b * 8 + 2:b * 8 + 3],
                                                         stats[:, b * 8 + 6:b * 8 + 7], -1.0,
                                                         ALU.mult, ALU.mult)],
                         waits=[r_tok, t_m], inc="dv", n=1)
            ap_tok = rec("vector",
                         lambda E, b=b: [E.tensor_scalar(lnt[b][:, :], xt[b][:, :],
                                                         stats[:, b * 8 + 6:b * 8 + 7],
                                                         stats[:, b * 8 + 7:b * 8 + 8],
                                                         ALU.mult, ALU.add)],
                         waits=[rb_tok, st_tok[b]], inc="dv", n=1)
            cons[b] = [ap_tok, s_sq]
            if statsd is not None:
                cons[b] = cons[b] + [rec("scalar",
                                         lambda E, b=b, i=i: [
                                             E.dma_start(statsd[i], stats[:, b * 8:b * 8 + 8])],
                                         waits=[rb_tok], inc=f"sd{b}", n=1, dma=True)]
            st_tok[b] = rec("scalar",
                            lambda E, b=b, r0=r0: [E.dma_start(lnscr[r0:r0 + 128, :], lnt[b][:, :])],
                            waits=[ap_tok], inc=f"lnt{b}", n=1, dma=True)
            st_list.append(st_tok[b])
        return st_tok, st_list

    ln_done, ln_st_list = ln_pass()
    do_prep_steps(8)
    wqkT_tok = wqkT_box[0]
    assert wqkT_tok is not None
    if PHASES <= 1:
        return _finish()
    p.barrier()

    zinit_tok = rec("vector", lambda E: [E.memset(T[:, :], 0.0), E.memset(ebcol[:, :], 0.0)],
                    waits=[ln_done[0], ln_done[1]], inc="dv", n=2)

    def proj_pass():
        NSEG = TROWS // 512
        ev_tok = [None, None]
        for s in range(NSEG):
            b = s % 2

            def em_ld(E, s=s, b=b):
                r = []
                for dc in range(16):
                    r.append(E.dma_start(lnfm[b][:, dc, :],
                                         lnscr[s * 512:(s + 1) * 512, dc * 128:(dc + 1) * 128],
                                         transpose=True))
                return r
            seg_ready = [ln_st_list[min(4 * s + 4, len(ln_st_list) - 1)],
                         ln_st_list[min(4 * s + 3, len(ln_st_list) - 1)]]
            ld = rec("sync", em_ld, waits=[wqkT_tok, ev_tok[b]] + seg_ready,
                     inc=f"lf{b}", n=16, dma=True)

            def em_mm(E, b=b):
                r = []
                for dc in range(16):
                    r.append(E.matmul(ps_S[:, 0:512], wqT[:, dc, :], lnfm[b][:, dc, :],
                                      start=(dc == 0), stop=(dc == 15)))
                for dc in range(16):
                    r.append(E.matmul(ps_S[:, 512:1024], wkT[:, dc, :], lnfm[b][:, dc, :],
                                      start=(dc == 0), stop=(dc == 15)))
                return r
            mm = rec("tensor", em_mm, waits=[ld, ev_tok[(b + 1) % 2]], inc="pe", n=32)

            def em_ev(E, s=s):
                return [
                    E.tensor_copy(qpre[:, s * 512:(s + 1) * 512], ps_S[:, 0:512]),
                    E.tensor_copy(kpre[:, s * 512:(s + 1) * 512], ps_S[:, 512:1024]),
                ]
            ev_tok[b] = rec("vector", em_ev, waits=[mm, zinit_tok], inc="dv", n=2)
        return ev_tok[(NSEG - 1) % 2]

    proj_tok = proj_pass()
    if PHASES <= 2:
        return _finish()
    p.barrier()


    tk = {
        "act_read_tr": None, "dve_read_tr": None,
        "act_read_b0": None, "dve_read_b0": None,
        "t_update": zinit_tok, "o_evac": None, "s_b": None,
        "vload": [None, None], "vcons": [None, None],
        "ebcol_guard": [None, None],
        "q_sb_free": None,
        "qg": [],
        "q_ln1q_guard": None,
    }

    def emit_chunk(c):
        is_ctx = c < NCTX
        b = c % 2
        r0 = c * CH

        tk["vload"][b] = rec("gpsimd",
                             lambda E, b=b, r0=r0: [E.dma_start(vtm[b][:, :], lnscr[r0:r0 + CH, :])],
                             waits=[tk["vcons"][b], ln_done[0], ln_done[1]],
                             inc=f"vt{b}", n=1, dma=True)

        t_k = rec("tensor",
                  lambda E, r0=r0: [E.matmul(TK, kpre[:, r0:r0 + CH], idf[:, :],
                                             start=True, stop=True, is_transpose=True,
                                             skip_group_check=True)],
                  waits=[proj_tok, tk["act_read_tr"], tk["dve_read_tr"]] + tk["qg"],
                  inc="pe", n=1)
        t_q = None
        if not is_ctx:
            t_q = rec("tensor",
                      lambda E, r0=r0: [E.matmul(TQ, qpre[:, r0:r0 + CH], idf[:, :],
                                                 start=True, stop=True, is_transpose=True,
                                                 skip_group_check=True)],
                      inc="pe", n=1)

        a_k = rec("scalar", lambda E: [E.activation(ktm[:, :], TK, AF.Sigmoid)],
                  waits=[t_k], inc="ac", n=1)
        a_e = rec("scalar", lambda E: [E.activation(t1c[:, :], ktm[:, :], AF.Exp, scale=-1.0)],
                  waits=[a_k], inc="ac", n=1)
        a_l = rec("scalar", lambda E: [E.activation(t2c[:, :], t1c[:, :], AF.Ln, bias=1.0)],
                  waits=[a_e], inc="ac", n=1)
        d_gf = rec("vector",
                   lambda E: [E.scalar_tensor_tensor(gftm[:, :], ktm[:, :], -1.0, t2c[:, :],
                                                     ALU.mult, ALU.subtract)],
                   waits=[a_l], inc="dv", n=1)

        mm_cs = rec("tensor",
                    lambda E: [E.matmul(B65, u65[:, :], gftm[:, :], start=True, stop=True,
                                        skip_group_check=True)],
                    waits=[d_gf, tk["act_read_b0"], tk["dve_read_b0"]], inc="pe", n=1)

        a_eb = rec("scalar", lambda E: [E.activation(expb[:, :], B65, AF.Exp)],
                   waits=[mm_cs], inc="ac", n=1)
        a_en = rec("scalar", lambda E: [E.activation(enb[:, :], PS[0:64, 0:128], AF.Exp, scale=-1.0)],
                   waits=[mm_cs], inc="ac", n=1)
        tk["act_read_b0"] = a_en

        t_gf = rec("tensor",
                   lambda E: [E.matmul(GFP, gftm[:, :], idf[0:64, 0:64],
                                       start=True, stop=True, is_transpose=True,
                                       skip_group_check=True)],
                   waits=[d_gf], inc="pe", n=1)
        d_bl = rec("vector", lambda E: [E.reduce_sum(blc[:, 0:1], GFP, axis=AX.X)],
                   waits=[t_gf], inc="dv", n=1)
        a_ebc = rec("scalar",
                    lambda E, b=b: [E.activation(ebcol[:, b:b + 1], blc[:, 0:1], AF.Exp)],
                    waits=[d_bl, tk["ebcol_guard"][b]], inc="ac", n=1)
        mm_ebb = rec("tensor",
                     lambda E: [E.matmul(EBB, onesf[64:65, 0:64], expb[64:65, :],
                                         start=True, stop=True, skip_group_check=True)],
                     waits=[a_eb], inc="pe", n=1)

        if not is_ctx:
            d_nm = rec("vector", lambda E: [E.reduce_max(nmx[:, 0:1], TQ, axis=AX.X, negate=True)],
                       waits=[t_q], inc="dv", n=1)
            a_qx = rec("scalar", lambda E: [E.activation(qxp[:, :], TQ, AF.Exp, bias=nmx[:, 0:1])],
                       waits=[d_nm], inc="ac", n=1)

            d_qs = rec("vector", lambda E: [E.reduce_sum(nmx[:, 1:2], qxp[:, :], axis=AX.X)],
                       waits=[a_qx], inc="dv", n=1)
            d_qr = rec("vector", lambda E: [E.reciprocal(nmx[:, 2:3], nmx[:, 1:2])],
                       waits=[d_qs], inc="dv", n=1)
            d_q = rec("vector",
                      lambda E: [E.scalar_tensor_tensor(qetm[:, :], qxp[:, :], nmx[:, 2:3],
                                                        expb[0:64, :], ALU.mult, ALU.mult)],
                      waits=[d_qr, a_eb], inc="dv", n=1)
            tk["act_read_tr"] = a_qx
        else:
            tk["act_read_tr"] = a_k

        d_ke1 = rec("vector", lambda E: [E.tensor_mul(ketm[:, :], ktm[:, :], enb[:, :])],
                    waits=[a_en], inc="dv", n=1)
        d_kd = rec("vector",
                   lambda E: [E.tensor_tensor(kdb[:, :], ketm[:, :], EBB, ALU.mult)],
                   waits=[d_ke1, mm_ebb], inc="dv", n=1)

        mm_at = None
        if not is_ctx:
            t_qe = rec("tensor",
                       lambda E: [E.matmul(TQE, qetm[:, :], idf[0:64, 0:64],
                                           start=True, stop=True, is_transpose=True,
                                           skip_group_check=True)],
                       waits=[d_q], inc="pe", n=1)
            t_ke = rec("tensor",
                       lambda E: [E.matmul(TKE, ketm[:, :], idf[0:64, 0:64],
                                           start=True, stop=True, is_transpose=True,
                                           skip_group_check=True)],
                       waits=[d_ke1], inc="pe", n=1)

            def em_fm(E):
                return [
                    E.tensor_copy(qefmb[:, :], TQE),
                    E.tensor_copy(kefmb[:, :], TKE),
                ]
            d_fm = rec("vector", em_fm, waits=[t_qe, t_ke], inc="dv", n=2)
            tk["dve_read_tr"] = d_fm

            mm_at = rec("tensor",
                        lambda E: [E.matmul(ATP, kefmb[:, 0:64], qefmb[:, 0:64],
                                            start=True, stop=True, skip_group_check=True)],
                        waits=[d_fm], inc="pe", n=1)
        else:
            tk["dve_read_tr"] = d_kd

        if not is_ctx:
            d_at = rec("vector",
                       lambda E: [E.tensor_tensor(atb[:, :], ATP, maskat[:, :], ALU.mult)],
                       waits=[mm_at], inc="dv", n=1)
            tk["dve_read_b0"] = d_at
        else:
            tk["dve_read_b0"] = d_kd
        a_sb = None
        if not is_ctx:
            a_sb = rec("scalar",
                       lambda E: [E.activation(S_b[:, :], T[:, :], AF.Copy)],
                       waits=[tk["t_update"]], inc="ac", n=1)

        mm_o = None
        if not is_ctx:
            def em_o(E, b=b):
                r = []
                for dvc in range(16):
                    col = 3072 + dvc * 64
                    r.append(E.matmul(PS[:, col:col + 64], vtm[b][:, dvc * 128:(dvc + 1) * 128],
                                      atb[:, :], start=True, stop=False,
                                      skip_group_check=True))
                    r.append(E.matmul(PS[:, col:col + 64], S_b[:, dvc * 128:(dvc + 1) * 128],
                                      qefmb[:, :], start=False, stop=True,
                                      skip_group_check=True))
                return r
            mm_o = rec("tensor", em_o,
                       waits=[tk["dve_read_b0"], a_sb, tk["vload"][b], tk["o_evac"],
                              tk["q_sb_free"]] + tk["qg"],
                       inc="pe", n=32)

        def em_s(E, b=b):
            r = []
            for q4 in range(4):
                r.append(E.matmul(ps_S[:, q4 * 512:(q4 + 1) * 512], kdb[:, :],
                                  vtm[b][:, q4 * 512:(q4 + 1) * 512], start=True, stop=True))
            return r
        mm_s = rec("tensor", em_s,
                   waits=[d_kd, tk["vload"][b], tk["t_update"]] + tk["qg"], inc="pe", n=4)
        tk["vcons"][b] = mm_s
        tk["qg"] = []

        d_T = rec("vector",
                  lambda E, b=b: [E.scalar_tensor_tensor(T[:, :], T[:, :], ebcol[:, b:b + 1],
                                                         ps_S[:, :], ALU.mult, ALU.add)],
                  waits=([mm_s, a_ebc, a_sb] if not is_ctx else [mm_s, a_ebc]),
                  inc="dv", n=1)
        tk["t_update"] = d_T
        tk["ebcol_guard"][b] = d_T

        if not is_ctx:
            qi = (c - NCTX) % 8
            d_oe = rec("vector",
                       lambda E, qi=qi: [E.tensor_copy(o_fm[:, :, qi * 64:(qi + 1) * 64],
                                                       ps_o.rearrange("p (a b) -> p a b", a=16))],
                       waits=[mm_o], inc="dv", n=1)
            tk["o_evac"] = d_oe
        if glad is not None and c == NCTX:
            def em_dbg(E):
                return [
                    E.dma_start(glad["ktm"][:, :], ktm[:, :]),
                    E.dma_start(glad["expb"][:, :], expb[:, :]),
                    E.dma_start(glad["qetm"][:, :], qetm[:, :]),
                    E.dma_start(glad["ketm"][:, :], ketm[:, :]),
                    E.dma_start(glad["atb"][:, :], atb[:, :]),
                    E.dma_start(glad["kdb"][:, :], kdb[:, :]),
                    E.dma_start(glad["ofm0"][:, :, :], o_fm[:, :, 0:64]),
                    E.dma_start(glad["qefmb"][:, :], qefmb[:, :]),
                    E.dma_start(glad["S_b"][:, :], S_b[:, :]),
                    E.dma_start(glad["ebcol"][:, :], ebcol[:, :]),
                    E.dma_start(glad["T"][:, :], T[:, :]),
                ]
            rec("scalar", em_dbg, waits=[tk["t_update"], tk["o_evac"]], inc="dbg9", n=11, dma=True)

    wo_psum_free = [None, None]

    glad = None
    if "glad" in DBG:
        def dout(name, shape, dt):
            return nc.dram_tensor(name, shape, dt, kind="ExternalOutput")
        glad = {
            "ktm": dout("g_ktm", [64, 128], F32), "expb": dout("g_expb", [65, 128], F32),
            "qetm": dout("g_qetm", [64, 128], F32), "ketm": dout("g_ketm", [64, 128], F32),
            "atb": dout("g_atb", [64, 64], BF16), "S_b": dout("g_Sb", [128, D], BF16),
            "ebcol": dout("g_ebcol", [128, 2], F32), "T": dout("g_T", [128, D], F32),
            "kdb": dout("g_kdb", [64, 128], BF16),
            "ofm0": dout("g_ofm0", [128, 16, 64], F32),
            "qefmb": dout("g_qefmb", [128, 64], BF16),
        }
    ofmd = scr("ofmd", [4, 128, 16, 512], BF16) if "ofmd" in DBG else None
    qkd = scr("qkd", [2, 128, TROWS], F32) if "qkd" in DBG else None
    sgd = scr("sgd", [4, 128, 16, 512], BF16) if "sgd" in DBG else None
    rbd = scr("rbd", [4, 128, 512], F32) if "rbd" in DBG else None
    gcold = nc.dram_tensor("gcold", [128, 16], F32, kind="ExternalOutput") if "gcold" in DBG else None

    def emit_quarter(q):
        mq0 = q * 512
        xr0 = CTX + mq0
        if ofmd is not None:
            rec("scalar", lambda E, q=q: [E.dma_start(ofmd[q], o_fm[:, :, :])],
                waits=[tk["o_evac"]], inc="dbg1", n=1, dma=True)
        if qkd is not None and q == 0:
            rec("scalar", lambda E: [E.dma_start(qkd[0], qpre[:, :]),
                                     E.dma_start(qkd[1], kpre[:, :])],
                waits=[proj_tok], inc="dbg2", n=2, dma=True)

        def em_lq(E):
            r = []
            for dc in range(16):
                r.append(E.dma_start(ln1q[:, dc, :],
                                     lnscr[xr0:xr0 + 512, dc * 128:(dc + 1) * 128], transpose=True))
            return r
        lq = rec("sync", em_lq, waits=[tk["q_ln1q_guard"], ln_done[0], ln_done[1]],
                 inc="l1q", n=16, dma=True)

        def emit_rms():
            prev_mm = None
            mm_ms = None
            for oc in range(16):
                a_sq = rec("scalar",
                           lambda E, oc=oc: [E.activation(osq[:, oc % 2, :], o_fm[:, oc, :],
                                                          AF.Square)],
                           waits=[tk["o_evac"], prev_mm], inc="ac", n=1)
                mm_ms = rec("tensor",
                            lambda E, oc=oc: [E.matmul(PS[0:1, 2048:2560], ones1b[:, :],
                                                       osq[:, oc % 2, :],
                                                       start=(oc == 0), stop=(oc == 15),
                                                       skip_group_check=True)],
                            waits=[a_sq, tk["t_update"]], inc="pe", n=1)
                prev_mm = mm_ms
            a_rs = rec("scalar",
                       lambda E: [E.activation(rrow[0:1, 0:512], PS[0:1, 2048:2560], AF.Sqrt,
                                               scale=1.0 / D, bias=epsv[0:1, 0:1])],
                       waits=[mm_ms], inc="ac", n=1)
            d_rr = rec("vector", lambda E: [E.reciprocal(rrow[0:1, 512:1024], rrow[0:1, 0:512])],
                       waits=[a_rs], inc="dv", n=1)
            mm_bc = rec("tensor",
                        lambda E: [E.matmul(PS[:, 2560:3072], onesf[0:1, :], rrow[0:1, 512:1024],
                                            start=True, stop=True, skip_group_check=True)],
                        waits=[d_rr], inc="pe", n=1)
            d_bc = rec("vector", lambda E: [E.tensor_copy(rstd_bc[:, :], PS[:, 2560:3072])],
                       waits=[mm_bc], inc="dv", n=1)
            return mm_bc, d_bc

        wst_tok = [None, None]
        sg_toks = []
        mm_bc = None
        d_bc = None
        for ocg in range(4):
            bi = ocg % 2

            def em_wst(E, ocg=ocg, bi=bi):
                r = []
                for d0 in range(0, 16, 4):
                    r.append(E.dma_start(
                        wstA[bi][:, d0:d0 + 4, :],
                        wogt2[d0:d0 + 4, :, ocg * 512:(ocg + 1) * 512]
                        .rearrange("dc p w -> p dc w")))
                    r.append(E.dma_start(
                        wstB[bi][:, d0:d0 + 4, :],
                        wigt2[d0:d0 + 4, :, ocg * 512:(ocg + 1) * 512]
                        .rearrange("dc p w -> p dc w")))
                return r
            wl = rec("sync", em_wst,
                     waits=[repT_done["wog"], repT_done["wig"], wst_tok[bi]],
                     inc=f"ws{bi}", n=8, dma=True)

            last_mm = None
            for oci in range(4):
                oc = ocg * 4 + oci

                def em_go(E, oci=oci, bi=bi):
                    r = []
                    for dc in range(16):
                        r.append(E.matmul(PS[:, oci * 512:(oci + 1) * 512],
                                          wstA[bi][:, dc, oci * 128:(oci + 1) * 128],
                                          o_fm[:, dc, :], start=(dc == 0), stop=False))
                    for dc in range(16):
                        r.append(E.matmul(PS[:, oci * 512:(oci + 1) * 512],
                                          wstB[bi][:, dc, oci * 128:(oci + 1) * 128],
                                          ln1q[:, dc, :], start=False, stop=(dc == 15)))
                    return r
                waits = [wl, lq, tk["o_evac"]]
                if ocg == 0 and oci == 0:
                    waits += [tk["t_update"], tk["act_read_tr"], tk["dve_read_tr"],
                              tk["act_read_b0"], tk["dve_read_b0"]]
                if len(sg_toks) >= 4:
                    waits += [sg_toks[len(sg_toks) - 4]]
                mm_go = rec("tensor", em_go, waits=waits, inc="pe", n=32)
                last_mm = mm_go
                a_sg = rec("scalar",
                           lambda E, oc=oc, oci=oci: [E.activation(sgq[:, oc, :],
                                                                   PS[:, oci * 512:(oci + 1) * 512],
                                                                   AF.Silu)],
                           waits=[mm_go], inc="ac", n=1)
                sg_toks.append(a_sg)
            wst_tok[bi] = last_mm
            if ocg == 0:
                mm_bc, d_bc = emit_rms()

        dbg_sg_tok = None
        if sgd is not None:
            dbg_sg_tok = rec("scalar", lambda E, q=q: [E.dma_start(sgd[q], sgq[:, :, :])],
                             waits=[sg_toks[-1]], inc="dbg3", n=1, dma=True)
        if "gcold" in DBG and q == 0:
            rec("scalar", lambda E: [E.dma_start(gcold[:, :], gcol[:, :])],
                waits=[gcol_tok], inc="dbg5", n=1, dma=True)
        o2_last = dbg_sg_tok
        for oc in range(16):
            o2_a = rec("vector",
                       lambda E, oc=oc: [E.scalar_tensor_tensor(o2t[:, :], o_fm[:, oc, :],
                                                                gcol[:, oc:oc + 1],
                                                                rstd_bc[:, :], ALU.mult, ALU.mult)],
                       waits=[sg_toks[oc], d_bc, gcol_tok, o2_last], inc="dv", n=1)
            o2_last = rec("vector",
                          lambda E, oc=oc: [E.tensor_tensor(sgq[:, oc, :], o2t[:, :],
                                                            sgq[:, oc, :], ALU.mult)],
                          waits=[o2_a], inc="dv", n=1)
        tk["q_sb_free"] = o2_last

        def em_wot(E):
            r = []
            for d0 in range(0, 16, 2):
                r.append(E.dma_start(
                    woT[:, d0:d0 + 2, :],
                    wot2[d0:d0 + 2, :, :].rearrange("dv p w -> p dv w")))
            return r
        wot_l = rec("sync", em_wot, waits=[repT_done["wo"], wst_tok[0], wst_tok[1]],
                    inc="wot", n=8, dma=True)

        last_st = None
        sq2_guard = [None]
        for t in range(4):
            pbase = 2048 if t % 2 == 0 else 0
            xl = rec("scalar",
                     lambda E, t=t: [E.dma_start(xres[:, :],
                                                 x_in[xr0 + t * 128:xr0 + (t + 1) * 128, :])],
                     waits=[wst_tok[0], wst_tok[1], last_st], inc="xr", n=1, dma=True)

            def em_wmm(E, t=t, pbase=pbase):
                r = []
                for dvc in range(16):
                    for os_ in range(4):
                        r.append(E.matmul(PS[:, pbase + os_ * 512:pbase + (os_ + 1) * 512],
                                          sgq[:, dvc, t * 128:(t + 1) * 128],
                                          woT[:, dvc, os_ * 512:(os_ + 1) * 512],
                                          start=(dvc == 0), stop=(dvc == 15),
                                          skip_group_check=True))
                return r
            mm_wo = rec("tensor", em_wmm,
                        waits=[wot_l, o2_last, wo_psum_free[t % 2]], inc="pe", n=64)

            d_x1 = rec("vector",
                       lambda E, pbase=pbase: [E.tensor_add(xres[:, :], xres[:, :],
                                                            PS[:, pbase:pbase + 2048])],
                       waits=[mm_wo, xl], inc="dv", n=1)
            wo_psum_free[t % 2] = d_x1
            s_sum = rec("vector", lambda E: [E.reduce_sum(stats[:, 0:1], xres[:, :], axis=AX.X)],
                        waits=[d_x1], inc="dv", n=1)
            a_sq2 = rec("scalar", lambda E: [E.activation(sqscr2[:, :], xres[:, :], AF.Square)],
                        waits=[d_x1, sq2_guard[0]], inc="ac", n=1)
            s_sq = rec("vector",
                       lambda E: [E.reduce_sum(stats[:, 1:2], sqscr2[:, :], axis=AX.X)],
                       waits=[a_sq2], inc="dv", n=1)
            sq2_guard[0] = s_sq

            t_m = rec("vector",
                      lambda E: [E.tensor_scalar(stats[:, 2:3], stats[:, 0:1], 1.0 / D,
                                                 None, ALU.mult)],
                      waits=[s_sum], inc="dv", n=1)
            t_m2 = rec("vector",
                       lambda E: [E.scalar_tensor_tensor(stats[:, 3:4], stats[:, 0:1],
                                                         1.0 / (D * D), stats[:, 0:1],
                                                         ALU.mult, ALU.mult)],
                       waits=[s_sum], inc="dv", n=1)
            v_tok = rec("vector",
                        lambda E: [E.tensor_scalar(stats[:, 4:5], stats[:, 1:2], 1.0 / D,
                                                   stats[:, 3:4], ALU.mult, ALU.subtract)],
                        waits=[s_sq, t_m2], inc="dv", n=1)
            sq_tok = rec("scalar", lambda E: [E.activation(stats[:, 5:6], stats[:, 4:5],
                                                           AF.Sqrt, bias=epsv[:, 0:1])],
                         waits=[v_tok], inc="ac", n=1)
            r_tok = rec("vector",
                        lambda E: [E.reciprocal(stats[:, 6:7], stats[:, 5:6])],
                        waits=[sq_tok], inc="dv", n=1)
            rb_tok = rec("vector",
                         lambda E: [E.tensor_scalar(stats[:, 7:8], stats[:, 2:3], stats[:, 6:7],
                                                    -1.0, ALU.mult, ALU.mult)],
                         waits=[r_tok, t_m], inc="dv", n=1)
            a_ln2 = rec("vector",
                        lambda E: [E.tensor_scalar(ln2t[:, :], xres[:, :], stats[:, 6:7],
                                                   stats[:, 7:8], ALU.mult, ALU.add)],
                        waits=[rb_tok, s_sq], inc="dv", n=1)

            def em_st(E, t=t):
                return [
                    E.dma_start(x1scr[mq0 + t * 128:mq0 + (t + 1) * 128, :], xres[:, :]),
                    E.dma_start(ln2scr[mq0 + t * 128:mq0 + (t + 1) * 128, :], ln2t[:, :]),
                ]
            last_st = rec("scalar", em_st, waits=[a_ln2], inc="qst", n=2, dma=True)

        if rbd is not None:
            rec("scalar", lambda E, q=q: [E.dma_start(rbd[q], rstd_bc[:, :])],
                waits=[d_bc], inc="dbg4", n=1, dma=True)
        tk["qg"] = [wo_psum_free[0], wo_psum_free[1]]
        tk["q_ln1q_guard"] = last_st
        return last_st

    q_done = []
    if SKIP_ATTN:
        q_done.append(ln_done[1])
    else:
        for c in range(NCH):
            emit_chunk(c)
            do_prep_steps(9)
            if PHASES >= 4 and c >= NCTX and (c - NCTX) % 8 == 7:
                q_done.append(emit_quarter((c - NCTX) // 8))
                do_prep_steps(12)
    do_prep_steps(10000)
    if PHASES <= 3:
        return _finish()
    p.barrier()

    def w2_transform():
        zw_flat = zw.ap().rearrange("g (s lr) d -> (g s) (lr d)", s=2)
        w2p_v = w2p.ap().rearrange("q gb (s lr) gi c -> lr gb s gi q c", s=2)
        sp_tok = [None, None]
        ev_toks = []
        wr_toks = []
        for qq in range(4):
            bi = qq % 2
            ld = rec("sync",
                     lambda E, qq=qq, bi=bi: [
                         E.dma_start(span[bi][:, :], zw_flat[:, qq * 32768:(qq + 1) * 32768])],
                     waits=[("pw0", p.cnt.get("pw0", 0)), ("pw1", p.cnt.get("pw1", 0)),
                            sp_tok[bi], q_done[-1]],
                     inc=f"sp{bi}", n=1, dma=True)
            for wv in range(16):
                eb = wv % 2

                def em_mm(E, bi=bi, wv=wv):
                    r = []
                    for j in range(4):
                        cs = wv * 4 + j
                        r.append(E.matmul(PS[:, j * 512:(j + 1) * 512], h64i2b[:, :],
                                          span[bi][:, cs * 512:(cs + 1) * 512],
                                          start=True, stop=True))
                    return r
                mm = rec("tensor", em_mm,
                         waits=[ld, ev_toks[-1] if ev_toks else None], inc="pe", n=4)
                a_ev = rec("vector",
                           lambda E, eb=eb: [E.tensor_copy(w2ev[eb][:, :], PS[:, 0:2048])],
                           waits=[mm, wr_toks[-2] if len(wr_toks) >= 2 else None],
                           inc="dv", n=1)
                ev_toks.append(a_ev)
                lq_ = (qq * 32768 + wv * 2048) // D

                def em_wr(E, lq_=lq_, eb=eb):
                    r = []
                    for dqi in range(4):
                        dst = w2p_v[lq_:lq_ + 1, :, :, :, dqi:dqi + 1].squeeze(0).squeeze(3)
                        r.append(E.dma_start(dst, w2ev[eb][:, dqi * 512:(dqi + 1) * 512]))
                    return r
                wr = rec("scalar", em_wr, waits=[a_ev], inc=f"we{eb}", n=4, dma=True)
                wr_toks.append(wr)
            sp_tok[bi] = wr_toks[-1]
        return [("we0", p.cnt.get("we0", 0)), ("we1", p.cnt.get("we1", 0))]

    if PHASES <= 4:
        return _finish()
    w2p_tok = w2_transform()
    if PHASES <= 5:
        return _finish()
    p.barrier()

    prev_tt_guard = []

    def ffn_tt(tt):
        tok0 = tt * 1024

        def em_ln2(E, tok0=tok0):
            r = []
            for dc in range(16):
                r.append(E.dma_start(ln2fm[:, dc, :],
                                     ln2scr[tok0:tok0 + 1024, dc * 128:(dc + 1) * 128],
                                     transpose=True))
            return r
        lq = rec("sync", em_ln2, waits=[q_done[-1]] + list(prev_tt_guard) + list(w2p_tok),
                 inc="lfm", n=16, dma=True)

        st_tok = [None, None]
        h_toks = []
        for g in range(64):
            bi = g % 2

            def em_st(E, g=g, bi=bi):
                return [
                    E.dma_start(w1st[bi][:, :, :].rearrange("p a b -> p (a b)"),
                                w1t2[g].rearrange("p a b -> p (a b)")),
                    E.dma_start(w3st[bi][:, :, :].rearrange("p a b -> p (a b)"),
                                w3t2[g].rearrange("p a b -> p (a b)")),
                ]
            sl = rec("sync", em_st,
                     waits=[("pw0", p.cnt.get("pw0", 0)), ("pw1", p.cnt.get("pw1", 0)),
                            st_tok[bi]],
                     inc=f"fs{bi}", n=2, dma=True)

            pA = 0 if g % 2 == 0 else 2048
            pB = 1024 if g % 2 == 0 else 3072

            def em_up(E, bi=bi, pA=pA, pB=pB):
                r = []
                for th in range(2):
                    for dc in range(16):
                        r.append(E.matmul(PS[:, pA + th * 512:pA + (th + 1) * 512],
                                          w1st[bi][:, dc, :],
                                          ln2fm[:, dc, th * 512:(th + 1) * 512],
                                          start=(dc == 0), stop=(dc == 15)))
                for th in range(2):
                    for dc in range(16):
                        r.append(E.matmul(PS[:, pB + th * 512:pB + (th + 1) * 512],
                                          w3st[bi][:, dc, :],
                                          ln2fm[:, dc, th * 512:(th + 1) * 512],
                                          start=(dc == 0), stop=(dc == 15)))
                return r
            waits = [sl, lq]
            if len(h_toks) >= 2:
                waits.append(h_toks[-2][1])
            if g < 2:
                waits.extend(prev_tt_guard)
            mm_up = rec("tensor", em_up, waits=waits, inc="pe", n=64)
            st_tok[bi] = mm_up

            a_si = rec("scalar",
                       lambda E, g=g, pA=pA: [E.activation(h_fm[:, g, :], PS[:, pA:pA + 1024],
                                                           AF.Silu)],
                       waits=[mm_up], inc="ac", n=1)
            d_h = rec("vector",
                      lambda E, g=g, pB=pB: [E.tensor_tensor(h_fm[:, g, :], h_fm[:, g, :],
                                                             PS[:, pB:pB + 1024], ALU.mult)],
                      waits=[a_si], inc="dv", n=1)

            def em_h128(E, g=g, pA=pA):
                return [
                    E.matmul(PS[:, pA:pA + 512], h128b[:, :], h_fm[:, g, 0:512],
                             start=True, stop=True),
                    E.matmul(PS[:, pA + 512:pA + 1024], h128b[:, :], h_fm[:, g, 512:1024],
                             start=True, stop=True),
                ]
            mm_h = rec("tensor", em_h128, waits=[d_h], inc="pe", n=2)
            d_y = rec("vector",
                      lambda E, g=g, pA=pA: [E.tensor_copy(h_fm[:, g, :], PS[:, pA:pA + 1024])],
                      waits=[mm_h], inc="dv", n=1)
            h_toks.append((d_h, d_y))

        w2_buf_tok = [None, None]
        last_ev = {t: None for t in range(8)}
        last_out_dv = []
        for dq in range(4):
            mm_w2 = None
            for gblk in range(8):
                bi = gblk % 2

                def em_wl(E, dq=dq, gblk=gblk, bi=bi):
                    return [E.dma_start(w2pb[bi][:, :, :].rearrange("p a b -> p (a b)"),
                                        w2p[dq, gblk].rearrange("p a b -> p (a b)"))]
                wl_waits = list(w2p_tok) + [w2_buf_tok[bi]]
                if dq == 0 and gblk < 2:
                    wl_waits += [h_toks[-1][1]]
                wl = rec("sync", em_wl, waits=wl_waits, inc=f"wp{bi}", n=1, dma=True)

                def em_w2(E, gblk=gblk, bi=bi):
                    r = []
                    for gin in range(8):
                        g = gblk * 8 + gin
                        for tg in range(8):
                            r.append(E.matmul(PS[:, tg * 512:(tg + 1) * 512],
                                              h_fm[:, g, tg * 128:(tg + 1) * 128],
                                              w2pb[bi][:, gin, :],
                                              start=(gblk == 0 and gin == 0),
                                              stop=(gblk == 7 and gin == 7),
                                              skip_group_check=True))
                    return r
                waits = [wl, h_toks[-1][1], h_toks[-2][1]]
                if gblk == 0 and last_out_dv:
                    waits += [last_out_dv[-1]]
                mm_w2 = rec("tensor", em_w2, waits=waits, inc="pe", n=64)
                w2_buf_tok[bi] = mm_w2

            for tg in range(8):
                xl = rec("scalar",
                         lambda E, tok0=tok0, dq=dq, tg=tg: [
                             E.dma_start(x1t[tg][:, :],
                                         x1scr[tok0 + tg * 128:tok0 + (tg + 1) * 128,
                                               dq * 512:(dq + 1) * 512])],
                         waits=list(w2p_tok) + [last_ev[tg]], inc=f"x1{tg}", n=1, dma=True)
                d_o = rec("vector",
                          lambda E, tg=tg: [
                              E.scalar_tensor_tensor(outt[tg][:, :],
                                                     PS[:, tg * 512:(tg + 1) * 512], RSQN,
                                                     x1t[tg][:, :], ALU.mult, ALU.add)],
                          waits=[mm_w2, xl, last_ev[tg]], inc="dv", n=1)
                last_out_dv.append(d_o)
                st = rec("scalar",
                         lambda E, tok0=tok0, dq=dq, tg=tg: [
                             E.dma_start(out[tok0 + tg * 128:tok0 + (tg + 1) * 128,
                                             dq * 512:(dq + 1) * 512], outt[tg][:, :])],
                         waits=[d_o], inc=f"os{tg}", n=1, dma=True)
                last_ev[tg] = st
        prev_tt_guard[:] = [(f"os{t}", p.cnt.get(f"os{t}", 0)) for t in range(8)]

    ffn_tt(0)
    ffn_tt(1)

    rec("sync", lambda E: [],
        waits=[(f"os{t}", p.cnt.get(f"os{t}", 0)) for t in range(8)], n=0)

    with nc.Block() as block:
        p.emit(block)
    return nc


_NC_CACHE = None
_LAST_P = None


def _get_nc():
    global _NC_CACHE
    if _NC_CACHE is None:
        _NC_CACHE = build_nc()
    return _NC_CACHE


def kernel(**inputs):
    x = np.ascontiguousarray(np.asarray(inputs["L0_attn_hidden_states"], dtype=np.float32))
    ws = {
        "wq": np.ascontiguousarray(np.asarray(inputs["L0_attn_Wq"], np.float32)),
        "wk": np.ascontiguousarray(np.asarray(inputs["L0_attn_Wk"], np.float32)),
        "wog": np.ascontiguousarray(np.asarray(inputs["L0_attn_Wog"], np.float32)),
        "wig": np.ascontiguousarray(np.asarray(inputs["L0_attn_Wig"], np.float32)),
        "wo": np.ascontiguousarray(np.asarray(inputs["L0_attn_Wo"], np.float32)),
        "gnw": np.ascontiguousarray(np.asarray(inputs["L0_attn_g_norm_w"], np.float32)),
        "w1": np.ascontiguousarray(np.asarray(inputs["L0_ffn_W1"], np.float32)),
        "w2": np.ascontiguousarray(np.asarray(inputs["L0_ffn_W2"], np.float32)),
        "w3": np.ascontiguousarray(np.asarray(inputs["L0_ffn_W3"], np.float32)),
    }
    in_maps = []
    for cid in range(NCORES):
        b, s = cid // 4, cid % 4
        xin = np.zeros((TROWS, D), np.float32)
        xin[CTX:] = x[b, s * TLOC:(s + 1) * TLOC]
        if s > 0:
            xin[:CTX] = x[b, s * TLOC - CTX:s * TLOC]
        m = {"x": xin}
        m.update(ws)
        in_maps.append(m)
    nc = _get_nc()
    res = run_bass_kernel_spmd(nc, in_maps, core_ids=list(range(NCORES)))
    outa = np.empty((B, L, D), np.float32)
    for cid in range(NCORES):
        b, s = cid // 4, cid % 4
        outa[b, s * TLOC:(s + 1) * TLOC] = res.results[cid]["out"]
    return outa


# version 07
# speedup vs baseline: 1.0538x; 1.0251x over previous
import math
import numpy as np
import ml_dtypes

import concourse.bass as bass
import concourse.mybir as mybir
from concourse.bass_utils import run_bass_kernel_spmd

F32 = mybir.dt.float32
BF16 = mybir.dt.bfloat16
AF = mybir.ActivationFunctionType
ALU = mybir.AluOpType
AX = mybir.AxisListType

B, L, D = 2, 8192, 2048
NCORES = 8
TLOC = 2048
CTX = 1024
TROWS = TLOC + CTX
CH = 64
NCH = TROWS // CH
NCTX = CTX // CH
DK = 128
HID = 8192
EPS = 1e-5
RSQN = 1.0 / math.sqrt(float(HID))

DBG = set()
PHASES = 6
GATE_CASTS = False
SKIP_ATTN = False
T_UPTO = 3
T_CS = 64
T_QQ = 4
DEFER_CASTS = False

ENGS = ("sync", "gpsimd", "scalar", "vector", "tensor")


def _hadamard(n):
    h = np.ones((n, n), np.float32)
    for i in range(n):
        for j in range(n):
            if bin(i & j).count("1") & 1:
                h[i, j] = -1.0
    return h


class P:

    def __init__(self, nc):
        self.nc = nc
        self.ops = {e: [] for e in ENGS}
        self.sems = {}
        self.cnt = {}
        for s in ("sy", "sc", "gp", "pe", "ac", "dv", "ph"):
            self.sems[s] = nc.alloc_semaphore(f"s_{s}")
            self.cnt[s] = 0

    def rec(self, eng, emit, waits=(), inc=None, n=1, dma=False):
        if inc is not None and inc not in self.sems:
            self.sems[inc] = self.nc.alloc_semaphore(f"s_{inc}")
            self.cnt[inc] = 0
        amt = 16 if dma else 1
        ws = []
        for w in waits:
            if w is None:
                continue
            s, v = w
            assert v <= self.cnt[s], f"wait on future token: {s} {v} > {self.cnt[s]}"
            ws.append((s, v))
        tok = None
        if inc is not None and n > 0:
            self.cnt[inc] += amt * n
            tok = (inc, self.cnt[inc])
        self.ops[eng].append((tuple(ws), emit, inc, amt, n))
        return tok

    def barrier(self):
        self.cnt["ph"] += 5
        target = self.cnt["ph"]
        ph = self.sems["ph"]
        for e in ENGS:
            self.ops[e].append(((), (lambda E, ph=ph: [E.sem_inc(ph, 1)]), None, 1, 1))
            self.ops[e].append(((("ph", target),), None, None, 1, 0))
        return ("ph", target)

    def emit(self, block):
        for e in ENGS:
            oplist = self.ops[e]
            sems = self.sems

            def runner(E, oplist=oplist):
                hw = {}
                for ws, em, inc, amt, n in oplist:
                    for s, v in ws:
                        if hw.get(s, 0) < v:
                            E.wait_ge(sems[s], v)
                            hw[s] = v
                    if em is None:
                        continue
                    instrs = em(E)
                    if inc is not None:
                        assert len(instrs) == n, f"{e}: {len(instrs)} != {n}"
                        for i in instrs:
                            i.then_inc(sems[inc], amt)

            getattr(block, e)(runner)


def build_nc():
    nc = bass.Bass("TRN2", target_bir_lowering=False)

    def din(name, shape):
        return nc.dram_tensor(name, shape, F32, kind="ExternalInput")

    x_in = din("x", [TROWS, D])
    wq, wk = din("wq", [DK, D]), din("wk", [DK, D])
    wog, wig, wo = din("wog", [D, D]), din("wig", [D, D]), din("wo", [D, D])
    gnw = din("gnw", [D])
    w1, w2, w3 = din("w1", [HID, D]), din("w2", [D, HID]), din("w3", [HID, D])
    out = nc.dram_tensor("out", [TLOC, D], F32, kind="ExternalOutput")

    def scr(name, shape, dt):
        kind = "ExternalOutput" if name in DBG else "Internal"
        return nc.dram_tensor(name, shape, dt, kind=kind)

    wqb, wkb = scr("wqb", [DK, D], BF16), scr("wkb", [DK, D], BF16)
    wogb, wigb, wob = scr("wogb", [D, D], BF16), scr("wigb", [D, D], BF16), scr("wob", [D, D], BF16)
    w1b, w3b, w2b = scr("w1b", [HID, D], BF16), scr("w3b", [HID, D], BF16), scr("w2b", [D, HID], BF16)
    w1t2 = scr("w1t2", [64, 128, 16, 128], BF16)
    w3t2 = scr("w3t2", [64, 128, 16, 128], BF16)
    wogt2 = scr("wogt2", [16, 128, D], BF16)
    wigt2 = scr("wigt2", [16, 128, D], BF16)
    wot2 = scr("wot2", [16, 128, D], BF16)
    zw = scr("zw", [64, 128, D], BF16)
    w2p = scr("w2p", [4, 8, 128, 8, 512], BF16)
    lnscr = scr("lnscr", [TROWS, D], BF16)
    x1scr = scr("x1scr", [TLOC, D], F32)
    ln2scr = scr("ln2scr", [TLOC, D], BF16)

    idf_c = nc.inline_tensor(np.eye(128, dtype=np.float32), name="idf_c")
    onesf_c = nc.inline_tensor(np.ones((128, 128), np.float32), name="onesf_c")
    u65_np = np.zeros((64, 65), np.float32)
    for j in range(64):
        u65_np[j, j:64] = 1.0
        u65_np[j, 64] = 1.0
    u65_c = nc.inline_tensor(u65_np, name="u65_c")
    mask_np = np.zeros((64, 64), np.float32)
    for j in range(64):
        mask_np[j, j:] = 1.0
    mask_c = nc.inline_tensor(mask_np, name="mask_c")
    h128_c = nc.inline_tensor(_hadamard(128).astype(ml_dtypes.bfloat16), name="h128_c")
    h64 = _hadamard(64)
    h64i2_np = np.zeros((128, 128), np.float32)
    for gp in range(64):
        for s in range(2):
            col = (gp // 8) * 16 + s * 8 + (gp % 8)
            for g in range(64):
                h64i2_np[2 * g + s, col] = h64[g, gp]
    h64i2_c = nc.inline_tensor(h64i2_np.astype(ml_dtypes.bfloat16), name="h64i2_c")
    ones1b_c = nc.inline_tensor(np.ones((128, 1), ml_dtypes.bfloat16), name="ones1b_c")
    eps_c = nc.inline_tensor(np.full((128, 1), EPS, np.float32), name="eps_c")

    AKB = 176
    arena = nc.alloc_sbuf_tensor("arena", [128, AKB * 1024], mybir.dt.uint8)

    def av(off_kb, pdim, shape, dt):
        nb = int(np.prod(shape)) * mybir.dt.size(dt)
        assert off_kb * 1024 + nb <= AKB * 1024
        ap = arena[0:pdim, off_kb * 1024: off_kb * 1024 + nb].bitcast(dt)
        if len(shape) > 1:
            names = " ".join(f"a{i}" for i in range(len(shape)))
            kw = {f"a{i}": int(shape[i]) for i in range(len(shape))}
            ap = ap.rearrange(f"p ({names}) -> p {names}", **kw)
        return ap

    idf = nc.alloc_sbuf_tensor("idf", [128, 128], F32)
    onesf = nc.alloc_sbuf_tensor("onesf", [128, 128], F32)
    u65 = nc.alloc_sbuf_tensor("u65", [64, 65], F32)
    maskat = nc.alloc_sbuf_tensor("maskat", [64, 64], F32)
    h128b = nc.alloc_sbuf_tensor("h128b", [128, 128], BF16)
    h64i2b = nc.alloc_sbuf_tensor("h64i2b", [128, 128], BF16)
    ones1b = nc.alloc_sbuf_tensor("ones1b", [128, 1], BF16)
    epsv = nc.alloc_sbuf_tensor("epsv", [128, 1], F32)
    wqT = nc.alloc_sbuf_tensor("wqT", [128, 16, 128], BF16)
    wkT = nc.alloc_sbuf_tensor("wkT", [128, 16, 128], BF16)
    gnat = nc.alloc_sbuf_tensor("gnat", [16, 128], F32)
    gcol = nc.alloc_sbuf_tensor("gcol", [128, 16], F32)
    stats = nc.alloc_sbuf_tensor("stats", [128, 16], F32)
    ktm = nc.alloc_sbuf_tensor("ktm", [64, 128], F32)
    t1c = nc.alloc_sbuf_tensor("t1c", [64, 128], F32)
    t2c = nc.alloc_sbuf_tensor("t2c", [64, 128], F32)
    gftm = nc.alloc_sbuf_tensor("gftm", [64, 128], F32)
    expb = nc.alloc_sbuf_tensor("expb", [65, 128], F32)
    enb = nc.alloc_sbuf_tensor("enb", [64, 128], F32)
    qxp = nc.alloc_sbuf_tensor("qxp", [64, 128], F32)
    qetm = nc.alloc_sbuf_tensor("qetm", [64, 128], F32)
    ketm = nc.alloc_sbuf_tensor("ketm", [64, 128], F32)
    kdb = nc.alloc_sbuf_tensor("kdb", [64, 128], BF16)
    blc = nc.alloc_sbuf_tensor("blc", [128, 1], F32)
    qefmb = nc.alloc_sbuf_tensor("qefmb", [128, 64], BF16)
    kefmb = nc.alloc_sbuf_tensor("kefmb", [128, 64], BF16)
    atb = nc.alloc_sbuf_tensor("atb", [64, 64], BF16)
    ebcol = nc.alloc_sbuf_tensor("ebcol", [128, 2], F32)
    nmx = nc.alloc_sbuf_tensor("nmx", [64, 4], F32)
    rrow = nc.alloc_sbuf_tensor("rrow", [1, 1024], F32)
    rstd_bc = nc.alloc_sbuf_tensor("rstd_bc", [128, 512], F32)
    wprep = nc.alloc_sbuf_tensor("wprep", [128, 2, 2048], BF16)

    xt = [av(0, 128, (D,), F32), av(8, 128, (D,), F32)]
    lnt = [av(16, 128, (D,), BF16), av(20, 128, (D,), BF16)]
    sqscr = av(28, 128, (D,), BF16)
    qpre = av(0, 128, (TROWS,), F32)
    kpre = av(12, 128, (TROWS,), F32)
    T = av(24, 128, (D,), F32)
    S_b = av(32, 128, (D,), BF16)
    vtm = [av(36, 64, (D,), BF16), av(40, 64, (D,), BF16)]
    o_fm = av(44, 128, (16, 512), BF16)
    lnfm = [av(76, 128, (16, 512), BF16), av(92, 128, (16, 512), BF16)]
    ln1q = av(76, 128, (16, 512), BF16)
    sgq = av(92, 128, (16, 512), BF16)
    wstA = [av(108, 128, (16, 512), BF16), av(124, 128, (16, 512), BF16)]
    wstB = [av(140, 128, (16, 512), BF16), av(156, 128, (16, 512), BF16)]
    woT = av(108, 128, (16, D), BF16)
    osq = av(172, 128, (2, 512), BF16)
    o2t = av(174, 128, (512,), F32)
    xres = av(76, 128, (D,), F32)
    ln2t = av(84, 128, (D,), BF16)
    sqscr2 = av(88, 128, (D,), BF16)
    h_fm = av(0, 128, (64, 1024), BF16)
    ln2fm = av(128, 128, (16, 1024), BF16)
    w1st = [av(160, 128, (16, 128), BF16), av(164, 128, (16, 128), BF16)]
    w3st = [av(168, 128, (16, 128), BF16), av(172, 128, (16, 128), BF16)]
    w2pb = [av(160, 128, (8, 512), BF16), av(168, 128, (8, 512), BF16)]
    x1t = [[av(128 + 4 * i + 2 * par, 128, (512,), F32) for par in range(2)]
           for i in range(8)]
    span = [av(0, 128, (32768,), BF16), av(64, 128, (32768,), BF16)]
    w2ev = [av(132, 128, (2048,), BF16), av(136, 128, (2048,), BF16)]

    PS = nc.alloc_psum_tensor("PS", [128, 4096], F32)
    B65 = PS[0:65, 0:128]
    EBB = PS[0:64, 128:256]
    ATP = PS[0:64, 256:320]
    TK = PS[0:64, 512:640]
    TQ = PS[0:64, 640:768]
    TQE = PS[0:128, 768:832]
    TKE = PS[0:128, 832:896]
    TGN = PS[0:128, 896:912]
    GFP = PS[0:128, 960:1024]
    ps_S = PS[:, 1024:3072]
    ps_o = PS[:, 3072:4096]

    p = P(nc)
    rec = p.rec

    cast_tok = {}

    def em_consts(E):
        r = [
            E.dma_start(idf[:, :], idf_c[:, :]),
            E.dma_start(onesf[:, :], onesf_c[:, :]),
            E.dma_start(u65[:, :], u65_c[:, :]),
            E.dma_start(maskat[:, :], mask_c[:, :]),
            E.dma_start(h128b[:, :], h128_c[:, :]),
            E.dma_start(h64i2b[:, :], h64i2_c[:, :]),
            E.dma_start(ones1b[:, :], ones1b_c[:, :]),
            E.dma_start(epsv[:, :], eps_c[:, :]),
            E.dma_start(gnat[:, :], gnw.ap().rearrange("(a b) -> a b", a=16)),
        ]
        return r

    tok_c = rec("sync", em_consts, inc="cst", n=9, dma=True)
    for _e in ENGS:
        rec(_e, None, waits=[tok_c], n=0)

    wqkT_box = [None]

    tok_gn = rec("tensor",
                 lambda E: [E.matmul(TGN, gnat[:, :], idf[0:16, 0:16], start=True, stop=True,
                                     is_transpose=True, skip_group_check=True)],
                 waits=[tok_c], inc="pe", n=1)
    gcol_tok = rec("vector", lambda E: [E.tensor_copy(gcol[:, :], TGN)], waits=[tok_gn], inc="dv", n=1)


    prep_list = []
    cast_done = {}

    def _cast_pieces(name, fsrc, bdst, row_lo, row_hi, col_lo, col_hi):
        for r0 in range(row_lo, row_hi, 128):
            for c0 in range(col_lo, col_hi, 2048):
                prep_list.append(("cast", name, fsrc, bdst, r0, c0))

    _cast_pieces("wq", wq, wqb, 0, DK, 0, D)
    _cast_pieces("wk", wk, wkb, 0, DK, 0, D)
    prep_list.append(("wqkT",))
    _cast_pieces("wog", wog, wogb, 0, D, 0, D)
    for dc in range(16):
        prep_list.append(("repT", "wog", wogb, wogt2, dc))
    _cast_pieces("wig", wig, wigb, 0, D, 0, D)
    for dc in range(16):
        prep_list.append(("repT", "wig", wigb, wigt2, dc))
    _cast_pieces("wo", wo, wob, 0, D, 0, D)
    for dc in range(16):
        prep_list.append(("repT", "wo", wob, wot2, dc))
    for wn, wfsrc, wbdst, wtdst in (("w1", w1, w1b, w1t2), ("w3", w3, w3b, w3t2)):
        for hq in range(4):
            _cast_pieces(wn, wfsrc, wbdst, hq * 2048, (hq + 1) * 2048, 0, D)
            for dc in range(16):
                prep_list.append(("repack", wbdst, wtdst, dc, hq))
    for pc in range(4):
        _cast_pieces("w2", w2, w2b, 0, D, pc * 2048, (pc + 1) * 2048)
        for g in range(pc * 16, (pc + 1) * 16):
            prep_list.append(("zw", g))

    prep_idx = [0]
    prep_buf_tok = [None, None]
    last_repack_tok = [None]
    last_zw_tok = [None]
    repT_done = {}

    def do_prep_steps(k):
        for _ in range(k):
            if prep_idx[0] >= len(prep_list):
                return
            step = prep_list[prep_idx[0]]
            bi = prep_idx[0] % 2
            prep_idx[0] += 1
            if step[0] == "cast":
                _, name, fsrc, bdst, r0, c0 = step
                nr = min(128, fsrc.shape[0] - r0)
                nc = min(2048, fsrc.shape[1] - c0)
                ld = rec("gpsimd",
                         lambda E, fsrc=fsrc, r0=r0, c0=c0, nr=nr, nc=nc, bi=bi: [
                             E.dma_start(wprep[0:nr, bi, 0:nc],
                                         fsrc[r0:r0 + nr, c0:c0 + nc])],
                         waits=[prep_buf_tok[bi]], inc=f"cl{bi}", n=1, dma=True)
                wr = rec("scalar",
                         lambda E, bdst=bdst, r0=r0, c0=c0, nr=nr, nc=nc, bi=bi: [
                             E.dma_start(bdst[r0:r0 + nr, c0:c0 + nc],
                                         wprep[0:nr, bi, 0:nc])],
                         waits=[ld], inc=f"pw{bi}", n=1, dma=True)
                cast_done[name] = wr
            elif step[0] == "wqkT":
                def em_wqkT(E):
                    r = []
                    for dc in range(16):
                        r.append(E.dma_start(wqT[:, dc, :],
                                             wqb[0:DK, dc * 128:(dc + 1) * 128], transpose=True))
                        r.append(E.dma_start(wkT[:, dc, :],
                                             wkb[0:DK, dc * 128:(dc + 1) * 128], transpose=True))
                    return r
                wr = rec("sync", em_wqkT, waits=[cast_done["wq"], cast_done["wk"]],
                         inc="wqk", n=32, dma=True)
                wqkT_box[0] = wr
            elif step[0] == "repT":
                _, name, bsrc, tdst, dc = step
                ld = rec("sync",
                         lambda E, bsrc=bsrc, dc=dc, bi=bi: [
                             E.dma_start(wprep[:, bi, :],
                                         bsrc[0:D, dc * 128:(dc + 1) * 128],
                                         transpose=True)],
                         waits=[cast_done[name], prep_buf_tok[bi]],
                         inc=f"pl{bi}", n=1, dma=True)
                wr = rec("scalar",
                         lambda E, tdst=tdst, dc=dc, bi=bi: [
                             E.dma_start(tdst[dc, :, :], wprep[:, bi, :])],
                         waits=[ld], inc=f"pw{bi}", n=1, dma=True)
                repT_done[name] = wr
            elif step[0] == "repack":
                _, wbsrc, wdst, dc, hq = step
                ld = rec("sync",
                         lambda E, wbsrc=wbsrc, dc=dc, hq=hq, bi=bi: [
                             E.dma_start(wprep[:, bi, :],
                                         wbsrc[hq * 2048:(hq + 1) * 2048,
                                               dc * 128:(dc + 1) * 128],
                                         transpose=True)],
                         waits=[cast_done["w1" if wbsrc is w1b else "w3"], prep_buf_tok[bi]],
                         inc=f"pl{bi}", n=1, dma=True)
                wr = rec("scalar",
                         lambda E, wdst=wdst, dc=dc, hq=hq, bi=bi: [
                             E.dma_start(wdst[hq * 16:(hq + 1) * 16, :, dc, :]
                                         .rearrange("hb p c -> p hb c"),
                                         wprep[:, bi, :].rearrange("p (hb c) -> p hb c", hb=16))],
                         waits=[ld], inc=f"pw{bi}", n=1, dma=True)
                last_repack_tok[0] = wr
            else:
                g = step[1]
                ld = rec("sync",
                         lambda E, g=g, bi=bi: [
                             E.dma_start(wprep[:, bi, :], w2b[0:D, g * 128:(g + 1) * 128],
                                         transpose=True)],
                         waits=[cast_done["w2"], prep_buf_tok[bi]], inc=f"pl{bi}", n=1, dma=True)
                wr = rec("scalar",
                         lambda E, g=g, bi=bi: [E.dma_start(zw[g, :, :], wprep[:, bi, :])],
                         waits=[ld], inc=f"pw{bi}", n=1, dma=True)
                last_zw_tok[0] = wr
            prep_buf_tok[bi] = wr

    def _finish():
        global _LAST_P
        _LAST_P = p
        fin = [(s, p.cnt[s]) for s in list(p.cnt) if p.cnt[s] > 0]
        rec("sync", lambda E: [], waits=fin, n=0)
        with nc.Block() as block:
            p.emit(block)
        return nc

    statsd = scr("statsd", [TROWS // 128, 128, 8], F32) if "statsd" in DBG else None

    def ln_pass():
        NT = TROWS // 128
        cons = [[], []]
        st_tok = [None, None]
        st_list = []
        sq_guard = [None]
        for i in range(NT):
            b = i % 2
            r0 = i * 128
            ld = rec("sync",
                     lambda E, b=b, r0=r0: [E.dma_start(xt[b][:, :], x_in[r0:r0 + 128, :])],
                     waits=cons[b], inc=f"xt{b}", n=1, dma=True)
            s_sum = rec("vector",
                        lambda E, b=b: [E.reduce_sum(stats[:, b * 8:b * 8 + 1], xt[b][:, :], axis=AX.X)],
                        waits=[ld], inc="dv", n=1)
            a_sq = rec("scalar",
                       lambda E, b=b: [E.activation(sqscr[:, :], xt[b][:, :], AF.Square)],
                       waits=[ld, sq_guard[0]], inc="ac", n=1)
            s_sq = rec("vector",
                       lambda E, b=b: [E.reduce_sum(stats[:, b * 8 + 1:b * 8 + 2], sqscr[:, :],
                                                    axis=AX.X)],
                       waits=[a_sq], inc="dv", n=1)
            sq_guard[0] = s_sq

            t_m = rec("vector",
                      lambda E, b=b: [E.tensor_scalar(stats[:, b * 8 + 2:b * 8 + 3],
                                                      stats[:, b * 8:b * 8 + 1],
                                                      1.0 / D, None, ALU.mult)],
                      waits=[s_sum], inc="dv", n=1)
            t_m2 = rec("vector",
                       lambda E, b=b: [E.scalar_tensor_tensor(stats[:, b * 8 + 3:b * 8 + 4],
                                                              stats[:, b * 8:b * 8 + 1],
                                                              1.0 / (D * D),
                                                              stats[:, b * 8:b * 8 + 1],
                                                              ALU.mult, ALU.mult)],
                       waits=[s_sum], inc="dv", n=1)
            v_tok = rec("vector",
                        lambda E, b=b: [E.tensor_scalar(stats[:, b * 8 + 4:b * 8 + 5],
                                                        stats[:, b * 8 + 1:b * 8 + 2],
                                                        1.0 / D, stats[:, b * 8 + 3:b * 8 + 4],
                                                        ALU.mult, ALU.subtract)],
                        waits=[s_sq, t_m2], inc="dv", n=1)
            sq_tok = rec("scalar",
                         lambda E, b=b: [E.activation(stats[:, b * 8 + 5:b * 8 + 6],
                                                      stats[:, b * 8 + 4:b * 8 + 5], AF.Sqrt, bias=epsv[:, 0:1])],
                         waits=[v_tok], inc="ac", n=1)
            r_tok = rec("vector",
                        lambda E, b=b: [E.reciprocal(stats[:, b * 8 + 6:b * 8 + 7],
                                                     stats[:, b * 8 + 5:b * 8 + 6])],
                        waits=[sq_tok], inc="dv", n=1)
            rb_tok = rec("vector",
                         lambda E, b=b: [E.tensor_scalar(stats[:, b * 8 + 7:b * 8 + 8],
                                                         stats[:, b * 8 + 2:b * 8 + 3],
                                                         stats[:, b * 8 + 6:b * 8 + 7], -1.0,
                                                         ALU.mult, ALU.mult)],
                         waits=[r_tok, t_m], inc="dv", n=1)
            ap_tok = rec("vector",
                         lambda E, b=b: [E.tensor_scalar(lnt[b][:, :], xt[b][:, :],
                                                         stats[:, b * 8 + 6:b * 8 + 7],
                                                         stats[:, b * 8 + 7:b * 8 + 8],
                                                         ALU.mult, ALU.add)],
                         waits=[rb_tok, st_tok[b]], inc="dv", n=1)
            cons[b] = [ap_tok, s_sq]
            if statsd is not None:
                cons[b] = cons[b] + [rec("scalar",
                                         lambda E, b=b, i=i: [
                                             E.dma_start(statsd[i], stats[:, b * 8:b * 8 + 8])],
                                         waits=[rb_tok], inc=f"sd{b}", n=1, dma=True)]
            st_tok[b] = rec("scalar",
                            lambda E, b=b, r0=r0: [E.dma_start(lnscr[r0:r0 + 128, :], lnt[b][:, :])],
                            waits=[ap_tok], inc=f"lnt{b}", n=1, dma=True)
            st_list.append(st_tok[b])
        return st_tok, st_list

    ln_done, ln_st_list = ln_pass()
    do_prep_steps(8)
    wqkT_tok = wqkT_box[0]
    assert wqkT_tok is not None
    if PHASES <= 1:
        return _finish()
    p.barrier()

    zinit_tok = rec("vector", lambda E: [E.memset(T[:, :], 0.0), E.memset(ebcol[:, :], 0.0)],
                    waits=[ln_done[0], ln_done[1]], inc="dv", n=2)

    def proj_pass():
        NSEG = TROWS // 512
        ev_tok = [None, None]
        for s in range(NSEG):
            b = s % 2

            def em_ld(E, s=s, b=b):
                r = []
                for dc in range(16):
                    r.append(E.dma_start(lnfm[b][:, dc, :],
                                         lnscr[s * 512:(s + 1) * 512, dc * 128:(dc + 1) * 128],
                                         transpose=True))
                return r
            seg_ready = [ln_st_list[min(4 * s + 4, len(ln_st_list) - 1)],
                         ln_st_list[min(4 * s + 3, len(ln_st_list) - 1)]]
            ld = rec("sync", em_ld, waits=[wqkT_tok, ev_tok[b]] + seg_ready,
                     inc=f"lf{b}", n=16, dma=True)

            def em_mm(E, b=b):
                r = []
                for dc in range(16):
                    r.append(E.matmul(ps_S[:, 0:512], wqT[:, dc, :], lnfm[b][:, dc, :],
                                      start=(dc == 0), stop=(dc == 15)))
                for dc in range(16):
                    r.append(E.matmul(ps_S[:, 512:1024], wkT[:, dc, :], lnfm[b][:, dc, :],
                                      start=(dc == 0), stop=(dc == 15)))
                return r
            mm = rec("tensor", em_mm, waits=[ld, ev_tok[(b + 1) % 2]], inc="pe", n=32)

            def em_ev(E, s=s):
                return [
                    E.tensor_copy(qpre[:, s * 512:(s + 1) * 512], ps_S[:, 0:512]),
                    E.tensor_copy(kpre[:, s * 512:(s + 1) * 512], ps_S[:, 512:1024]),
                ]
            ev_tok[b] = rec("vector", em_ev, waits=[mm, zinit_tok], inc="dv", n=2)
        return ev_tok[(NSEG - 1) % 2]

    proj_tok = proj_pass()
    if PHASES <= 2:
        return _finish()
    p.barrier()


    tk = {
        "act_read_tr": None, "dve_read_tr": None,
        "act_read_b0": None, "dve_read_b0": None,
        "t_update": zinit_tok, "o_evac": None, "s_b": None,
        "vload": [None, None], "vcons": [None, None],
        "ebcol_guard": [None, None],
        "q_sb_free": None,
        "qg": [],
        "q_ln1q_guard": None,
    }

    def emit_chunk(c):
        is_ctx = c < NCTX
        b = c % 2
        r0 = c * CH

        tk["vload"][b] = rec("gpsimd",
                             lambda E, b=b, r0=r0: [E.dma_start(vtm[b][:, :], lnscr[r0:r0 + CH, :])],
                             waits=[tk["vcons"][b], ln_done[0], ln_done[1]],
                             inc=f"vt{b}", n=1, dma=True)

        t_k = rec("tensor",
                  lambda E, r0=r0: [E.matmul(TK, kpre[:, r0:r0 + CH], idf[:, :],
                                             start=True, stop=True, is_transpose=True,
                                             skip_group_check=True)],
                  waits=[proj_tok, tk["act_read_tr"], tk["dve_read_tr"]] + tk["qg"],
                  inc="pe", n=1)
        t_q = None
        if not is_ctx:
            t_q = rec("tensor",
                      lambda E, r0=r0: [E.matmul(TQ, qpre[:, r0:r0 + CH], idf[:, :],
                                                 start=True, stop=True, is_transpose=True,
                                                 skip_group_check=True)],
                      inc="pe", n=1)

        a_k = rec("scalar", lambda E: [E.activation(ktm[:, :], TK, AF.Sigmoid)],
                  waits=[t_k], inc="ac", n=1)
        a_e = rec("scalar", lambda E: [E.activation(t1c[:, :], ktm[:, :], AF.Exp, scale=-1.0)],
                  waits=[a_k], inc="ac", n=1)
        a_l = rec("scalar", lambda E: [E.activation(t2c[:, :], t1c[:, :], AF.Ln, bias=1.0)],
                  waits=[a_e], inc="ac", n=1)
        d_gf = rec("vector",
                   lambda E: [E.scalar_tensor_tensor(gftm[:, :], ktm[:, :], -1.0, t2c[:, :],
                                                     ALU.mult, ALU.subtract)],
                   waits=[a_l], inc="dv", n=1)

        mm_cs = rec("tensor",
                    lambda E: [E.matmul(B65, u65[:, :], gftm[:, :], start=True, stop=True,
                                        skip_group_check=True)],
                    waits=[d_gf, tk["act_read_b0"], tk["dve_read_b0"]], inc="pe", n=1)

        a_eb = rec("scalar", lambda E: [E.activation(expb[:, :], B65, AF.Exp)],
                   waits=[mm_cs], inc="ac", n=1)
        a_en = rec("scalar", lambda E: [E.activation(enb[:, :], PS[0:64, 0:128], AF.Exp, scale=-1.0)],
                   waits=[mm_cs], inc="ac", n=1)
        tk["act_read_b0"] = a_en

        t_gf = rec("tensor",
                   lambda E: [E.matmul(GFP, gftm[:, :], idf[0:64, 0:64],
                                       start=True, stop=True, is_transpose=True,
                                       skip_group_check=True)],
                   waits=[d_gf], inc="pe", n=1)
        d_bl = rec("vector", lambda E: [E.reduce_sum(blc[:, 0:1], GFP, axis=AX.X)],
                   waits=[t_gf], inc="dv", n=1)
        a_ebc = rec("scalar",
                    lambda E, b=b: [E.activation(ebcol[:, b:b + 1], blc[:, 0:1], AF.Exp)],
                    waits=[d_bl, tk["ebcol_guard"][b]], inc="ac", n=1)
        mm_ebb = rec("tensor",
                     lambda E: [E.matmul(EBB, onesf[64:65, 0:64], expb[64:65, :],
                                         start=True, stop=True, skip_group_check=True)],
                     waits=[a_eb], inc="pe", n=1)

        if not is_ctx:
            d_nm = rec("vector", lambda E: [E.reduce_max(nmx[:, 0:1], TQ, axis=AX.X, negate=True)],
                       waits=[t_q], inc="dv", n=1)
            a_qx = rec("scalar", lambda E: [E.activation(qxp[:, :], TQ, AF.Exp, bias=nmx[:, 0:1])],
                       waits=[d_nm], inc="ac", n=1)

            d_qs = rec("vector", lambda E: [E.reduce_sum(nmx[:, 1:2], qxp[:, :], axis=AX.X)],
                       waits=[a_qx], inc="dv", n=1)
            d_qr = rec("vector", lambda E: [E.reciprocal(nmx[:, 2:3], nmx[:, 1:2])],
                       waits=[d_qs], inc="dv", n=1)
            d_q = rec("vector",
                      lambda E: [E.scalar_tensor_tensor(qetm[:, :], qxp[:, :], nmx[:, 2:3],
                                                        expb[0:64, :], ALU.mult, ALU.mult)],
                      waits=[d_qr, a_eb], inc="dv", n=1)
            tk["act_read_tr"] = a_qx
        else:
            tk["act_read_tr"] = a_k

        d_ke1 = rec("vector", lambda E: [E.tensor_mul(ketm[:, :], ktm[:, :], enb[:, :])],
                    waits=[a_en], inc="dv", n=1)
        d_kd = rec("vector",
                   lambda E: [E.tensor_tensor(kdb[:, :], ketm[:, :], EBB, ALU.mult)],
                   waits=[d_ke1, mm_ebb], inc="dv", n=1)

        mm_at = None
        if not is_ctx:
            t_qe = rec("tensor",
                       lambda E: [E.matmul(TQE, qetm[:, :], idf[0:64, 0:64],
                                           start=True, stop=True, is_transpose=True,
                                           skip_group_check=True)],
                       waits=[d_q], inc="pe", n=1)
            t_ke = rec("tensor",
                       lambda E: [E.matmul(TKE, ketm[:, :], idf[0:64, 0:64],
                                           start=True, stop=True, is_transpose=True,
                                           skip_group_check=True)],
                       waits=[d_ke1], inc="pe", n=1)

            def em_fm(E):
                return [
                    E.tensor_copy(qefmb[:, :], TQE),
                    E.tensor_copy(kefmb[:, :], TKE),
                ]
            d_fm = rec("vector", em_fm, waits=[t_qe, t_ke], inc="dv", n=2)
            tk["dve_read_tr"] = d_fm

            mm_at = rec("tensor",
                        lambda E: [E.matmul(ATP, kefmb[:, 0:64], qefmb[:, 0:64],
                                            start=True, stop=True, skip_group_check=True)],
                        waits=[d_fm], inc="pe", n=1)
        else:
            tk["dve_read_tr"] = d_kd

        if not is_ctx:
            d_at = rec("vector",
                       lambda E: [E.tensor_tensor(atb[:, :], ATP, maskat[:, :], ALU.mult)],
                       waits=[mm_at], inc="dv", n=1)
            tk["dve_read_b0"] = d_at
        else:
            tk["dve_read_b0"] = d_kd
        a_sb = None
        if not is_ctx:
            a_sb = rec("scalar",
                       lambda E: [E.activation(S_b[:, :], T[:, :], AF.Copy)],
                       waits=[tk["t_update"]], inc="ac", n=1)

        mm_o = None
        if not is_ctx:
            def em_o(E, b=b):
                r = []
                for dvc in range(16):
                    col = 3072 + dvc * 64
                    r.append(E.matmul(PS[:, col:col + 64], vtm[b][:, dvc * 128:(dvc + 1) * 128],
                                      atb[:, :], start=True, stop=False,
                                      skip_group_check=True))
                    r.append(E.matmul(PS[:, col:col + 64], S_b[:, dvc * 128:(dvc + 1) * 128],
                                      qefmb[:, :], start=False, stop=True,
                                      skip_group_check=True))
                return r
            mm_o = rec("tensor", em_o,
                       waits=[tk["dve_read_b0"], a_sb, tk["vload"][b], tk["o_evac"],
                              tk["q_sb_free"]] + tk["qg"],
                       inc="pe", n=32)

        def em_s(E, b=b):
            r = []
            for q4 in range(4):
                r.append(E.matmul(ps_S[:, q4 * 512:(q4 + 1) * 512], kdb[:, :],
                                  vtm[b][:, q4 * 512:(q4 + 1) * 512], start=True, stop=True))
            return r
        mm_s = rec("tensor", em_s,
                   waits=[d_kd, tk["vload"][b], tk["t_update"]] + tk["qg"], inc="pe", n=4)
        tk["vcons"][b] = mm_s
        tk["qg"] = []

        d_T = rec("vector",
                  lambda E, b=b: [E.scalar_tensor_tensor(T[:, :], T[:, :], ebcol[:, b:b + 1],
                                                         ps_S[:, :], ALU.mult, ALU.add)],
                  waits=([mm_s, a_ebc, a_sb] if not is_ctx else [mm_s, a_ebc]),
                  inc="dv", n=1)
        tk["t_update"] = d_T
        tk["ebcol_guard"][b] = d_T

        if not is_ctx:
            qi = (c - NCTX) % 8
            d_oe = rec("vector",
                       lambda E, qi=qi: [E.tensor_copy(o_fm[:, :, qi * 64:(qi + 1) * 64],
                                                       ps_o.rearrange("p (a b) -> p a b", a=16))],
                       waits=[mm_o], inc="dv", n=1)
            tk["o_evac"] = d_oe
        if glad is not None and c == NCTX:
            def em_dbg(E):
                return [
                    E.dma_start(glad["ktm"][:, :], ktm[:, :]),
                    E.dma_start(glad["expb"][:, :], expb[:, :]),
                    E.dma_start(glad["qetm"][:, :], qetm[:, :]),
                    E.dma_start(glad["ketm"][:, :], ketm[:, :]),
                    E.dma_start(glad["atb"][:, :], atb[:, :]),
                    E.dma_start(glad["kdb"][:, :], kdb[:, :]),
                    E.dma_start(glad["ofm0"][:, :, :], o_fm[:, :, 0:64]),
                    E.dma_start(glad["qefmb"][:, :], qefmb[:, :]),
                    E.dma_start(glad["S_b"][:, :], S_b[:, :]),
                    E.dma_start(glad["ebcol"][:, :], ebcol[:, :]),
                    E.dma_start(glad["T"][:, :], T[:, :]),
                ]
            rec("scalar", em_dbg, waits=[tk["t_update"], tk["o_evac"]], inc="dbg9", n=11, dma=True)

    wo_psum_free = [None, None]

    glad = None
    if "glad" in DBG:
        def dout(name, shape, dt):
            return nc.dram_tensor(name, shape, dt, kind="ExternalOutput")
        glad = {
            "ktm": dout("g_ktm", [64, 128], F32), "expb": dout("g_expb", [65, 128], F32),
            "qetm": dout("g_qetm", [64, 128], F32), "ketm": dout("g_ketm", [64, 128], F32),
            "atb": dout("g_atb", [64, 64], BF16), "S_b": dout("g_Sb", [128, D], BF16),
            "ebcol": dout("g_ebcol", [128, 2], F32), "T": dout("g_T", [128, D], F32),
            "kdb": dout("g_kdb", [64, 128], BF16),
            "ofm0": dout("g_ofm0", [128, 16, 64], F32),
            "qefmb": dout("g_qefmb", [128, 64], BF16),
        }
    ofmd = scr("ofmd", [4, 128, 16, 512], BF16) if "ofmd" in DBG else None
    qkd = scr("qkd", [2, 128, TROWS], F32) if "qkd" in DBG else None
    sgd = scr("sgd", [4, 128, 16, 512], BF16) if "sgd" in DBG else None
    rbd = scr("rbd", [4, 128, 512], F32) if "rbd" in DBG else None
    gcold = nc.dram_tensor("gcold", [128, 16], F32, kind="ExternalOutput") if "gcold" in DBG else None

    def emit_quarter(q):
        mq0 = q * 512
        xr0 = CTX + mq0
        if ofmd is not None:
            rec("scalar", lambda E, q=q: [E.dma_start(ofmd[q], o_fm[:, :, :])],
                waits=[tk["o_evac"]], inc="dbg1", n=1, dma=True)
        if qkd is not None and q == 0:
            rec("scalar", lambda E: [E.dma_start(qkd[0], qpre[:, :]),
                                     E.dma_start(qkd[1], kpre[:, :])],
                waits=[proj_tok], inc="dbg2", n=2, dma=True)

        def em_lq(E):
            r = []
            for dc in range(16):
                r.append(E.dma_start(ln1q[:, dc, :],
                                     lnscr[xr0:xr0 + 512, dc * 128:(dc + 1) * 128], transpose=True))
            return r
        lq = rec("sync", em_lq, waits=[tk["q_ln1q_guard"], ln_done[0], ln_done[1]],
                 inc="l1q", n=16, dma=True)

        def emit_rms():
            prev_mm = None
            mm_ms = None
            for oc in range(16):
                a_sq = rec("scalar",
                           lambda E, oc=oc: [E.activation(osq[:, oc % 2, :], o_fm[:, oc, :],
                                                          AF.Square)],
                           waits=[tk["o_evac"], prev_mm], inc="ac", n=1)
                mm_ms = rec("tensor",
                            lambda E, oc=oc: [E.matmul(PS[0:1, 2048:2560], ones1b[:, :],
                                                       osq[:, oc % 2, :],
                                                       start=(oc == 0), stop=(oc == 15),
                                                       skip_group_check=True)],
                            waits=[a_sq, tk["t_update"]], inc="pe", n=1)
                prev_mm = mm_ms
            a_rs = rec("scalar",
                       lambda E: [E.activation(rrow[0:1, 0:512], PS[0:1, 2048:2560], AF.Sqrt,
                                               scale=1.0 / D, bias=epsv[0:1, 0:1])],
                       waits=[mm_ms], inc="ac", n=1)
            d_rr = rec("vector", lambda E: [E.reciprocal(rrow[0:1, 512:1024], rrow[0:1, 0:512])],
                       waits=[a_rs], inc="dv", n=1)
            mm_bc = rec("tensor",
                        lambda E: [E.matmul(PS[:, 2560:3072], onesf[0:1, :], rrow[0:1, 512:1024],
                                            start=True, stop=True, skip_group_check=True)],
                        waits=[d_rr], inc="pe", n=1)
            d_bc = rec("vector", lambda E: [E.tensor_copy(rstd_bc[:, :], PS[:, 2560:3072])],
                       waits=[mm_bc], inc="dv", n=1)
            return mm_bc, d_bc

        wst_tok = [None, None]
        sg_toks = []
        mm_bc = None
        d_bc = None
        for ocg in range(4):
            bi = ocg % 2

            def em_wst(E, ocg=ocg, bi=bi):
                r = []
                for d0 in range(0, 16, 4):
                    r.append(E.dma_start(
                        wstA[bi][:, d0:d0 + 4, :],
                        wogt2[d0:d0 + 4, :, ocg * 512:(ocg + 1) * 512]
                        .rearrange("dc p w -> p dc w")))
                    r.append(E.dma_start(
                        wstB[bi][:, d0:d0 + 4, :],
                        wigt2[d0:d0 + 4, :, ocg * 512:(ocg + 1) * 512]
                        .rearrange("dc p w -> p dc w")))
                return r
            wl = rec("sync", em_wst,
                     waits=[repT_done["wog"], repT_done["wig"], wst_tok[bi]],
                     inc=f"ws{bi}", n=8, dma=True)

            last_mm = None
            for oci in range(4):
                oc = ocg * 4 + oci

                def em_go(E, oci=oci, bi=bi):
                    r = []
                    for dc in range(16):
                        r.append(E.matmul(PS[:, oci * 512:(oci + 1) * 512],
                                          wstA[bi][:, dc, oci * 128:(oci + 1) * 128],
                                          o_fm[:, dc, :], start=(dc == 0), stop=False))
                    for dc in range(16):
                        r.append(E.matmul(PS[:, oci * 512:(oci + 1) * 512],
                                          wstB[bi][:, dc, oci * 128:(oci + 1) * 128],
                                          ln1q[:, dc, :], start=False, stop=(dc == 15)))
                    return r
                waits = [wl, lq, tk["o_evac"]]
                if ocg == 0 and oci == 0:
                    waits += [tk["t_update"], tk["act_read_tr"], tk["dve_read_tr"],
                              tk["act_read_b0"], tk["dve_read_b0"]]
                if len(sg_toks) >= 4:
                    waits += [sg_toks[len(sg_toks) - 4]]
                mm_go = rec("tensor", em_go, waits=waits, inc="pe", n=32)
                last_mm = mm_go
                a_sg = rec("scalar",
                           lambda E, oc=oc, oci=oci: [E.activation(sgq[:, oc, :],
                                                                   PS[:, oci * 512:(oci + 1) * 512],
                                                                   AF.Silu)],
                           waits=[mm_go], inc="ac", n=1)
                sg_toks.append(a_sg)
            wst_tok[bi] = last_mm
            if ocg == 0:
                mm_bc, d_bc = emit_rms()

        dbg_sg_tok = None
        if sgd is not None:
            dbg_sg_tok = rec("scalar", lambda E, q=q: [E.dma_start(sgd[q], sgq[:, :, :])],
                             waits=[sg_toks[-1]], inc="dbg3", n=1, dma=True)
        if "gcold" in DBG and q == 0:
            rec("scalar", lambda E: [E.dma_start(gcold[:, :], gcol[:, :])],
                waits=[gcol_tok], inc="dbg5", n=1, dma=True)
        o2_last = dbg_sg_tok
        for oc in range(16):
            o2_a = rec("vector",
                       lambda E, oc=oc: [E.scalar_tensor_tensor(o2t[:, :], o_fm[:, oc, :],
                                                                gcol[:, oc:oc + 1],
                                                                rstd_bc[:, :], ALU.mult, ALU.mult)],
                       waits=[sg_toks[oc], d_bc, gcol_tok, o2_last], inc="dv", n=1)
            o2_last = rec("vector",
                          lambda E, oc=oc: [E.tensor_tensor(sgq[:, oc, :], o2t[:, :],
                                                            sgq[:, oc, :], ALU.mult)],
                          waits=[o2_a], inc="dv", n=1)
        tk["q_sb_free"] = o2_last

        def em_wot(E):
            r = []
            for d0 in range(0, 16, 2):
                r.append(E.dma_start(
                    woT[:, d0:d0 + 2, :],
                    wot2[d0:d0 + 2, :, :].rearrange("dv p w -> p dv w")))
            return r
        wot_l = rec("sync", em_wot, waits=[repT_done["wo"], wst_tok[0], wst_tok[1]],
                    inc="wot", n=8, dma=True)

        last_st = None
        sq2_guard = [None]
        for t in range(4):
            pbase = 2048 if t % 2 == 0 else 0
            xl = rec("scalar",
                     lambda E, t=t: [E.dma_start(xres[:, :],
                                                 x_in[xr0 + t * 128:xr0 + (t + 1) * 128, :])],
                     waits=[wst_tok[0], wst_tok[1], last_st], inc="xr", n=1, dma=True)

            def em_wmm(E, t=t, pbase=pbase):
                r = []
                for dvc in range(16):
                    for os_ in range(4):
                        r.append(E.matmul(PS[:, pbase + os_ * 512:pbase + (os_ + 1) * 512],
                                          sgq[:, dvc, t * 128:(t + 1) * 128],
                                          woT[:, dvc, os_ * 512:(os_ + 1) * 512],
                                          start=(dvc == 0), stop=(dvc == 15),
                                          skip_group_check=True))
                return r
            mm_wo = rec("tensor", em_wmm,
                        waits=[wot_l, o2_last, wo_psum_free[t % 2]], inc="pe", n=64)

            d_x1 = rec("vector",
                       lambda E, pbase=pbase: [E.tensor_add(xres[:, :], xres[:, :],
                                                            PS[:, pbase:pbase + 2048])],
                       waits=[mm_wo, xl], inc="dv", n=1)
            wo_psum_free[t % 2] = d_x1
            s_sum = rec("vector", lambda E: [E.reduce_sum(stats[:, 0:1], xres[:, :], axis=AX.X)],
                        waits=[d_x1], inc="dv", n=1)
            a_sq2 = rec("scalar", lambda E: [E.activation(sqscr2[:, :], xres[:, :], AF.Square)],
                        waits=[d_x1, sq2_guard[0]], inc="ac", n=1)
            s_sq = rec("vector",
                       lambda E: [E.reduce_sum(stats[:, 1:2], sqscr2[:, :], axis=AX.X)],
                       waits=[a_sq2], inc="dv", n=1)
            sq2_guard[0] = s_sq

            t_m = rec("vector",
                      lambda E: [E.tensor_scalar(stats[:, 2:3], stats[:, 0:1], 1.0 / D,
                                                 None, ALU.mult)],
                      waits=[s_sum], inc="dv", n=1)
            t_m2 = rec("vector",
                       lambda E: [E.scalar_tensor_tensor(stats[:, 3:4], stats[:, 0:1],
                                                         1.0 / (D * D), stats[:, 0:1],
                                                         ALU.mult, ALU.mult)],
                       waits=[s_sum], inc="dv", n=1)
            v_tok = rec("vector",
                        lambda E: [E.tensor_scalar(stats[:, 4:5], stats[:, 1:2], 1.0 / D,
                                                   stats[:, 3:4], ALU.mult, ALU.subtract)],
                        waits=[s_sq, t_m2], inc="dv", n=1)
            sq_tok = rec("scalar", lambda E: [E.activation(stats[:, 5:6], stats[:, 4:5],
                                                           AF.Sqrt, bias=epsv[:, 0:1])],
                         waits=[v_tok], inc="ac", n=1)
            r_tok = rec("vector",
                        lambda E: [E.reciprocal(stats[:, 6:7], stats[:, 5:6])],
                        waits=[sq_tok], inc="dv", n=1)
            rb_tok = rec("vector",
                         lambda E: [E.tensor_scalar(stats[:, 7:8], stats[:, 2:3], stats[:, 6:7],
                                                    -1.0, ALU.mult, ALU.mult)],
                         waits=[r_tok, t_m], inc="dv", n=1)
            a_ln2 = rec("vector",
                        lambda E: [E.tensor_scalar(ln2t[:, :], xres[:, :], stats[:, 6:7],
                                                   stats[:, 7:8], ALU.mult, ALU.add)],
                        waits=[rb_tok, s_sq], inc="dv", n=1)

            def em_st(E, t=t):
                return [
                    E.dma_start(x1scr[mq0 + t * 128:mq0 + (t + 1) * 128, :], xres[:, :]),
                    E.dma_start(ln2scr[mq0 + t * 128:mq0 + (t + 1) * 128, :], ln2t[:, :]),
                ]
            last_st = rec("scalar", em_st, waits=[a_ln2], inc="qst", n=2, dma=True)

        if rbd is not None:
            rec("scalar", lambda E, q=q: [E.dma_start(rbd[q], rstd_bc[:, :])],
                waits=[d_bc], inc="dbg4", n=1, dma=True)
        tk["qg"] = [wo_psum_free[0], wo_psum_free[1]]
        tk["q_ln1q_guard"] = last_st
        return last_st

    q_done = []
    if SKIP_ATTN:
        q_done.append(ln_done[1])
    else:
        for c in range(NCH):
            emit_chunk(c)
            do_prep_steps(9)
            if PHASES >= 4 and c >= NCTX and (c - NCTX) % 8 == 7:
                q_done.append(emit_quarter((c - NCTX) // 8))
                do_prep_steps(12)
    do_prep_steps(10000)
    if PHASES <= 3:
        return _finish()
    p.barrier()

    def w2_transform():
        zw_flat = zw.ap().rearrange("g (s lr) d -> (g s) (lr d)", s=2)
        w2p_v = w2p.ap().rearrange("q gb (s lr) gi c -> lr gb s gi q c", s=2)
        sp_tok = [None, None]
        ev_toks = []
        wr_toks = []
        for qq in range(4):
            bi = qq % 2
            ld = rec("sync",
                     lambda E, qq=qq, bi=bi: [
                         E.dma_start(span[bi][:, :], zw_flat[:, qq * 32768:(qq + 1) * 32768])],
                     waits=[("pw0", p.cnt.get("pw0", 0)), ("pw1", p.cnt.get("pw1", 0)),
                            sp_tok[bi], q_done[-1]],
                     inc=f"sp{bi}", n=1, dma=True)
            for wv in range(16):
                eb = wv % 2

                def em_mm(E, bi=bi, wv=wv):
                    r = []
                    for j in range(4):
                        cs = wv * 4 + j
                        r.append(E.matmul(PS[:, j * 512:(j + 1) * 512], h64i2b[:, :],
                                          span[bi][:, cs * 512:(cs + 1) * 512],
                                          start=True, stop=True))
                    return r
                mm = rec("tensor", em_mm,
                         waits=[ld, ev_toks[-1] if ev_toks else None], inc="pe", n=4)
                a_ev = rec("vector",
                           lambda E, eb=eb: [E.tensor_copy(w2ev[eb][:, :], PS[:, 0:2048])],
                           waits=[mm, wr_toks[-2] if len(wr_toks) >= 2 else None],
                           inc="dv", n=1)
                ev_toks.append(a_ev)
                lq_ = (qq * 32768 + wv * 2048) // D

                def em_wr(E, lq_=lq_, eb=eb):
                    r = []
                    for dqi in range(4):
                        dst = w2p_v[lq_:lq_ + 1, :, :, :, dqi:dqi + 1].squeeze(0).squeeze(3)
                        r.append(E.dma_start(dst, w2ev[eb][:, dqi * 512:(dqi + 1) * 512]))
                    return r
                wr = rec("scalar", em_wr, waits=[a_ev], inc=f"we{eb}", n=4, dma=True)
                wr_toks.append(wr)
            sp_tok[bi] = wr_toks[-1]
        return [("we0", p.cnt.get("we0", 0)), ("we1", p.cnt.get("we1", 0))]

    if PHASES <= 4:
        return _finish()
    w2p_tok = w2_transform()
    if PHASES <= 5:
        return _finish()
    p.barrier()

    prev_tt_guard = []

    def ffn_tt(tt):
        tok0 = tt * 1024

        def em_ln2(E, tok0=tok0):
            r = []
            for dc in range(16):
                r.append(E.dma_start(ln2fm[:, dc, :],
                                     ln2scr[tok0:tok0 + 1024, dc * 128:(dc + 1) * 128],
                                     transpose=True))
            return r
        lq = rec("sync", em_ln2, waits=[q_done[-1]] + list(prev_tt_guard) + list(w2p_tok),
                 inc="lfm", n=16, dma=True)

        st_tok = [None, None]
        h_toks = []
        pend_h = [None]

        def emit_h128(gp, pAp, d_hp):
            def em_h128(E, g=gp, pA=pAp):
                return [
                    E.matmul(PS[:, pA:pA + 512], h128b[:, :], h_fm[:, g, 0:512],
                             start=True, stop=True),
                    E.matmul(PS[:, pA + 512:pA + 1024], h128b[:, :], h_fm[:, g, 512:1024],
                             start=True, stop=True),
                ]
            mm_h = rec("tensor", em_h128, waits=[d_hp], inc="pe", n=2)
            d_y = rec("vector",
                      lambda E, g=gp, pA=pAp: [E.tensor_copy(h_fm[:, g, :], PS[:, pA:pA + 1024])],
                      waits=[mm_h], inc="dv", n=1)
            h_toks.append((d_hp, d_y))

        for g in range(64):
            bi = g % 2

            def em_st(E, g=g, bi=bi):
                return [
                    E.dma_start(w1st[bi][:, :, :].rearrange("p a b -> p (a b)"),
                                w1t2[g].rearrange("p a b -> p (a b)")),
                    E.dma_start(w3st[bi][:, :, :].rearrange("p a b -> p (a b)"),
                                w3t2[g].rearrange("p a b -> p (a b)")),
                ]
            sl = rec("sync", em_st,
                     waits=[("pw0", p.cnt.get("pw0", 0)), ("pw1", p.cnt.get("pw1", 0)),
                            st_tok[bi]],
                     inc=f"fs{bi}", n=2, dma=True)

            pA = 0 if g % 2 == 0 else 2048
            pB = 1024 if g % 2 == 0 else 3072

            def em_up(E, bi=bi, pA=pA, pB=pB):
                r = []
                for th in range(2):
                    for dc in range(16):
                        r.append(E.matmul(PS[:, pA + th * 512:pA + (th + 1) * 512],
                                          w1st[bi][:, dc, :],
                                          ln2fm[:, dc, th * 512:(th + 1) * 512],
                                          start=(dc == 0), stop=(dc == 15)))
                for th in range(2):
                    for dc in range(16):
                        r.append(E.matmul(PS[:, pB + th * 512:pB + (th + 1) * 512],
                                          w3st[bi][:, dc, :],
                                          ln2fm[:, dc, th * 512:(th + 1) * 512],
                                          start=(dc == 0), stop=(dc == 15)))
                return r
            waits = [sl, lq]
            if h_toks:
                waits.append(h_toks[-1][1])
            if g < 2:
                waits.extend(prev_tt_guard)
            mm_up = rec("tensor", em_up, waits=waits, inc="pe", n=64)
            st_tok[bi] = mm_up

            if pend_h[0] is not None:
                emit_h128(*pend_h[0])

            a_si = rec("scalar",
                       lambda E, g=g, pA=pA: [E.activation(h_fm[:, g, :], PS[:, pA:pA + 1024],
                                                           AF.Silu)],
                       waits=[mm_up], inc="ac", n=1)
            d_h = rec("vector",
                      lambda E, g=g, pB=pB: [E.tensor_tensor(h_fm[:, g, :], h_fm[:, g, :],
                                                             PS[:, pB:pB + 1024], ALU.mult)],
                      waits=[a_si], inc="dv", n=1)
            pend_h[0] = (g, pA, d_h)
        emit_h128(*pend_h[0])
        pend_h[0] = None

        w2_buf_tok = [None, None]
        last_st = {(t, par): None for t in range(8) for par in range(2)}
        last_out_dv = []
        for dq in range(4):
            par = dq % 2
            xl_toks = []
            for tg in range(8):
                xw = [last_st[(tg, par)]] if last_st[(tg, par)] is not None \
                    else list(w2p_tok) + list(prev_tt_guard)
                xl = rec("scalar",
                         lambda E, tok0=tok0, dq=dq, tg=tg, par=par: [
                             E.dma_start(x1t[tg][par][:, :],
                                         x1scr[tok0 + tg * 128:tok0 + (tg + 1) * 128,
                                               dq * 512:(dq + 1) * 512])],
                         waits=xw, inc=f"x1{tg}", n=1, dma=True)
                xl_toks.append(xl)
            mm_w2 = None
            for gblk in range(8):
                bi = gblk % 2

                def em_wl(E, dq=dq, gblk=gblk, bi=bi):
                    return [E.dma_start(w2pb[bi][:, :, :].rearrange("p a b -> p (a b)"),
                                        w2p[dq, gblk].rearrange("p a b -> p (a b)"))]
                wl_waits = list(w2p_tok) + [w2_buf_tok[bi]]
                if dq == 0 and gblk < 2:
                    wl_waits += [h_toks[-1][1]]
                wl = rec("sync", em_wl, waits=wl_waits, inc=f"wp{bi}", n=1, dma=True)

                def em_w2(E, gblk=gblk, bi=bi):
                    r = []
                    for gin in range(8):
                        g = gblk * 8 + gin
                        for tg in range(8):
                            r.append(E.matmul(PS[:, tg * 512:(tg + 1) * 512],
                                              h_fm[:, g, tg * 128:(tg + 1) * 128],
                                              w2pb[bi][:, gin, :],
                                              start=(gblk == 0 and gin == 0),
                                              stop=(gblk == 7 and gin == 7),
                                              skip_group_check=True))
                    return r
                waits = [wl, h_toks[-1][1], h_toks[-2][1]]
                if gblk == 0 and last_out_dv:
                    waits += [last_out_dv[-1]]
                mm_w2 = rec("tensor", em_w2, waits=waits, inc="pe", n=64)
                w2_buf_tok[bi] = mm_w2

            for tg in range(8):
                d_o = rec("vector",
                          lambda E, tg=tg, par=par: [
                              E.scalar_tensor_tensor(x1t[tg][par][:, :],
                                                     PS[:, tg * 512:(tg + 1) * 512], RSQN,
                                                     x1t[tg][par][:, :], ALU.mult, ALU.add)],
                          waits=[mm_w2, xl_toks[tg]], inc="dv", n=1)
                last_out_dv.append(d_o)
                st = rec("scalar",
                         lambda E, tok0=tok0, dq=dq, tg=tg, par=par: [
                             E.dma_start(out[tok0 + tg * 128:tok0 + (tg + 1) * 128,
                                             dq * 512:(dq + 1) * 512], x1t[tg][par][:, :])],
                         waits=[d_o], inc=f"os{tg}", n=1, dma=True)
                last_st[(tg, par)] = st
        prev_tt_guard[:] = [(f"os{t}", p.cnt.get(f"os{t}", 0)) for t in range(8)]

    ffn_tt(0)
    ffn_tt(1)

    rec("sync", lambda E: [],
        waits=[(f"os{t}", p.cnt.get(f"os{t}", 0)) for t in range(8)], n=0)

    with nc.Block() as block:
        p.emit(block)
    return nc


_NC_CACHE = None
_LAST_P = None


def _get_nc():
    global _NC_CACHE
    if _NC_CACHE is None:
        _NC_CACHE = build_nc()
    return _NC_CACHE


def kernel(**inputs):
    x = np.ascontiguousarray(np.asarray(inputs["L0_attn_hidden_states"], dtype=np.float32))
    ws = {
        "wq": np.ascontiguousarray(np.asarray(inputs["L0_attn_Wq"], np.float32)),
        "wk": np.ascontiguousarray(np.asarray(inputs["L0_attn_Wk"], np.float32)),
        "wog": np.ascontiguousarray(np.asarray(inputs["L0_attn_Wog"], np.float32)),
        "wig": np.ascontiguousarray(np.asarray(inputs["L0_attn_Wig"], np.float32)),
        "wo": np.ascontiguousarray(np.asarray(inputs["L0_attn_Wo"], np.float32)),
        "gnw": np.ascontiguousarray(np.asarray(inputs["L0_attn_g_norm_w"], np.float32)),
        "w1": np.ascontiguousarray(np.asarray(inputs["L0_ffn_W1"], np.float32)),
        "w2": np.ascontiguousarray(np.asarray(inputs["L0_ffn_W2"], np.float32)),
        "w3": np.ascontiguousarray(np.asarray(inputs["L0_ffn_W3"], np.float32)),
    }
    in_maps = []
    for cid in range(NCORES):
        b, s = cid // 4, cid % 4
        xin = np.zeros((TROWS, D), np.float32)
        xin[CTX:] = x[b, s * TLOC:(s + 1) * TLOC]
        if s > 0:
            xin[:CTX] = x[b, s * TLOC - CTX:s * TLOC]
        m = {"x": xin}
        m.update(ws)
        in_maps.append(m)
    nc = _get_nc()
    res = run_bass_kernel_spmd(nc, in_maps, core_ids=list(range(NCORES)))
    outa = np.empty((B, L, D), np.float32)
    for cid in range(NCORES):
        b, s = cid // 4, cid % 4
        outa[b, s * TLOC:(s + 1) * TLOC] = res.results[cid]["out"]
    return outa


# version 08 (final)
# speedup vs baseline: 1.1303x; 1.0725x over previous
import math
import numpy as np
import ml_dtypes

import concourse.bass as bass
import concourse.mybir as mybir
from concourse.bass_utils import run_bass_kernel_spmd

F32 = mybir.dt.float32
BF16 = mybir.dt.bfloat16
AF = mybir.ActivationFunctionType
ALU = mybir.AluOpType
AX = mybir.AxisListType

B, L, D = 2, 8192, 2048
NCORES = 8
TLOC = 2048
CTX = 1024
TROWS = TLOC + CTX
CH = 64
NCH = TROWS // CH
NCTX = CTX // CH
DK = 128
HID = 8192
EPS = 1e-5
RSQN = 1.0 / math.sqrt(float(HID))

DBG = set()
PHASES = 6
GATE_CASTS = False
SKIP_ATTN = False
T_UPTO = 3
T_CS = 64
T_QQ = 4
DEFER_CASTS = False

ENGS = ("sync", "gpsimd", "scalar", "vector", "tensor")


def _hadamard(n):
    h = np.ones((n, n), np.float32)
    for i in range(n):
        for j in range(n):
            if bin(i & j).count("1") & 1:
                h[i, j] = -1.0
    return h


class P:

    def __init__(self, nc):
        self.nc = nc
        self.ops = {e: [] for e in ENGS}
        self.sems = {}
        self.cnt = {}
        for s in ("sy", "sc", "gp", "pe", "ac", "dv", "ph"):
            self.sems[s] = nc.alloc_semaphore(f"s_{s}")
            self.cnt[s] = 0

    def rec(self, eng, emit, waits=(), inc=None, n=1, dma=False):
        if inc is not None and inc not in self.sems:
            self.sems[inc] = self.nc.alloc_semaphore(f"s_{inc}")
            self.cnt[inc] = 0
        amt = 16 if dma else 1
        ws = []
        for w in waits:
            if w is None:
                continue
            s, v = w
            assert v <= self.cnt[s], f"wait on future token: {s} {v} > {self.cnt[s]}"
            ws.append((s, v))
        tok = None
        if inc is not None and n > 0:
            self.cnt[inc] += amt * n
            tok = (inc, self.cnt[inc])
        self.ops[eng].append((tuple(ws), emit, inc, amt, n))
        return tok

    def barrier(self):
        self.cnt["ph"] += 5
        target = self.cnt["ph"]
        ph = self.sems["ph"]
        for e in ENGS:
            self.ops[e].append(((), (lambda E, ph=ph: [E.sem_inc(ph, 1)]), None, 1, 1))
            self.ops[e].append(((("ph", target),), None, None, 1, 0))
        return ("ph", target)

    def emit(self, block):
        for e in ENGS:
            oplist = self.ops[e]
            sems = self.sems

            def runner(E, oplist=oplist):
                hw = {}
                for ws, em, inc, amt, n in oplist:
                    for s, v in ws:
                        if hw.get(s, 0) < v:
                            E.wait_ge(sems[s], v)
                            hw[s] = v
                    if em is None:
                        continue
                    instrs = em(E)
                    if inc is not None:
                        assert len(instrs) == n, f"{e}: {len(instrs)} != {n}"
                        for i in instrs:
                            i.then_inc(sems[inc], amt)

            getattr(block, e)(runner)


def build_nc():
    nc = bass.Bass("TRN2", target_bir_lowering=False)

    def din(name, shape):
        return nc.dram_tensor(name, shape, F32, kind="ExternalInput")

    x_in = din("x", [TROWS, D])
    wq, wk = din("wq", [DK, D]), din("wk", [DK, D])
    wog, wig, wo = din("wog", [D, D]), din("wig", [D, D]), din("wo", [D, D])
    gnw = din("gnw", [D])
    w1, w2, w3 = din("w1", [HID, D]), din("w2", [D, HID]), din("w3", [HID, D])
    out = nc.dram_tensor("out", [TLOC, D], F32, kind="ExternalOutput")

    def scr(name, shape, dt):
        kind = "ExternalOutput" if name in DBG else "Internal"
        return nc.dram_tensor(name, shape, dt, kind=kind)

    wqb, wkb = scr("wqb", [DK, D], BF16), scr("wkb", [DK, D], BF16)
    wogb, wigb, wob = scr("wogb", [D, D], BF16), scr("wigb", [D, D], BF16), scr("wob", [D, D], BF16)
    w1b, w3b, w2b = scr("w1b", [HID, D], BF16), scr("w3b", [HID, D], BF16), scr("w2b", [D, HID], BF16)
    w1t2 = scr("w1t2", [64, 128, 16, 128], BF16)
    w3t2 = scr("w3t2", [64, 128, 16, 128], BF16)
    wogt2 = scr("wogt2", [16, 128, D], BF16)
    wigt2 = scr("wigt2", [16, 128, D], BF16)
    wot2 = scr("wot2", [16, 128, D], BF16)
    zw = scr("zw", [64, 128, D], BF16)
    w2p = scr("w2p", [4, 8, 128, 8, 512], BF16)
    lnscr = scr("lnscr", [TROWS, D], BF16)
    x1scr = scr("x1scr", [TLOC, D], F32)
    ln2scr = scr("ln2scr", [TLOC, D], BF16)

    idf_c = nc.inline_tensor(np.eye(128, dtype=np.float32), name="idf_c")
    onesf_c = nc.inline_tensor(np.ones((128, 128), np.float32), name="onesf_c")
    u65_np = np.zeros((64, 65), np.float32)
    for j in range(64):
        u65_np[j, j:64] = 1.0
        u65_np[j, 64] = 1.0
    u65_c = nc.inline_tensor(u65_np, name="u65_c")
    mask_np = np.zeros((64, 64), np.float32)
    for j in range(64):
        mask_np[j, j:] = 1.0
    mask_c = nc.inline_tensor(mask_np, name="mask_c")
    h128_c = nc.inline_tensor(_hadamard(128).astype(ml_dtypes.bfloat16), name="h128_c")
    h64 = _hadamard(64)
    h64i2_np = np.zeros((128, 128), np.float32)
    for gp in range(64):
        for s in range(2):
            col = (gp // 8) * 16 + s * 8 + (gp % 8)
            for g in range(64):
                h64i2_np[2 * g + s, col] = h64[g, gp]
    h64i2_c = nc.inline_tensor(h64i2_np.astype(ml_dtypes.bfloat16), name="h64i2_c")
    ones1b_c = nc.inline_tensor(np.ones((128, 1), ml_dtypes.bfloat16), name="ones1b_c")
    eps_c = nc.inline_tensor(np.full((128, 1), EPS, np.float32), name="eps_c")

    AKB = 176
    arena = nc.alloc_sbuf_tensor("arena", [128, AKB * 1024], mybir.dt.uint8)

    def av(off_kb, pdim, shape, dt):
        nb = int(np.prod(shape)) * mybir.dt.size(dt)
        assert off_kb * 1024 + nb <= AKB * 1024
        ap = arena[0:pdim, off_kb * 1024: off_kb * 1024 + nb].bitcast(dt)
        if len(shape) > 1:
            names = " ".join(f"a{i}" for i in range(len(shape)))
            kw = {f"a{i}": int(shape[i]) for i in range(len(shape))}
            ap = ap.rearrange(f"p ({names}) -> p {names}", **kw)
        return ap

    idf = nc.alloc_sbuf_tensor("idf", [128, 128], F32)
    onesf = nc.alloc_sbuf_tensor("onesf", [128, 128], F32)
    u65 = nc.alloc_sbuf_tensor("u65", [64, 65], F32)
    maskat = nc.alloc_sbuf_tensor("maskat", [64, 64], F32)
    h128b = nc.alloc_sbuf_tensor("h128b", [128, 128], BF16)
    h64i2b = nc.alloc_sbuf_tensor("h64i2b", [128, 128], BF16)
    ones1b = nc.alloc_sbuf_tensor("ones1b", [128, 1], BF16)
    epsv = nc.alloc_sbuf_tensor("epsv", [128, 1], F32)
    wqT = nc.alloc_sbuf_tensor("wqT", [128, 16, 128], BF16)
    wkT = nc.alloc_sbuf_tensor("wkT", [128, 16, 128], BF16)
    gnat = nc.alloc_sbuf_tensor("gnat", [16, 128], F32)
    gcol = nc.alloc_sbuf_tensor("gcol", [128, 16], F32)
    stats = nc.alloc_sbuf_tensor("stats", [128, 16], F32)
    ktm = nc.alloc_sbuf_tensor("ktm", [64, 128], F32)
    t1c = nc.alloc_sbuf_tensor("t1c", [64, 128], F32)
    t2c = nc.alloc_sbuf_tensor("t2c", [64, 128], F32)
    gftm = nc.alloc_sbuf_tensor("gftm", [64, 128], F32)
    expb = nc.alloc_sbuf_tensor("expb", [65, 128], F32)
    enb = nc.alloc_sbuf_tensor("enb", [64, 128], F32)
    qxp = nc.alloc_sbuf_tensor("qxp", [64, 128], F32)
    qetm = nc.alloc_sbuf_tensor("qetm", [64, 128], F32)
    ketm = nc.alloc_sbuf_tensor("ketm", [64, 128], F32)
    kdb = nc.alloc_sbuf_tensor("kdb", [64, 128], BF16)
    blc = nc.alloc_sbuf_tensor("blc", [128, 1], F32)
    qefmb = nc.alloc_sbuf_tensor("qefmb", [128, 64], BF16)
    kefmb = nc.alloc_sbuf_tensor("kefmb", [128, 64], BF16)
    atb = nc.alloc_sbuf_tensor("atb", [64, 64], BF16)
    ebcol = nc.alloc_sbuf_tensor("ebcol", [128, 2], F32)
    nmx = nc.alloc_sbuf_tensor("nmx", [64, 4], F32)
    rrow = nc.alloc_sbuf_tensor("rrow", [1, 1024], F32)
    rstd_bc = nc.alloc_sbuf_tensor("rstd_bc", [128, 512], F32)
    wprep = nc.alloc_sbuf_tensor("wprep", [128, 2, 2048], BF16)

    xt = [av(0, 128, (D,), F32), av(8, 128, (D,), F32)]
    lnt = [av(16, 128, (D,), BF16), av(20, 128, (D,), BF16)]
    sqscr = av(28, 128, (D,), BF16)
    qpre = av(0, 128, (TROWS,), F32)
    kpre = av(12, 128, (TROWS,), F32)
    T = av(24, 128, (D,), F32)
    S_b = av(32, 128, (D,), BF16)
    vtm = [av(36, 64, (D,), BF16), av(40, 64, (D,), BF16)]
    o_fm = av(44, 128, (16, 512), BF16)
    lnfm = [av(76, 128, (16, 512), BF16), av(92, 128, (16, 512), BF16)]
    ln1q = av(76, 128, (16, 512), BF16)
    sgq = av(92, 128, (16, 512), BF16)
    wstA = [av(108, 128, (16, 512), BF16), av(124, 128, (16, 512), BF16)]
    wstB = [av(140, 128, (16, 512), BF16), av(156, 128, (16, 512), BF16)]
    woT = av(108, 128, (16, D), BF16)
    osq = av(172, 128, (2, 512), BF16)
    o2t = av(174, 128, (512,), F32)
    xres = av(76, 128, (D,), F32)
    ln2t = av(84, 128, (D,), BF16)
    sqscr2 = av(88, 128, (D,), BF16)
    h_fm = av(0, 128, (64, 1024), BF16)
    wprep_s = [wprep[:, 0, :], wprep[:, 1, :]] + \
        [av(60 + 4 * i, 128, (2048,), BF16) for i in range(4)]
    ln2fm = av(128, 128, (16, 1024), BF16)
    w1st = [av(160, 128, (16, 128), BF16), av(164, 128, (16, 128), BF16)]
    w3st = [av(168, 128, (16, 128), BF16), av(172, 128, (16, 128), BF16)]
    w2pb = [av(160, 128, (8, 512), BF16), av(168, 128, (8, 512), BF16)]
    x1t = [[av(128 + 4 * i + 2 * par, 128, (512,), F32) for par in range(2)]
           for i in range(8)]
    span = [av(0, 128, (32768,), BF16), av(64, 128, (32768,), BF16)]
    w2ev = [av(132, 128, (2048,), BF16), av(136, 128, (2048,), BF16)]

    PS = nc.alloc_psum_tensor("PS", [128, 4096], F32)
    B65 = PS[0:65, 0:128]
    EBB = PS[0:64, 128:256]
    ATP = PS[0:64, 256:320]
    TK = PS[0:64, 512:640]
    TQ = PS[0:64, 640:768]
    TQE = PS[0:128, 768:832]
    TKE = PS[0:128, 832:896]
    TGN = PS[0:128, 896:912]
    GFP = PS[0:128, 960:1024]
    ps_S = PS[:, 1024:3072]
    ps_o = PS[:, 3072:4096]

    p = P(nc)
    rec = p.rec

    cast_tok = {}

    def em_consts(E):
        r = [
            E.dma_start(idf[:, :], idf_c[:, :]),
            E.dma_start(onesf[:, :], onesf_c[:, :]),
            E.dma_start(u65[:, :], u65_c[:, :]),
            E.dma_start(maskat[:, :], mask_c[:, :]),
            E.dma_start(h128b[:, :], h128_c[:, :]),
            E.dma_start(h64i2b[:, :], h64i2_c[:, :]),
            E.dma_start(ones1b[:, :], ones1b_c[:, :]),
            E.dma_start(epsv[:, :], eps_c[:, :]),
            E.dma_start(gnat[:, :], gnw.ap().rearrange("(a b) -> a b", a=16)),
        ]
        return r

    tok_c = rec("sync", em_consts, inc="cst", n=9, dma=True)
    for _e in ENGS:
        rec(_e, None, waits=[tok_c], n=0)

    wqkT_box = [None]

    tok_gn = rec("tensor",
                 lambda E: [E.matmul(TGN, gnat[:, :], idf[0:16, 0:16], start=True, stop=True,
                                     is_transpose=True, skip_group_check=True)],
                 waits=[tok_c], inc="pe", n=1)
    gcol_tok = rec("vector", lambda E: [E.tensor_copy(gcol[:, :], TGN)], waits=[tok_gn], inc="dv", n=1)


    prep_list = []
    cast_done = {}

    def _cast_pieces(name, fsrc, bdst, row_lo, row_hi, col_lo, col_hi):
        for r0 in range(row_lo, row_hi, 128):
            for c0 in range(col_lo, col_hi, 2048):
                prep_list.append(("cast", name, fsrc, bdst, r0, c0))

    _cast_pieces("wq", wq, wqb, 0, DK, 0, D)
    _cast_pieces("wk", wk, wkb, 0, DK, 0, D)
    prep_list.append(("wqkT",))
    _cast_pieces("wog", wog, wogb, 0, D, 0, D)
    for dc in range(16):
        prep_list.append(("repT", "wog", wogb, wogt2, dc))
    _cast_pieces("wig", wig, wigb, 0, D, 0, D)
    for dc in range(16):
        prep_list.append(("repT", "wig", wigb, wigt2, dc))
    _cast_pieces("wo", wo, wob, 0, D, 0, D)
    for dc in range(16):
        prep_list.append(("repT", "wo", wob, wot2, dc))
    for wn, wfsrc, wbdst, wtdst in (("w1", w1, w1b, w1t2), ("w3", w3, w3b, w3t2)):
        for hq in range(4):
            _cast_pieces(wn, wfsrc, wbdst, hq * 2048, (hq + 1) * 2048, 0, D)
            for dc in range(16):
                prep_list.append(("repack", wbdst, wtdst, dc, hq))
    for pc in range(4):
        _cast_pieces("w2", w2, w2b, 0, D, pc * 2048, (pc + 1) * 2048)
        for g in range(pc * 16, (pc + 1) * 16):
            prep_list.append(("zw", g))

    NPREP = 6
    prep_idx = [0]
    prep_buf_tok = [None] * NPREP
    last_repack_tok = [None]
    last_zw_tok = [None]
    repT_done = {}

    def do_prep_steps(k):
        for _ in range(k):
            if prep_idx[0] >= len(prep_list):
                return
            step = prep_list[prep_idx[0]]
            bi = prep_idx[0] % NPREP
            prep_idx[0] += 1
            if step[0] == "cast":
                _, name, fsrc, bdst, r0, c0 = step
                nr = min(128, fsrc.shape[0] - r0)
                nc = min(2048, fsrc.shape[1] - c0)
                ld = rec("gpsimd",
                         lambda E, fsrc=fsrc, r0=r0, c0=c0, nr=nr, nc=nc, bi=bi: [
                             E.dma_start(wprep_s[bi][0:nr, 0:nc],
                                         fsrc[r0:r0 + nr, c0:c0 + nc])],
                         waits=[prep_buf_tok[bi]], inc=f"cl{bi}", n=1, dma=True)
                wr = rec("scalar",
                         lambda E, bdst=bdst, r0=r0, c0=c0, nr=nr, nc=nc, bi=bi: [
                             E.dma_start(bdst[r0:r0 + nr, c0:c0 + nc],
                                         wprep_s[bi][0:nr, 0:nc])],
                         waits=[ld], inc=f"pw{bi}", n=1, dma=True)
                cast_done[name] = wr
            elif step[0] == "wqkT":
                def em_wqkT(E):
                    r = []
                    for dc in range(16):
                        r.append(E.dma_start(wqT[:, dc, :],
                                             wqb[0:DK, dc * 128:(dc + 1) * 128], transpose=True))
                        r.append(E.dma_start(wkT[:, dc, :],
                                             wkb[0:DK, dc * 128:(dc + 1) * 128], transpose=True))
                    return r
                wr = rec("sync", em_wqkT, waits=[cast_done["wq"], cast_done["wk"]],
                         inc="wqk", n=32, dma=True)
                wqkT_box[0] = wr
            elif step[0] == "repT":
                _, name, bsrc, tdst, dc = step
                ld = rec("sync",
                         lambda E, bsrc=bsrc, dc=dc, bi=bi: [
                             E.dma_start(wprep_s[bi][:, :],
                                         bsrc[0:D, dc * 128:(dc + 1) * 128],
                                         transpose=True)],
                         waits=[cast_done[name], prep_buf_tok[bi]],
                         inc=f"pl{bi}", n=1, dma=True)
                wr = rec("scalar",
                         lambda E, tdst=tdst, dc=dc, bi=bi: [
                             E.dma_start(tdst[dc, :, :], wprep_s[bi][:, :])],
                         waits=[ld], inc=f"pw{bi}", n=1, dma=True)
                repT_done[name] = wr
            elif step[0] == "repack":
                _, wbsrc, wdst, dc, hq = step
                ld = rec("sync",
                         lambda E, wbsrc=wbsrc, dc=dc, hq=hq, bi=bi: [
                             E.dma_start(wprep_s[bi][:, :],
                                         wbsrc[hq * 2048:(hq + 1) * 2048,
                                               dc * 128:(dc + 1) * 128],
                                         transpose=True)],
                         waits=[cast_done["w1" if wbsrc is w1b else "w3"], prep_buf_tok[bi]],
                         inc=f"pl{bi}", n=1, dma=True)
                wr = rec("scalar",
                         lambda E, wdst=wdst, dc=dc, hq=hq, bi=bi: [
                             E.dma_start(wdst[hq * 16:(hq + 1) * 16, :, dc, :]
                                         .rearrange("hb p c -> p hb c"),
                                         wprep_s[bi][:, :].rearrange("p (hb c) -> p hb c", hb=16))],
                         waits=[ld], inc=f"pw{bi}", n=1, dma=True)
                last_repack_tok[0] = wr
            else:
                g = step[1]
                ld = rec("sync",
                         lambda E, g=g, bi=bi: [
                             E.dma_start(wprep_s[bi][:, :], w2b[0:D, g * 128:(g + 1) * 128],
                                         transpose=True)],
                         waits=[cast_done["w2"], prep_buf_tok[bi]], inc=f"pl{bi}", n=1, dma=True)
                wr = rec("scalar",
                         lambda E, g=g, bi=bi: [E.dma_start(zw[g, :, :], wprep_s[bi][:, :])],
                         waits=[ld], inc=f"pw{bi}", n=1, dma=True)
                last_zw_tok[0] = wr
            prep_buf_tok[bi] = wr

    def _finish():
        global _LAST_P
        _LAST_P = p
        fin = [(s, p.cnt[s]) for s in list(p.cnt) if p.cnt[s] > 0]
        rec("sync", lambda E: [], waits=fin, n=0)
        with nc.Block() as block:
            p.emit(block)
        return nc

    statsd = scr("statsd", [TROWS // 128, 128, 8], F32) if "statsd" in DBG else None

    def ln_pass():
        NT = TROWS // 128
        cons = [[], []]
        st_tok = [None, None]
        st_list = []
        sq_guard = [None]
        for i in range(NT):
            b = i % 2
            r0 = i * 128
            ld = rec("sync",
                     lambda E, b=b, r0=r0: [E.dma_start(xt[b][:, :], x_in[r0:r0 + 128, :])],
                     waits=cons[b], inc=f"xt{b}", n=1, dma=True)
            s_sum = rec("vector",
                        lambda E, b=b: [E.reduce_sum(stats[:, b * 8:b * 8 + 1], xt[b][:, :], axis=AX.X)],
                        waits=[ld], inc="dv", n=1)
            a_sq = rec("scalar",
                       lambda E, b=b: [E.activation(sqscr[:, :], xt[b][:, :], AF.Square)],
                       waits=[ld, sq_guard[0]], inc="ac", n=1)
            s_sq = rec("vector",
                       lambda E, b=b: [E.reduce_sum(stats[:, b * 8 + 1:b * 8 + 2], sqscr[:, :],
                                                    axis=AX.X)],
                       waits=[a_sq], inc="dv", n=1)
            sq_guard[0] = s_sq

            t_m = rec("vector",
                      lambda E, b=b: [E.tensor_scalar(stats[:, b * 8 + 2:b * 8 + 3],
                                                      stats[:, b * 8:b * 8 + 1],
                                                      1.0 / D, None, ALU.mult)],
                      waits=[s_sum], inc="dv", n=1)
            t_m2 = rec("vector",
                       lambda E, b=b: [E.scalar_tensor_tensor(stats[:, b * 8 + 3:b * 8 + 4],
                                                              stats[:, b * 8:b * 8 + 1],
                                                              1.0 / (D * D),
                                                              stats[:, b * 8:b * 8 + 1],
                                                              ALU.mult, ALU.mult)],
                       waits=[s_sum], inc="dv", n=1)
            v_tok = rec("vector",
                        lambda E, b=b: [E.tensor_scalar(stats[:, b * 8 + 4:b * 8 + 5],
                                                        stats[:, b * 8 + 1:b * 8 + 2],
                                                        1.0 / D, stats[:, b * 8 + 3:b * 8 + 4],
                                                        ALU.mult, ALU.subtract)],
                        waits=[s_sq, t_m2], inc="dv", n=1)
            sq_tok = rec("scalar",
                         lambda E, b=b: [E.activation(stats[:, b * 8 + 5:b * 8 + 6],
                                                      stats[:, b * 8 + 4:b * 8 + 5], AF.Sqrt, bias=epsv[:, 0:1])],
                         waits=[v_tok], inc="ac", n=1)
            r_tok = rec("vector",
                        lambda E, b=b: [E.reciprocal(stats[:, b * 8 + 6:b * 8 + 7],
                                                     stats[:, b * 8 + 5:b * 8 + 6])],
                        waits=[sq_tok], inc="dv", n=1)
            rb_tok = rec("vector",
                         lambda E, b=b: [E.tensor_scalar(stats[:, b * 8 + 7:b * 8 + 8],
                                                         stats[:, b * 8 + 2:b * 8 + 3],
                                                         stats[:, b * 8 + 6:b * 8 + 7], -1.0,
                                                         ALU.mult, ALU.mult)],
                         waits=[r_tok, t_m], inc="dv", n=1)
            ap_tok = rec("vector",
                         lambda E, b=b: [E.tensor_scalar(lnt[b][:, :], xt[b][:, :],
                                                         stats[:, b * 8 + 6:b * 8 + 7],
                                                         stats[:, b * 8 + 7:b * 8 + 8],
                                                         ALU.mult, ALU.add)],
                         waits=[rb_tok, st_tok[b]], inc="dv", n=1)
            cons[b] = [ap_tok, s_sq]
            if statsd is not None:
                cons[b] = cons[b] + [rec("scalar",
                                         lambda E, b=b, i=i: [
                                             E.dma_start(statsd[i], stats[:, b * 8:b * 8 + 8])],
                                         waits=[rb_tok], inc=f"sd{b}", n=1, dma=True)]
            st_tok[b] = rec("scalar",
                            lambda E, b=b, r0=r0: [E.dma_start(lnscr[r0:r0 + 128, :], lnt[b][:, :])],
                            waits=[ap_tok], inc=f"lnt{b}", n=1, dma=True)
            st_list.append(st_tok[b])
        return st_tok, st_list

    ln_done, ln_st_list = ln_pass()
    do_prep_steps(8)
    wqkT_tok = wqkT_box[0]
    assert wqkT_tok is not None
    if PHASES <= 1:
        return _finish()
    p.barrier()

    zinit_tok = rec("vector", lambda E: [E.memset(T[:, :], 0.0), E.memset(ebcol[:, :], 0.0)],
                    waits=[ln_done[0], ln_done[1]], inc="dv", n=2)

    def proj_pass():
        NSEG = TROWS // 512
        ev_tok = [None, None]
        for s in range(NSEG):
            b = s % 2

            def em_ld(E, s=s, b=b):
                r = []
                for dc in range(16):
                    r.append(E.dma_start(lnfm[b][:, dc, :],
                                         lnscr[s * 512:(s + 1) * 512, dc * 128:(dc + 1) * 128],
                                         transpose=True))
                return r
            seg_ready = [ln_st_list[min(4 * s + 4, len(ln_st_list) - 1)],
                         ln_st_list[min(4 * s + 3, len(ln_st_list) - 1)]]
            ld = rec("sync", em_ld, waits=[wqkT_tok, ev_tok[b]] + seg_ready,
                     inc=f"lf{b}", n=16, dma=True)

            def em_mm(E, b=b):
                r = []
                for dc in range(16):
                    r.append(E.matmul(ps_S[:, 0:512], wqT[:, dc, :], lnfm[b][:, dc, :],
                                      start=(dc == 0), stop=(dc == 15)))
                for dc in range(16):
                    r.append(E.matmul(ps_S[:, 512:1024], wkT[:, dc, :], lnfm[b][:, dc, :],
                                      start=(dc == 0), stop=(dc == 15)))
                return r
            mm = rec("tensor", em_mm, waits=[ld, ev_tok[(b + 1) % 2]], inc="pe", n=32)

            def em_ev(E, s=s):
                return [
                    E.tensor_copy(qpre[:, s * 512:(s + 1) * 512], ps_S[:, 0:512]),
                    E.tensor_copy(kpre[:, s * 512:(s + 1) * 512], ps_S[:, 512:1024]),
                ]
            ev_tok[b] = rec("vector", em_ev, waits=[mm, zinit_tok], inc="dv", n=2)
        return ev_tok[(NSEG - 1) % 2]

    proj_tok = proj_pass()
    if PHASES <= 2:
        return _finish()
    p.barrier()


    tk = {
        "act_read_tr": None, "dve_read_tr": None,
        "act_read_b0": None, "dve_read_b0": None,
        "t_update": zinit_tok, "o_evac": None, "s_b": None,
        "vload": [None, None], "vcons": [None, None],
        "ebcol_guard": [None, None],
        "q_sb_free": None,
        "qg": [],
        "q_ln1q_guard": None,
    }

    def emit_chunk(c):
        is_ctx = c < NCTX
        b = c % 2
        r0 = c * CH

        tk["vload"][b] = rec("gpsimd",
                             lambda E, b=b, r0=r0: [E.dma_start(vtm[b][:, :], lnscr[r0:r0 + CH, :])],
                             waits=[tk["vcons"][b], ln_done[0], ln_done[1]],
                             inc=f"vt{b}", n=1, dma=True)

        t_k = rec("tensor",
                  lambda E, r0=r0: [E.matmul(TK, kpre[:, r0:r0 + CH], idf[:, :],
                                             start=True, stop=True, is_transpose=True,
                                             skip_group_check=True)],
                  waits=[proj_tok, tk["act_read_tr"], tk["dve_read_tr"]] + tk["qg"],
                  inc="pe", n=1)
        t_q = None
        if not is_ctx:
            t_q = rec("tensor",
                      lambda E, r0=r0: [E.matmul(TQ, qpre[:, r0:r0 + CH], idf[:, :],
                                                 start=True, stop=True, is_transpose=True,
                                                 skip_group_check=True)],
                      inc="pe", n=1)

        a_k = rec("scalar", lambda E: [E.activation(ktm[:, :], TK, AF.Sigmoid)],
                  waits=[t_k], inc="ac", n=1)
        a_e = rec("scalar", lambda E: [E.activation(t1c[:, :], ktm[:, :], AF.Exp, scale=-1.0)],
                  waits=[a_k], inc="ac", n=1)
        a_l = rec("scalar", lambda E: [E.activation(t2c[:, :], t1c[:, :], AF.Ln, bias=1.0)],
                  waits=[a_e], inc="ac", n=1)
        d_gf = rec("vector",
                   lambda E: [E.scalar_tensor_tensor(gftm[:, :], ktm[:, :], -1.0, t2c[:, :],
                                                     ALU.mult, ALU.subtract)],
                   waits=[a_l], inc="dv", n=1)

        mm_cs = rec("tensor",
                    lambda E: [E.matmul(B65, u65[:, :], gftm[:, :], start=True, stop=True,
                                        skip_group_check=True)],
                    waits=[d_gf, tk["act_read_b0"], tk["dve_read_b0"]], inc="pe", n=1)

        a_eb = rec("scalar", lambda E: [E.activation(expb[:, :], B65, AF.Exp)],
                   waits=[mm_cs], inc="ac", n=1)
        a_en = rec("scalar", lambda E: [E.activation(enb[:, :], PS[0:64, 0:128], AF.Exp, scale=-1.0)],
                   waits=[mm_cs], inc="ac", n=1)
        tk["act_read_b0"] = a_en

        t_gf = rec("tensor",
                   lambda E: [E.matmul(GFP, gftm[:, :], idf[0:64, 0:64],
                                       start=True, stop=True, is_transpose=True,
                                       skip_group_check=True)],
                   waits=[d_gf], inc="pe", n=1)
        d_bl = rec("vector", lambda E: [E.reduce_sum(blc[:, 0:1], GFP, axis=AX.X)],
                   waits=[t_gf], inc="dv", n=1)
        a_ebc = rec("scalar",
                    lambda E, b=b: [E.activation(ebcol[:, b:b + 1], blc[:, 0:1], AF.Exp)],
                    waits=[d_bl, tk["ebcol_guard"][b]], inc="ac", n=1)
        mm_ebb = rec("tensor",
                     lambda E: [E.matmul(EBB, onesf[64:65, 0:64], expb[64:65, :],
                                         start=True, stop=True, skip_group_check=True)],
                     waits=[a_eb], inc="pe", n=1)

        if not is_ctx:
            d_nm = rec("vector", lambda E: [E.reduce_max(nmx[:, 0:1], TQ, axis=AX.X, negate=True)],
                       waits=[t_q], inc="dv", n=1)
            a_qx = rec("scalar", lambda E: [E.activation(qxp[:, :], TQ, AF.Exp, bias=nmx[:, 0:1])],
                       waits=[d_nm], inc="ac", n=1)

            d_qs = rec("vector", lambda E: [E.reduce_sum(nmx[:, 1:2], qxp[:, :], axis=AX.X)],
                       waits=[a_qx], inc="dv", n=1)
            d_qr = rec("vector", lambda E: [E.reciprocal(nmx[:, 2:3], nmx[:, 1:2])],
                       waits=[d_qs], inc="dv", n=1)
            d_q = rec("vector",
                      lambda E: [E.scalar_tensor_tensor(qetm[:, :], qxp[:, :], nmx[:, 2:3],
                                                        expb[0:64, :], ALU.mult, ALU.mult)],
                      waits=[d_qr, a_eb], inc="dv", n=1)
            tk["act_read_tr"] = a_qx
        else:
            tk["act_read_tr"] = a_k

        d_ke1 = rec("vector", lambda E: [E.tensor_mul(ketm[:, :], ktm[:, :], enb[:, :])],
                    waits=[a_en], inc="dv", n=1)
        d_kd = rec("vector",
                   lambda E: [E.tensor_tensor(kdb[:, :], ketm[:, :], EBB, ALU.mult)],
                   waits=[d_ke1, mm_ebb], inc="dv", n=1)

        mm_at = None
        if not is_ctx:
            t_qe = rec("tensor",
                       lambda E: [E.matmul(TQE, qetm[:, :], idf[0:64, 0:64],
                                           start=True, stop=True, is_transpose=True,
                                           skip_group_check=True)],
                       waits=[d_q], inc="pe", n=1)
            t_ke = rec("tensor",
                       lambda E: [E.matmul(TKE, ketm[:, :], idf[0:64, 0:64],
                                           start=True, stop=True, is_transpose=True,
                                           skip_group_check=True)],
                       waits=[d_ke1], inc="pe", n=1)

            def em_fm(E):
                return [
                    E.tensor_copy(qefmb[:, :], TQE),
                    E.tensor_copy(kefmb[:, :], TKE),
                ]
            d_fm = rec("vector", em_fm, waits=[t_qe, t_ke], inc="dv", n=2)
            tk["dve_read_tr"] = d_fm

            mm_at = rec("tensor",
                        lambda E: [E.matmul(ATP, kefmb[:, 0:64], qefmb[:, 0:64],
                                            start=True, stop=True, skip_group_check=True)],
                        waits=[d_fm], inc="pe", n=1)
        else:
            tk["dve_read_tr"] = d_kd

        if not is_ctx:
            d_at = rec("vector",
                       lambda E: [E.tensor_tensor(atb[:, :], ATP, maskat[:, :], ALU.mult)],
                       waits=[mm_at], inc="dv", n=1)
            tk["dve_read_b0"] = d_at
        else:
            tk["dve_read_b0"] = d_kd
        a_sb = None
        if not is_ctx:
            a_sb = rec("scalar",
                       lambda E: [E.activation(S_b[:, :], T[:, :], AF.Copy)],
                       waits=[tk["t_update"]], inc="ac", n=1)

        mm_o = None
        if not is_ctx:
            def em_o(E, b=b):
                r = []
                for dvc in range(16):
                    col = 3072 + dvc * 64
                    r.append(E.matmul(PS[:, col:col + 64], vtm[b][:, dvc * 128:(dvc + 1) * 128],
                                      atb[:, :], start=True, stop=False,
                                      skip_group_check=True))
                    r.append(E.matmul(PS[:, col:col + 64], S_b[:, dvc * 128:(dvc + 1) * 128],
                                      qefmb[:, :], start=False, stop=True,
                                      skip_group_check=True))
                return r
            mm_o = rec("tensor", em_o,
                       waits=[tk["dve_read_b0"], a_sb, tk["vload"][b], tk["o_evac"],
                              tk["q_sb_free"]] + tk["qg"],
                       inc="pe", n=32)

        def em_s(E, b=b):
            r = []
            for q4 in range(4):
                r.append(E.matmul(ps_S[:, q4 * 512:(q4 + 1) * 512], kdb[:, :],
                                  vtm[b][:, q4 * 512:(q4 + 1) * 512], start=True, stop=True))
            return r
        mm_s = rec("tensor", em_s,
                   waits=[d_kd, tk["vload"][b], tk["t_update"]] + tk["qg"], inc="pe", n=4)
        tk["vcons"][b] = mm_s
        tk["qg"] = []

        d_T = rec("vector",
                  lambda E, b=b: [E.scalar_tensor_tensor(T[:, :], T[:, :], ebcol[:, b:b + 1],
                                                         ps_S[:, :], ALU.mult, ALU.add)],
                  waits=([mm_s, a_ebc, a_sb] if not is_ctx else [mm_s, a_ebc]),
                  inc="dv", n=1)
        tk["t_update"] = d_T
        tk["ebcol_guard"][b] = d_T

        if not is_ctx:
            qi = (c - NCTX) % 8
            d_oe = rec("vector",
                       lambda E, qi=qi: [E.tensor_copy(o_fm[:, :, qi * 64:(qi + 1) * 64],
                                                       ps_o.rearrange("p (a b) -> p a b", a=16))],
                       waits=[mm_o], inc="dv", n=1)
            tk["o_evac"] = d_oe
        if glad is not None and c == NCTX:
            def em_dbg(E):
                return [
                    E.dma_start(glad["ktm"][:, :], ktm[:, :]),
                    E.dma_start(glad["expb"][:, :], expb[:, :]),
                    E.dma_start(glad["qetm"][:, :], qetm[:, :]),
                    E.dma_start(glad["ketm"][:, :], ketm[:, :]),
                    E.dma_start(glad["atb"][:, :], atb[:, :]),
                    E.dma_start(glad["kdb"][:, :], kdb[:, :]),
                    E.dma_start(glad["ofm0"][:, :, :], o_fm[:, :, 0:64]),
                    E.dma_start(glad["qefmb"][:, :], qefmb[:, :]),
                    E.dma_start(glad["S_b"][:, :], S_b[:, :]),
                    E.dma_start(glad["ebcol"][:, :], ebcol[:, :]),
                    E.dma_start(glad["T"][:, :], T[:, :]),
                ]
            rec("scalar", em_dbg, waits=[tk["t_update"], tk["o_evac"]], inc="dbg9", n=11, dma=True)

    wo_psum_free = [None, None]

    glad = None
    if "glad" in DBG:
        def dout(name, shape, dt):
            return nc.dram_tensor(name, shape, dt, kind="ExternalOutput")
        glad = {
            "ktm": dout("g_ktm", [64, 128], F32), "expb": dout("g_expb", [65, 128], F32),
            "qetm": dout("g_qetm", [64, 128], F32), "ketm": dout("g_ketm", [64, 128], F32),
            "atb": dout("g_atb", [64, 64], BF16), "S_b": dout("g_Sb", [128, D], BF16),
            "ebcol": dout("g_ebcol", [128, 2], F32), "T": dout("g_T", [128, D], F32),
            "kdb": dout("g_kdb", [64, 128], BF16),
            "ofm0": dout("g_ofm0", [128, 16, 64], F32),
            "qefmb": dout("g_qefmb", [128, 64], BF16),
        }
    ofmd = scr("ofmd", [4, 128, 16, 512], BF16) if "ofmd" in DBG else None
    qkd = scr("qkd", [2, 128, TROWS], F32) if "qkd" in DBG else None
    sgd = scr("sgd", [4, 128, 16, 512], BF16) if "sgd" in DBG else None
    rbd = scr("rbd", [4, 128, 512], F32) if "rbd" in DBG else None
    gcold = nc.dram_tensor("gcold", [128, 16], F32, kind="ExternalOutput") if "gcold" in DBG else None

    def emit_quarter(q):
        mq0 = q * 512
        xr0 = CTX + mq0
        if ofmd is not None:
            rec("scalar", lambda E, q=q: [E.dma_start(ofmd[q], o_fm[:, :, :])],
                waits=[tk["o_evac"]], inc="dbg1", n=1, dma=True)
        if qkd is not None and q == 0:
            rec("scalar", lambda E: [E.dma_start(qkd[0], qpre[:, :]),
                                     E.dma_start(qkd[1], kpre[:, :])],
                waits=[proj_tok], inc="dbg2", n=2, dma=True)

        def em_lq(E):
            r = []
            for dc in range(16):
                r.append(E.dma_start(ln1q[:, dc, :],
                                     lnscr[xr0:xr0 + 512, dc * 128:(dc + 1) * 128], transpose=True))
            return r
        lq = rec("sync", em_lq, waits=[tk["q_ln1q_guard"], ln_done[0], ln_done[1]],
                 inc="l1q", n=16, dma=True)

        def emit_rms():
            prev_mm = None
            mm_ms = None
            for oc in range(16):
                a_sq = rec("scalar",
                           lambda E, oc=oc: [E.activation(osq[:, oc % 2, :], o_fm[:, oc, :],
                                                          AF.Square)],
                           waits=[tk["o_evac"], prev_mm], inc="ac", n=1)
                mm_ms = rec("tensor",
                            lambda E, oc=oc: [E.matmul(PS[0:1, 2048:2560], ones1b[:, :],
                                                       osq[:, oc % 2, :],
                                                       start=(oc == 0), stop=(oc == 15),
                                                       skip_group_check=True)],
                            waits=[a_sq, tk["t_update"]], inc="pe", n=1)
                prev_mm = mm_ms
            a_rs = rec("scalar",
                       lambda E: [E.activation(rrow[0:1, 0:512], PS[0:1, 2048:2560], AF.Sqrt,
                                               scale=1.0 / D, bias=epsv[0:1, 0:1])],
                       waits=[mm_ms], inc="ac", n=1)
            d_rr = rec("vector", lambda E: [E.reciprocal(rrow[0:1, 512:1024], rrow[0:1, 0:512])],
                       waits=[a_rs], inc="dv", n=1)
            mm_bc = rec("tensor",
                        lambda E: [E.matmul(PS[:, 2560:3072], onesf[0:1, :], rrow[0:1, 512:1024],
                                            start=True, stop=True, skip_group_check=True)],
                        waits=[d_rr], inc="pe", n=1)
            d_bc = rec("vector", lambda E: [E.tensor_copy(rstd_bc[:, :], PS[:, 2560:3072])],
                       waits=[mm_bc], inc="dv", n=1)
            return mm_bc, d_bc

        wst_tok = [None, None]
        sg_toks = []
        mm_bc = None
        d_bc = None
        for ocg in range(4):
            bi = ocg % 2

            def em_wst(E, ocg=ocg, bi=bi):
                r = []
                for d0 in range(0, 16, 4):
                    r.append(E.dma_start(
                        wstA[bi][:, d0:d0 + 4, :],
                        wogt2[d0:d0 + 4, :, ocg * 512:(ocg + 1) * 512]
                        .rearrange("dc p w -> p dc w")))
                    r.append(E.dma_start(
                        wstB[bi][:, d0:d0 + 4, :],
                        wigt2[d0:d0 + 4, :, ocg * 512:(ocg + 1) * 512]
                        .rearrange("dc p w -> p dc w")))
                return r
            wl = rec("sync", em_wst,
                     waits=[repT_done["wog"], repT_done["wig"], wst_tok[bi]],
                     inc=f"ws{bi}", n=8, dma=True)

            last_mm = None
            for oci in range(4):
                oc = ocg * 4 + oci

                def em_go(E, oci=oci, bi=bi):
                    r = []
                    for dc in range(16):
                        r.append(E.matmul(PS[:, oci * 512:(oci + 1) * 512],
                                          wstA[bi][:, dc, oci * 128:(oci + 1) * 128],
                                          o_fm[:, dc, :], start=(dc == 0), stop=False))
                    for dc in range(16):
                        r.append(E.matmul(PS[:, oci * 512:(oci + 1) * 512],
                                          wstB[bi][:, dc, oci * 128:(oci + 1) * 128],
                                          ln1q[:, dc, :], start=False, stop=(dc == 15)))
                    return r
                waits = [wl, lq, tk["o_evac"]]
                if ocg == 0 and oci == 0:
                    waits += [tk["t_update"], tk["act_read_tr"], tk["dve_read_tr"],
                              tk["act_read_b0"], tk["dve_read_b0"]]
                if len(sg_toks) >= 4:
                    waits += [sg_toks[len(sg_toks) - 4]]
                mm_go = rec("tensor", em_go, waits=waits, inc="pe", n=32)
                last_mm = mm_go
                a_sg = rec("scalar",
                           lambda E, oc=oc, oci=oci: [E.activation(sgq[:, oc, :],
                                                                   PS[:, oci * 512:(oci + 1) * 512],
                                                                   AF.Silu)],
                           waits=[mm_go], inc="ac", n=1)
                sg_toks.append(a_sg)
            wst_tok[bi] = last_mm
            if ocg == 0:
                mm_bc, d_bc = emit_rms()

        dbg_sg_tok = None
        if sgd is not None:
            dbg_sg_tok = rec("scalar", lambda E, q=q: [E.dma_start(sgd[q], sgq[:, :, :])],
                             waits=[sg_toks[-1]], inc="dbg3", n=1, dma=True)
        if "gcold" in DBG and q == 0:
            rec("scalar", lambda E: [E.dma_start(gcold[:, :], gcol[:, :])],
                waits=[gcol_tok], inc="dbg5", n=1, dma=True)
        o2_last = dbg_sg_tok
        for oc in range(16):
            o2_a = rec("vector",
                       lambda E, oc=oc: [E.scalar_tensor_tensor(o2t[:, :], o_fm[:, oc, :],
                                                                gcol[:, oc:oc + 1],
                                                                rstd_bc[:, :], ALU.mult, ALU.mult)],
                       waits=[sg_toks[oc], d_bc, gcol_tok, o2_last], inc="dv", n=1)
            o2_last = rec("vector",
                          lambda E, oc=oc: [E.tensor_tensor(sgq[:, oc, :], o2t[:, :],
                                                            sgq[:, oc, :], ALU.mult)],
                          waits=[o2_a], inc="dv", n=1)
        tk["q_sb_free"] = o2_last

        def em_wot(E):
            r = []
            for d0 in range(0, 16, 2):
                r.append(E.dma_start(
                    woT[:, d0:d0 + 2, :],
                    wot2[d0:d0 + 2, :, :].rearrange("dv p w -> p dv w")))
            return r
        wot_l = rec("sync", em_wot, waits=[repT_done["wo"], wst_tok[0], wst_tok[1]],
                    inc="wot", n=8, dma=True)

        last_st = None
        sq2_guard = [None]
        for t in range(4):
            pbase = 2048 if t % 2 == 0 else 0
            xl = rec("scalar",
                     lambda E, t=t: [E.dma_start(xres[:, :],
                                                 x_in[xr0 + t * 128:xr0 + (t + 1) * 128, :])],
                     waits=[wst_tok[0], wst_tok[1], last_st], inc="xr", n=1, dma=True)

            def em_wmm(E, t=t, pbase=pbase):
                r = []
                for dvc in range(16):
                    for os_ in range(4):
                        r.append(E.matmul(PS[:, pbase + os_ * 512:pbase + (os_ + 1) * 512],
                                          sgq[:, dvc, t * 128:(t + 1) * 128],
                                          woT[:, dvc, os_ * 512:(os_ + 1) * 512],
                                          start=(dvc == 0), stop=(dvc == 15),
                                          skip_group_check=True))
                return r
            mm_wo = rec("tensor", em_wmm,
                        waits=[wot_l, o2_last, wo_psum_free[t % 2]], inc="pe", n=64)

            d_x1 = rec("vector",
                       lambda E, pbase=pbase: [E.tensor_add(xres[:, :], xres[:, :],
                                                            PS[:, pbase:pbase + 2048])],
                       waits=[mm_wo, xl], inc="dv", n=1)
            wo_psum_free[t % 2] = d_x1
            s_sum = rec("vector", lambda E: [E.reduce_sum(stats[:, 0:1], xres[:, :], axis=AX.X)],
                        waits=[d_x1], inc="dv", n=1)
            a_sq2 = rec("scalar", lambda E: [E.activation(sqscr2[:, :], xres[:, :], AF.Square)],
                        waits=[d_x1, sq2_guard[0]], inc="ac", n=1)
            s_sq = rec("vector",
                       lambda E: [E.reduce_sum(stats[:, 1:2], sqscr2[:, :], axis=AX.X)],
                       waits=[a_sq2], inc="dv", n=1)
            sq2_guard[0] = s_sq

            t_m = rec("vector",
                      lambda E: [E.tensor_scalar(stats[:, 2:3], stats[:, 0:1], 1.0 / D,
                                                 None, ALU.mult)],
                      waits=[s_sum], inc="dv", n=1)
            t_m2 = rec("vector",
                       lambda E: [E.scalar_tensor_tensor(stats[:, 3:4], stats[:, 0:1],
                                                         1.0 / (D * D), stats[:, 0:1],
                                                         ALU.mult, ALU.mult)],
                       waits=[s_sum], inc="dv", n=1)
            v_tok = rec("vector",
                        lambda E: [E.tensor_scalar(stats[:, 4:5], stats[:, 1:2], 1.0 / D,
                                                   stats[:, 3:4], ALU.mult, ALU.subtract)],
                        waits=[s_sq, t_m2], inc="dv", n=1)
            sq_tok = rec("scalar", lambda E: [E.activation(stats[:, 5:6], stats[:, 4:5],
                                                           AF.Sqrt, bias=epsv[:, 0:1])],
                         waits=[v_tok], inc="ac", n=1)
            r_tok = rec("vector",
                        lambda E: [E.reciprocal(stats[:, 6:7], stats[:, 5:6])],
                        waits=[sq_tok], inc="dv", n=1)
            rb_tok = rec("vector",
                         lambda E: [E.tensor_scalar(stats[:, 7:8], stats[:, 2:3], stats[:, 6:7],
                                                    -1.0, ALU.mult, ALU.mult)],
                         waits=[r_tok, t_m], inc="dv", n=1)
            a_ln2 = rec("vector",
                        lambda E: [E.tensor_scalar(ln2t[:, :], xres[:, :], stats[:, 6:7],
                                                   stats[:, 7:8], ALU.mult, ALU.add)],
                        waits=[rb_tok, s_sq], inc="dv", n=1)

            def em_st(E, t=t):
                return [
                    E.dma_start(x1scr[mq0 + t * 128:mq0 + (t + 1) * 128, :], xres[:, :]),
                    E.dma_start(ln2scr[mq0 + t * 128:mq0 + (t + 1) * 128, :], ln2t[:, :]),
                ]
            last_st = rec("scalar", em_st, waits=[a_ln2], inc="qst", n=2, dma=True)

        if rbd is not None:
            rec("scalar", lambda E, q=q: [E.dma_start(rbd[q], rstd_bc[:, :])],
                waits=[d_bc], inc="dbg4", n=1, dma=True)
        tk["qg"] = [wo_psum_free[0], wo_psum_free[1]]
        tk["q_ln1q_guard"] = last_st
        return last_st

    q_done = []
    if SKIP_ATTN:
        q_done.append(ln_done[1])
    else:
        for c in range(NCH):
            emit_chunk(c)
            do_prep_steps(9)
            if PHASES >= 4 and c >= NCTX and (c - NCTX) % 8 == 7:
                q_done.append(emit_quarter((c - NCTX) // 8))
                do_prep_steps(12)
    do_prep_steps(10000)
    if PHASES <= 3:
        return _finish()
    p.barrier()

    def w2_transform():
        zw_flat = zw.ap().rearrange("g (s lr) d -> (g s) (lr d)", s=2)
        w2p_v = w2p.ap().rearrange("q gb (s lr) gi c -> lr gb s gi q c", s=2)
        sp_tok = [None, None]
        ev_toks = []
        wr_toks = []
        for qq in range(4):
            bi = qq % 2
            ld = rec("sync",
                     lambda E, qq=qq, bi=bi: [
                         E.dma_start(span[bi][:, :], zw_flat[:, qq * 32768:(qq + 1) * 32768])],
                     waits=[(f"pw{j}", p.cnt.get(f"pw{j}", 0)) for j in range(NPREP)] +
                           [sp_tok[bi], q_done[-1]],
                     inc=f"sp{bi}", n=1, dma=True)
            for wv in range(16):
                eb = wv % 2

                def em_mm(E, bi=bi, wv=wv):
                    r = []
                    for j in range(4):
                        cs = wv * 4 + j
                        r.append(E.matmul(PS[:, j * 512:(j + 1) * 512], h64i2b[:, :],
                                          span[bi][:, cs * 512:(cs + 1) * 512],
                                          start=True, stop=True))
                    return r
                mm = rec("tensor", em_mm,
                         waits=[ld, ev_toks[-1] if ev_toks else None], inc="pe", n=4)
                a_ev = rec("vector",
                           lambda E, eb=eb: [E.tensor_copy(w2ev[eb][:, :], PS[:, 0:2048])],
                           waits=[mm, wr_toks[-2] if len(wr_toks) >= 2 else None],
                           inc="dv", n=1)
                ev_toks.append(a_ev)
                lq_ = (qq * 32768 + wv * 2048) // D

                def em_wr(E, lq_=lq_, eb=eb):
                    r = []
                    for dqi in range(4):
                        dst = w2p_v[lq_:lq_ + 1, :, :, :, dqi:dqi + 1].squeeze(0).squeeze(3)
                        r.append(E.dma_start(dst, w2ev[eb][:, dqi * 512:(dqi + 1) * 512]))
                    return r
                wr = rec("scalar", em_wr, waits=[a_ev], inc=f"we{eb}", n=4, dma=True)
                wr_toks.append(wr)
            sp_tok[bi] = wr_toks[-1]
        return [("we0", p.cnt.get("we0", 0)), ("we1", p.cnt.get("we1", 0))]

    if PHASES <= 4:
        return _finish()
    w2p_tok = w2_transform()
    if PHASES <= 5:
        return _finish()
    p.barrier()

    prev_tt_guard = []

    def ffn_tt(tt):
        tok0 = tt * 1024

        def em_ln2(E, tok0=tok0):
            r = []
            for dc in range(16):
                r.append(E.dma_start(ln2fm[:, dc, :],
                                     ln2scr[tok0:tok0 + 1024, dc * 128:(dc + 1) * 128],
                                     transpose=True))
            return r
        lq = rec("sync", em_ln2, waits=[q_done[-1]] + list(prev_tt_guard) + list(w2p_tok),
                 inc="lfm", n=16, dma=True)

        st_tok = [None, None]
        h_toks = []
        pend_h = [None]

        def emit_h128(gp, pAp, d_hp):
            def em_h128(E, g=gp, pA=pAp):
                return [
                    E.matmul(PS[:, pA:pA + 512], h128b[:, :], h_fm[:, g, 0:512],
                             start=True, stop=True),
                    E.matmul(PS[:, pA + 512:pA + 1024], h128b[:, :], h_fm[:, g, 512:1024],
                             start=True, stop=True),
                ]
            mm_h = rec("tensor", em_h128, waits=[d_hp], inc="pe", n=2)
            d_y = rec("vector",
                      lambda E, g=gp, pA=pAp: [E.tensor_copy(h_fm[:, g, :], PS[:, pA:pA + 1024])],
                      waits=[mm_h], inc="dv", n=1)
            h_toks.append((d_hp, d_y))

        for g in range(64):
            bi = g % 2

            def em_st(E, g=g, bi=bi):
                return [
                    E.dma_start(w1st[bi][:, :, :].rearrange("p a b -> p (a b)"),
                                w1t2[g].rearrange("p a b -> p (a b)")),
                    E.dma_start(w3st[bi][:, :, :].rearrange("p a b -> p (a b)"),
                                w3t2[g].rearrange("p a b -> p (a b)")),
                ]
            sl = rec("sync", em_st,
                     waits=[(f"pw{j}", p.cnt.get(f"pw{j}", 0)) for j in range(NPREP)] +
                           [st_tok[bi]],
                     inc=f"fs{bi}", n=2, dma=True)

            pA = 0 if g % 2 == 0 else 2048
            pB = 1024 if g % 2 == 0 else 3072

            def em_up(E, bi=bi, pA=pA, pB=pB):
                r = []
                for th in range(2):
                    for dc in range(16):
                        r.append(E.matmul(PS[:, pA + th * 512:pA + (th + 1) * 512],
                                          w1st[bi][:, dc, :],
                                          ln2fm[:, dc, th * 512:(th + 1) * 512],
                                          start=(dc == 0), stop=(dc == 15)))
                for th in range(2):
                    for dc in range(16):
                        r.append(E.matmul(PS[:, pB + th * 512:pB + (th + 1) * 512],
                                          w3st[bi][:, dc, :],
                                          ln2fm[:, dc, th * 512:(th + 1) * 512],
                                          start=(dc == 0), stop=(dc == 15)))
                return r
            waits = [sl, lq]
            if h_toks:
                waits.append(h_toks[-1][1])
            if g < 2:
                waits.extend(prev_tt_guard)
            mm_up = rec("tensor", em_up, waits=waits, inc="pe", n=64)
            st_tok[bi] = mm_up

            if pend_h[0] is not None:
                emit_h128(*pend_h[0])

            a_si = rec("scalar",
                       lambda E, g=g, pA=pA: [E.activation(h_fm[:, g, :], PS[:, pA:pA + 1024],
                                                           AF.Silu)],
                       waits=[mm_up], inc="ac", n=1)
            d_h = rec("vector",
                      lambda E, g=g, pB=pB: [E.tensor_tensor(h_fm[:, g, :], h_fm[:, g, :],
                                                             PS[:, pB:pB + 1024], ALU.mult)],
                      waits=[a_si], inc="dv", n=1)
            pend_h[0] = (g, pA, d_h)
        emit_h128(*pend_h[0])
        pend_h[0] = None

        w2_buf_tok = [None, None]
        last_st = {(t, par): None for t in range(8) for par in range(2)}
        last_out_dv = []
        for dq in range(4):
            par = dq % 2
            xl_toks = []
            for tg in range(8):
                xw = [last_st[(tg, par)]] if last_st[(tg, par)] is not None \
                    else list(w2p_tok) + list(prev_tt_guard)
                xl = rec("scalar",
                         lambda E, tok0=tok0, dq=dq, tg=tg, par=par: [
                             E.dma_start(x1t[tg][par][:, :],
                                         x1scr[tok0 + tg * 128:tok0 + (tg + 1) * 128,
                                               dq * 512:(dq + 1) * 512])],
                         waits=xw, inc=f"x1{tg}", n=1, dma=True)
                xl_toks.append(xl)
            mm_w2 = None
            for gblk in range(8):
                bi = gblk % 2

                def em_wl(E, dq=dq, gblk=gblk, bi=bi):
                    return [E.dma_start(w2pb[bi][:, :, :].rearrange("p a b -> p (a b)"),
                                        w2p[dq, gblk].rearrange("p a b -> p (a b)"))]
                wl_waits = list(w2p_tok) + [w2_buf_tok[bi]]
                if dq == 0 and gblk < 2:
                    wl_waits += [h_toks[-1][1]]
                wl = rec("sync", em_wl, waits=wl_waits, inc=f"wp{bi}", n=1, dma=True)

                def em_w2(E, gblk=gblk, bi=bi):
                    r = []
                    for gin in range(8):
                        g = gblk * 8 + gin
                        for tg in range(8):
                            r.append(E.matmul(PS[:, tg * 512:(tg + 1) * 512],
                                              h_fm[:, g, tg * 128:(tg + 1) * 128],
                                              w2pb[bi][:, gin, :],
                                              start=(gblk == 0 and gin == 0),
                                              stop=(gblk == 7 and gin == 7),
                                              skip_group_check=True))
                    return r
                waits = [wl, h_toks[-1][1], h_toks[-2][1]]
                if gblk == 0 and last_out_dv:
                    waits += [last_out_dv[-1]]
                mm_w2 = rec("tensor", em_w2, waits=waits, inc="pe", n=64)
                w2_buf_tok[bi] = mm_w2

            for tg in range(8):
                d_o = rec("vector",
                          lambda E, tg=tg, par=par: [
                              E.scalar_tensor_tensor(x1t[tg][par][:, :],
                                                     PS[:, tg * 512:(tg + 1) * 512], RSQN,
                                                     x1t[tg][par][:, :], ALU.mult, ALU.add)],
                          waits=[mm_w2, xl_toks[tg]], inc="dv", n=1)
                last_out_dv.append(d_o)
                st = rec("scalar",
                         lambda E, tok0=tok0, dq=dq, tg=tg, par=par: [
                             E.dma_start(out[tok0 + tg * 128:tok0 + (tg + 1) * 128,
                                             dq * 512:(dq + 1) * 512], x1t[tg][par][:, :])],
                         waits=[d_o], inc=f"os{tg}", n=1, dma=True)
                last_st[(tg, par)] = st
        prev_tt_guard[:] = [(f"os{t}", p.cnt.get(f"os{t}", 0)) for t in range(8)]

    ffn_tt(0)
    ffn_tt(1)

    rec("sync", lambda E: [],
        waits=[(f"os{t}", p.cnt.get(f"os{t}", 0)) for t in range(8)], n=0)

    with nc.Block() as block:
        p.emit(block)
    return nc


_NC_CACHE = None
_LAST_P = None


def _get_nc():
    global _NC_CACHE
    if _NC_CACHE is None:
        _NC_CACHE = build_nc()
    return _NC_CACHE


def kernel(**inputs):
    x = np.ascontiguousarray(np.asarray(inputs["L0_attn_hidden_states"], dtype=np.float32))
    ws = {
        "wq": np.ascontiguousarray(np.asarray(inputs["L0_attn_Wq"], np.float32)),
        "wk": np.ascontiguousarray(np.asarray(inputs["L0_attn_Wk"], np.float32)),
        "wog": np.ascontiguousarray(np.asarray(inputs["L0_attn_Wog"], np.float32)),
        "wig": np.ascontiguousarray(np.asarray(inputs["L0_attn_Wig"], np.float32)),
        "wo": np.ascontiguousarray(np.asarray(inputs["L0_attn_Wo"], np.float32)),
        "gnw": np.ascontiguousarray(np.asarray(inputs["L0_attn_g_norm_w"], np.float32)),
        "w1": np.ascontiguousarray(np.asarray(inputs["L0_ffn_W1"], np.float32)),
        "w2": np.ascontiguousarray(np.asarray(inputs["L0_ffn_W2"], np.float32)),
        "w3": np.ascontiguousarray(np.asarray(inputs["L0_ffn_W3"], np.float32)),
    }
    in_maps = []
    for cid in range(NCORES):
        b, s = cid // 4, cid % 4
        xin = np.zeros((TROWS, D), np.float32)
        xin[CTX:] = x[b, s * TLOC:(s + 1) * TLOC]
        if s > 0:
            xin[:CTX] = x[b, s * TLOC - CTX:s * TLOC]
        m = {"x": xin}
        m.update(ws)
        in_maps.append(m)
    nc = _get_nc()
    res = run_bass_kernel_spmd(nc, in_maps, core_ids=list(range(NCORES)))
    outa = np.empty((B, L, D), np.float32)
    for cid in range(NCORES):
        b, s = cid // 4, cid % 4
        outa[b, s * TLOC:(s + 1) * TLOC] = res.results[cid]["out"]
    return outa
